# version 05
# baseline (speedup 1.0000x reference)
import numpy as np
import concourse.bass as bass
import concourse.mybir as mybir
from concourse.bass_utils import run_bass_kernel_spmd

F32 = mybir.dt.float32
BF16 = mybir.dt.bfloat16
FP16 = mybir.dt.float16
I32 = mybir.dt.int32
AF = mybir.ActivationFunctionType
ALU = mybir.AluOpType
AX = mybir.AxisListType

T = 2048
C = 1024
NH = 16
NT = T // 128
B_LOC = 2
EPS_ATT = 1e-6
EPS_LN = 1e-5
NEG_BIG = -1.0e30
INV_LN2 = float(np.float32(1.0 / np.log(2.0)))

V, S, TE, G, SY = "vector", "scalar", "tensor", "gpsimd", "sync"


class Res:
    __slots__ = ("w", "rs")

    def __init__(self):
        self.w = None
        self.rs = []


class Prog:
    ENGINES = (SY, V, S, TE, G)
    SEM = {SY: "ld", V: "sv", S: "ss", TE: "st", G: "sg"}

    def __init__(self):
        self.steps = {e: [] for e in self.ENGINES}
        self.counts = {"ld": 0, "sv": 0, "ss": 0, "st": 0, "sg": 0, "gst": 0}
        self.hwm = {}

    def _emit(self, eng, fn, deps, sem, amt, is_dma=False):
        mysems = ("st",) if eng == TE else ()
        waits = []
        for d in deps:
            if d is None:
                continue
            dsem, dval = d
            if dsem in mysems:
                continue
            if dsem in ("ld", "gst"):
                dval = self.counts[dsem]
                if dval == 0:
                    continue
            key = (eng, dsem)
            if self.hwm.get(key, -1) < dval:
                self.hwm[key] = dval
                waits.append((dsem, dval))
        self.counts[sem] += amt
        tok = (sem, self.counts[sem])
        self.steps[eng].append([fn, waits, sem, amt, tok])
        return tok

    def op(self, eng, fn, reads=(), writes=(), deps=()):
        dd = [r.w for r in reads if r is not None] + list(deps)
        for wres in writes:
            dd.append(wres.w)
            dd.extend(wres.rs)
        sem = self.SEM[eng] if eng != G else "sg"
        tok = self._emit(eng, fn, dd, sem, 1)
        for r in reads:
            if r is not None:
                r.rs.append(tok)
        for wres in writes:
            wres.w = tok
            wres.rs = []
        return tok

    def dma(self, eng, dst_ap, src_ap, reads=(), writes=(), deps=()):
        dd = [r.w for r in reads if r is not None] + list(deps)
        for wres in writes:
            dd.append(wres.w)
            dd.extend(wres.rs)
        sem = "ld" if eng == SY else "gst"
        tok = self._emit(eng, lambda e, d=dst_ap, r=src_ap: e.dma_start(out=d, in_=r),
                         dd, sem, 16, is_dma=True)
        for r in reads:
            if r is not None:
                r.rs.append(tok)
        for wres in writes:
            wres.w = tok
            wres.rs = []
        return tok

    def replay(self, nc, final_sem_totals):
        waited = {}
        for e in self.ENGINES:
            for _, waits, _, _, _ in self.steps[e]:
                for sem, val in waits:
                    waited.setdefault(sem, set()).add(val)
        remap = {}
        newcount = {k: 0 for k in self.counts}
        order = {}
        for e in self.ENGINES:
            for st in self.steps[e]:
                order.setdefault(st[2], []).append(st)
        for sem, sts in order.items():
            sts.sort(key=lambda st: st[4][1])
            for st in sts:
                keep = (st[4][1] in waited.get(sem, ())) or sem in ("ld", "gst")
                if keep:
                    newcount[sem] += st[3]
                    remap[(sem, st[4][1])] = newcount[sem]
                else:
                    st[3] = 0
        final_gst = newcount["gst"]

        waited_new = {}
        for e in self.ENGINES:
            for _, waits, _, _, _ in self.steps[e]:
                for sem, val in waits:
                    waited_new.setdefault(sem, set()).add(remap[(sem, val)])

        from contextlib import ExitStack
        with ExitStack() as sctx:
            sems = {k: sctx.enter_context(nc.semaphore(k)) for k in self.counts}
            with nc.Block() as block:
                def make(name):
                    def body(eng):
                        qpos = {"ld": 0, "gst": 0}
                        for fn, waits, sem, amt, tok in self.steps[name]:
                            for wsem, wval in waits:
                                eng.wait_ge(sems[wsem], remap[(wsem, wval)])
                            if amt and sem in qpos:
                                pre = remap[(sem, tok[1])] - amt
                                if pre in waited_new.get(sem, ()):
                                    eng.wait_ge(sems[sem], pre)
                            ins = fn(eng)
                            if amt:
                                ins.then_inc(sems[sem], amt)
                        if name == G and final_gst:
                            eng.wait_ge(sems["gst"], final_gst)
                    return body
                block.sync(make(SY))
                block.vector(make(V))
                block.scalar(make(S))
                block.tensor(make(TE))
                block.gpsimd(make(G))


def build_core_program(dump=None):
    nc = bass.Bass("TRN2", target_bir_lowering=False)

    x_in = nc.dram_tensor("x_in", [B_LOC, T, C], F32, kind="ExternalInput").ap()
    wqkv = nc.dram_tensor("wqkv", [3 * C, C], F32, kind="ExternalInput").ap()
    wout = nc.dram_tensor("wout", [C, C], F32, kind="ExternalInput").ap()
    hypk = nc.dram_tensor("hypk", [1, NH], F32, kind="ExternalInput").ap()
    w1 = nc.dram_tensor("w1", [4 * C, C], F32, kind="ExternalInput").ap()
    b1 = nc.dram_tensor("b1", [128, 32], F32, kind="ExternalInput").ap()
    w2 = nc.dram_tensor("w2", [C, 4 * C], F32, kind="ExternalInput").ap()
    b2 = nc.dram_tensor("b2", [1, C], F32, kind="ExternalInput").ap()
    ident_in = nc.dram_tensor("ident", [128, 128], BF16, kind="ExternalInput").ap()
    mask_in = nc.dram_tensor("mask", [128, 128], BF16, kind="ExternalInput").ap()
    cos_in = nc.dram_tensor("cos_t", [T, 32], F32, kind="ExternalInput").ap()
    sin_in = nc.dram_tensor("sin_t", [T, 32], F32, kind="ExternalInput").ap()
    y_out = nc.dram_tensor("y", [B_LOC, T, C], F32, kind="ExternalOutput").ap()

    w1t_d = nc.dram_tensor("w1t_d", [128, 32 * C], BF16, kind="Internal").ap()
    w2qt_d = nc.dram_tensor("w2qt_d", [128, 32 * C], BF16, kind="Internal").ap()
    ares_d = nc.dram_tensor("ares_d", [T, C], FP16, kind="Internal").ap()

    dumps = {}
    def dump_t(name, shape, dt):
        dumps[name] = nc.dram_tensor(name, shape, dt, kind="ExternalOutput").ap()
    if dump == "xnT":
        dump_t("d_xnT", [8, 128, T], BF16)
    elif dump == "qkv":
        dump_t("d_q", [2, 128, T], BF16)
        dump_t("d_k", [2, 128, T], BF16)
        dump_t("d_v", [2, 128, NT * 65], BF16)
    elif dump in ("att", "att1"):
        dump_t("d_ao", [8, 128, T], BF16)
        if dump == "att1":
            dump_t("d_p", [128, 256], BF16)
            dump_t("d_ts", [128, 256], F32)
            dump_t("d_mcol", [128, 4], F32)
            dump_t("d_pvst", [128, 128], F32)
            dump_t("d_bc", [64, 128], F32)
            dump_t("d_rs", [1, 128], F32)
    elif dump == "ares":
        dump_t("d_ares", [T, C], FP16)
    elif dump == "hq":
        dump_t("d_hq", [32, 128, 512], BF16)
    elif dump == "wprep":
        dump_t("d_w1t", [128, 32 * C], BF16)
        dump_t("d_w2qt", [128, 32 * C], BF16)

    P = Prog()
    from contextlib import ExitStack
    ctx = ExitStack()

    def sb(name, shape, dt):
        h = ctx.enter_context(nc.sbuf_tensor(name, shape, dt))
        return h, Res()

    def psum(name, shape, dt):
        h = ctx.enter_context(nc.psum_tensor(name, shape, dt))
        return h, Res()

    ident, r_ident = sb("identb", [128, 128], BF16)
    maskt, r_maskt = sb("maskt", [128, 128], BF16)
    cos_sb, r_cos = sb("cos_sb", [128, NT * 32], F32)
    sin_sb, r_sin = sb("sin_sb", [128, NT * 32], F32)
    b1_sb, r_b1 = sb("b1_sb", [128, 32], F32)
    b2_sb, r_b2 = sb("b2_sb", [1, C], F32)
    b2q_bc, r_b2bc = sb("b2q_bc", [128, C], FP16)
    hk_sb, r_hk = sb("hk_sb", [1, NH], F32)
    hkd, r_hkd = sb("hkd", [1, 8 * NH], F32)
    scl, r_scl = sb("scl", [128, 8 * NH], F32)
    ones_r, r_onesr = sb("ones_r", [1, 128], F32)
    ones_rb, r_onesrb = sb("ones_rb", [1, 128], BF16)
    ones64, r_ones64 = sb("ones64", [64, 1], BF16)
    O_NEGRK, O_RK, O_K2, O_A, O_B, O_KC = 0, NH, 2 * NH, 3 * NH, 4 * NH, 5 * NH

    xld, r_xld = zip(*[sb(f"xld{i}", [128, C], F32) for i in range(2)])
    xn_tok, r_xntok = sb("xn_tok", [128, C], BF16)
    xn_T, r_xnT = zip(*[sb(f"xnT{c}", [128, T], BF16) for c in range(8)])
    stat, r_stat = sb("stat", [128, 16], F32)

    wld, r_wld = zip(*[sb(f"wld{i}", [128, C], F32) for i in range(2)])
    wbf, r_wbf = sb("wbf", [128, 512], BF16)
    wbf2, r_wbf2 = sb("wbf2", [128, 512], BF16)
    wqkvT_g, r_wqkvT = zip(*[sb(f"wqkvTg{c}", [128, 3 * 128], BF16) for c in range(8)])
    woutT, r_woutT = zip(*[sb(f"woutT{c}", [128, 256], BF16) for c in range(8)])

    q_ext, r_qext = zip(*[sb(f"q_ext{a}", [65, T], BF16) for a in range(2)])
    k_ext, r_kext = zip(*[sb(f"k_ext{a}", [65, T], BF16) for a in range(2)])
    v_ext, r_vext = zip(*[sb(f"v_ext{a}", [128, NT * 65], BF16) for a in range(2)])
    roped, r_roped = sb("roped", [128, 256], BF16)
    ropet, r_ropet = sb("ropet", [128, 64], F32)
    q0row, r_q0row = sb("q0row", [1, 512], F32)
    q0rowb, r_q0rowb = sb("q0rowb", [1, 512], BF16)
    attn_out, r_aout = zip(*[sb(f"attn_out{cc}", [128, T], BF16) for cc in range(8)])
    ash, r_ash = sb("ash", [64, 128], BF16)
    afst, r_afst = zip(*[sb(f"afst{i}", [128, 512], FP16) for i in range(2)])
    arld, r_arld = zip(*[sb(f"arld{i}", [128, 512], FP16) for i in range(2)])

    arena, _r_arena = sb("arena", [128, 5120], F32)
    tb = [arena[:, 0:512], arena[:, 512:1024]]
    ub = [arena[:, 1024:1536], arena[:, 1536:2048]]
    r_s = arena[:, 2048:4096]
    h_sl = arena[:, 0:512]
    q1_sl = arena[:, 512:1024]
    q2_sl = arena[:, 1024:1536]
    q4_sl = arena[:, 1536:2048]
    q5_sl = arena[:, 2048:2560]
    of_sl = arena[:, 2560:3072]
    sg_sl = arena[:, 3072:3584]
    fin_sl = arena[:, 3584:4096]
    sq_sl = arena[:, 4608:5120]
    ar_res = {k: Res() for k in
              ("tb0", "tb1", "ub0", "ub1", "rs",
               "h", "q1", "q2", "q4", "q5", "of", "sg", "fin", "sq")}
    AR_CORE = ("tb0", "tb1", "ub0", "ub1", "rs",
               "h", "q1", "q2", "q4", "q5", "of", "sg", "fin")

    def arena_switch(keys=AR_CORE):
        toks = []
        for k in keys:
            r = ar_res[k]
            toks.append(r.w)
            toks.extend(r.rs)
        toks = [t for t in toks if t is not None]
        for k in keys:
            ar_res[k] = Res()
            ar_res[k].rs = list(toks)
        return toks

    sqb, r_sqb = sb("sqb", [128, 512], BF16)
    p_bf, r_pbf = sb("p_bf", [128, T], BF16)
    pT_bf, r_pT = sb("pT_bf", [128, T], BF16)
    mcol, r_mcol = sb("mcol", [128, 4], F32)
    rsrow, r_rsrow = sb("rsrow", [1, 128], F32)
    rsrow_b, r_rsrowb = sb("rsrow_b", [1, 128], BF16)
    bc_sb, r_bc = sb("bc_sb", [64, 128], F32)
    pvst, r_pvst = sb("pvst", [65, 128], F32)
    qi, r_qi = sb("qi", [128, 512], I32)
    cstb, r_cstb = sb("cstb", [128, 6], F32)

    w1t_hc, r_w1thc = zip(*[sb(f"w1thc{i}", [128, C], BF16) for i in range(2)])
    w2s, r_w2s = zip(*[sb(f"w2s{i}", [128, 512], BF16) for i in range(2)])
    hq, r_hq = zip(*[sb(f"hq{c}", [128, 512], BF16) for c in range(32)])

    ps_p, r_psp = zip(*[psum(f"ps{i}", [128, 512], F32) for i in range(4)])
    ps_pv, r_pspv = psum("ps_pv", [128, 128], F32)
    ps_bc, r_psbc = psum("ps_bc", [64, 128], F32)
    pst, r_pst = zip(*[psum(f"pst{i}", [128, 128], BF16) for i in range(2)])
    ps_a, ps_b, ps_c, ps_d = ps_p
    r_psa, r_psb, r_psc, r_psd = r_psp

    op, dma = P.op, P.dma

    dma(SY, ident[:], ident_in, writes=[r_ident])
    dma(SY, maskt[:], mask_in, writes=[r_maskt])
    dma(SY, cos_sb[:].rearrange("p (n f) -> p n f", f=32),
        cos_in.rearrange("(n p) f -> p n f", p=128), writes=[r_cos])
    dma(SY, sin_sb[:].rearrange("p (n f) -> p n f", f=32),
        sin_in.rearrange("(n p) f -> p n f", p=128), writes=[r_sin])
    dma(SY, b1_sb[:], b1, writes=[r_b1])
    dma(SY, b2_sb[:], b2, writes=[r_b2])
    dma(SY, hk_sb[:], hypk, writes=[r_hk])

    op(V, lambda e: e.memset(ones_r[:], 1.0), writes=[r_onesr])
    op(V, lambda e: e.memset(ones_rb[:], 1.0), writes=[r_onesrb])
    op(V, lambda e: e.memset(ones64[:], 1.0), writes=[r_ones64])
    op(V, lambda e: e.memset(cstb[:, 0:1], 2.0 ** -30), writes=[r_cstb])
    op(V, lambda e: e.memset(cstb[:, 1:2], EPS_LN), writes=[r_cstb])
    op(V, lambda e: e.memset(cstb[:, 2:3], -1.0 + 2e-7), writes=[r_cstb])
    op(V, lambda e: e.memset(cstb[:, 3:4], 2.0), writes=[r_cstb])
    op(S, lambda e: e.activation(out=cstb[:, 3:4], in_=cstb[:, 3:4], func=AF.Ln),
       writes=[r_cstb])
    op(V, lambda e: e.tensor_scalar(out=cstb[:, 4:5], in0=cstb[:, 3:4], scalar1=-1.0,
                                    scalar2=None, op0=ALU.mult), writes=[r_cstb])

    op(V, lambda e: e.memset(hkd[:], 0.0), writes=[r_hkd])
    op(V, lambda e: e.memset(mcol[:], 0.0), writes=[r_mcol])
    op(V, lambda e: e.reciprocal(out=hkd[:, O_RK:O_RK + NH], in_=hk_sb[:]),
       reads=[r_hk], writes=[r_hkd])
    op(V, lambda e: e.tensor_scalar(out=hkd[:, O_NEGRK:O_NEGRK + NH],
                                    in0=hkd[:, O_RK:O_RK + NH], scalar1=-1.0,
                                    scalar2=None, op0=ALU.mult), writes=[r_hkd])
    op(S, lambda e: e.activation(out=hkd[:, O_A:O_A + NH], in_=hk_sb[:], func=AF.Ln),
       reads=[r_hk], writes=[r_hkd])
    op(S, lambda e: e.activation(out=hkd[:, O_A:O_A + NH], in_=hkd[:, O_A:O_A + NH],
                                 func=AF.Exp, scale=-0.5), writes=[r_hkd])
    op(V, lambda e: e.tensor_scalar(out=hkd[:, O_B:O_B + NH], in0=hkd[:, O_A:O_A + NH],
                                    scalar1=EPS_ATT, scalar2=None, op0=ALU.mult),
       writes=[r_hkd])
    op(TE, lambda e: e.matmul(out=ps_a[:, 0:8 * NH], lhsT=ones_r[0:1, :], rhs=hkd[0:1, :],
                              start=True, stop=True),
       reads=[r_onesr, r_hkd], writes=[r_psa])
    op(V, lambda e: e.tensor_copy(out=scl[:], in_=ps_a[:, 0:8 * NH]),
       reads=[r_psa], writes=[r_scl])

    def quant_chain(src_ap, dst_ap, man, minexp, res_src, res_dst,
                    c1, c2, c4, c5, ci, tw, sg, r_c1, r_c2, r_c4, r_c5, r_ci, r_sgn,
                    r_tw, signed, nrow=128):
        nlev = man + 2
        if signed:
            op(S, lambda e: e.activation(out=sg, in_=src_ap, func=AF.Sign),
               reads=[res_src], writes=[r_sgn])
            op(S, lambda e: e.activation(out=c5, in_=src_ap, func=AF.Abs),
               reads=[res_src], writes=[r_c5])
            ax, r_ax = c5, r_c5
        else:
            ax, r_ax = src_ap, res_src
        zb = cstb[0:nrow, 0:1]
        op(S, lambda e: e.activation(out=c1, in_=ax, func=AF.Ln, bias=zb),
           reads=[r_ax, r_cstb], writes=[r_c1])
        op(V, lambda e: e.tensor_scalar(out=c1, in0=c1, scalar1=INV_LN2, scalar2=63.5,
                                        op0=ALU.mult, op1=ALU.add), writes=[r_c1])
        op(V, lambda e: e.tensor_copy(out=ci, in_=c1), reads=[r_c1], writes=[r_ci])
        op(V, lambda e: e.tensor_copy(out=c1, in_=ci), reads=[r_ci], writes=[r_c1])
        op(V, lambda e: e.tensor_scalar(out=c2, in0=c1, scalar1=-1.0,
                                        scalar2=float(64 + man), op0=ALU.mult,
                                        op1=ALU.add), reads=[r_c1], writes=[r_c2])
        op(V, lambda e: e.tensor_scalar(out=c2, in0=c2, scalar1=float(man - minexp),
                                        scalar2=None, op0=ALU.min), writes=[r_c2])
        op(S, lambda e: e.activation(out=c4, in_=c2, func=AF.Exp,
                                     scale=cstb[0:nrow, 4:5]),
           reads=[r_c2, r_cstb], writes=[r_c4])
        op(S, lambda e: e.activation(out=c2, in_=c2, func=AF.Exp,
                                     scale=cstb[0:nrow, 3:4]),
           reads=[r_cstb], writes=[r_c2])
        op(V, lambda e: e.tensor_mul(out=c2, in0=ax, in1=c2),
           reads=[r_ax], writes=[r_c2])
        op(V, lambda e: e.tensor_copy(out=c1, in_=c2), reads=[r_c2], writes=[r_c1])
        for lev in range(nlev):
            stepv = float(2 ** (nlev - 1 - lev))
            thr = stepv - 0.5
            op(V, lambda e, th=thr, sv=stepv: e.tensor_scalar(
                out=ci, in0=c1, scalar1=th, scalar2=sv, op0=ALU.is_ge, op1=ALU.mult),
               reads=[r_c1], writes=[r_ci])
            op(V, lambda e: e.tensor_sub(out=c1, in0=c1, in1=ci),
               reads=[r_ci], writes=[r_c1])
        op(V, lambda e: e.tensor_sub(out=c2, in0=c2, in1=c1),
           reads=[r_c1], writes=[r_c2])
        tok = op(V, lambda e: e.tensor_mul(out=dst_ap, in0=c2, in1=c4),
                 reads=[r_c2, r_c4], writes=[res_dst])
        if signed:
            tok = op(V, lambda e: e.tensor_mul(out=dst_ap, in0=dst_ap, in1=sg),
                     reads=[r_sgn], writes=[res_dst])
        return tok

    for hf in range(2):
        quant_chain(b2_sb[0:1, 512 * hf:512 * hf + 512],
                    b2_sb[0:1, 512 * hf:512 * hf + 512], 10, -14, r_b2, r_b2,
                    q1_sl[0:1, 0:512], q2_sl[0:1, 0:512], q4_sl[0:1, 0:512],
                    q5_sl[0:1, 0:512], qi[0:1, 0:512], None,
                    sg_sl[0:1, 0:512],
                    ar_res["q1"], ar_res["q2"], ar_res["q4"], ar_res["q5"], r_qi,
                    ar_res["sg"], None, True, nrow=1)
    for hf in range(2):
        pq, rq = (ps_a, r_psa) if hf == 0 else (ps_b, r_psb)
        op(TE, lambda e, hf=hf, pq=pq: e.matmul(
            out=pq[:, 0:512], lhsT=ones_r[0:1, :], rhs=b2_sb[0:1, 512 * hf:512 * hf + 512],
            start=True, stop=True), reads=[r_onesr, r_b2], writes=[rq])
        op(V, lambda e, hf=hf, pq=pq: e.tensor_copy(
            out=b2q_bc[:, 512 * hf:512 * hf + 512], in_=pq[:, 0:512]),
           reads=[rq], writes=[r_b2bc])

    for hc in range(32):
        buf, rb = wld[hc % 2], r_wld[hc % 2]
        dma(SY, buf[:], w1[128 * hc:128 * hc + 128, :], writes=[rb])
        for hf in range(2):
            op(V, lambda e, b=buf, hf=hf: e.tensor_copy(
                out=wbf[:], in_=b[:, 512 * hf:512 * hf + 512]),
               reads=[rb], writes=[r_wbf])
            for c in range(4):
                op(TE, lambda e, c=c: e.transpose(out=pst[c % 2][:],
                                                  in_=wbf[:, 128 * c:128 * c + 128],
                                                  identity=ident[:]),
                   reads=[r_wbf, r_ident], writes=[r_pst[c % 2]])
                op(V, lambda e, c=c: e.tensor_copy(out=wbf2[:, 128 * c:128 * c + 128],
                                                   in_=pst[c % 2][:]),
                   reads=[r_pst[c % 2]], writes=[r_wbf2])
            dma(G, w1t_d[:, C * hc + 512 * hf:C * hc + 512 * hf + 512], wbf2[:],
                reads=[r_wbf2])

    for cc in range(8):
        for piece in range(8):
            buf, rb = wld[piece % 2], r_wld[piece % 2]
            dma(SY, buf[:, 0:512],
                w2[128 * cc:128 * cc + 128, 512 * piece:512 * piece + 512], writes=[rb])
            quant_chain(buf[:, 0:512], buf[:, 0:512], 3, -6, rb, rb,
                        q1_sl, q2_sl, q4_sl, q5_sl, qi[:, 0:512], None, sg_sl,
                        ar_res["q1"], ar_res["q2"], ar_res["q4"], ar_res["q5"], r_qi,
                        ar_res["sg"], None, True)
            op(V, lambda e, b=buf: e.tensor_copy(out=wbf[:, 0:512], in_=b[:, 0:512]),
               reads=[rb], writes=[r_wbf])
            for j in range(4):
                hcg = 4 * piece + j
                op(TE, lambda e, j=j: e.transpose(out=pst[j % 2][:],
                                                  in_=wbf[:, 128 * j:128 * j + 128],
                                                  identity=ident[:]),
                   reads=[r_wbf, r_ident], writes=[r_pst[j % 2]])
                op(V, lambda e, j=j: e.tensor_copy(out=wbf2[:, 128 * j:128 * j + 128],
                                                   in_=pst[j % 2][:]),
                   reads=[r_pst[j % 2]], writes=[r_wbf2])
                dma(G, w2qt_d[:, C * hcg + 128 * cc:C * hcg + 128 * cc + 128],
                    wbf2[:, 128 * j:128 * j + 128], reads=[r_wbf2])

    if dump == "wprep":
        for hc in range(64):
            dma(SY, wbf2[:], w1t_d[:, 512 * hc:512 * hc + 512], writes=[r_wbf2])
            dma(G, dumps["d_w1t"][:, 512 * hc:512 * hc + 512], wbf2[:],
                reads=[r_wbf2])
            dma(SY, wbf2[:], w2qt_d[:, 512 * hc:512 * hc + 512], writes=[r_wbf2])
            dma(G, dumps["d_w2qt"][:, 512 * hc:512 * hc + 512], wbf2[:],
                reads=[r_wbf2])

    nseq = B_LOC if dump is None else (0 if dump == "wprep" else 1)
    for sq in range(nseq):
        arena_switch()
        for i in range(NT):
            buf, rb = xld[i % 2], r_xld[i % 2]
            dma(SY, buf[:], x_in[sq, 128 * i:128 * i + 128, :], writes=[rb])
            op(V, lambda e, b=buf: e.reduce_sum(out=stat[:, 0:1], in_=b[:], axis=AX.X),
               reads=[rb], writes=[r_stat])
            for hf in range(2):
                op(V, lambda e, b=buf, hf=hf: e.tensor_mul(
                    out=sq_sl, in0=b[:, 512 * hf:512 * hf + 512],
                    in1=b[:, 512 * hf:512 * hf + 512]),
                   reads=[rb], writes=[ar_res["sq"]])
                op(V, lambda e, hf=hf: e.reduce_sum(out=stat[:, 1 + hf:2 + hf],
                                                    in_=sq_sl, axis=AX.X),
                   reads=[ar_res["sq"]], writes=[r_stat])
            op(V, lambda e: e.tensor_add(out=stat[:, 3:4], in0=stat[:, 1:2],
                                         in1=stat[:, 2:3]), writes=[r_stat])
            op(V, lambda e: e.tensor_scalar(out=stat[:, 4:5], in0=stat[:, 0:1],
                                            scalar1=1.0 / C, scalar2=None,
                                            op0=ALU.mult), writes=[r_stat])
            op(V, lambda e: e.tensor_scalar(out=stat[:, 5:6], in0=stat[:, 3:4],
                                            scalar1=1.0 / C, scalar2=None,
                                            op0=ALU.mult), writes=[r_stat])
            op(V, lambda e: e.tensor_mul(out=stat[:, 6:7], in0=stat[:, 4:5],
                                         in1=stat[:, 4:5]), writes=[r_stat])
            op(V, lambda e: e.tensor_sub(out=stat[:, 7:8], in0=stat[:, 5:6],
                                         in1=stat[:, 6:7]), writes=[r_stat])
            op(S, lambda e: e.activation(out=stat[:, 8:9], in_=stat[:, 7:8],
                                         func=AF.Ln, bias=cstb[:, 1:2]),
               reads=[r_stat, r_cstb], writes=[r_stat])
            op(S, lambda e: e.activation(out=stat[:, 9:10], in_=stat[:, 8:9],
                                         func=AF.Exp, scale=-0.5), writes=[r_stat])
            op(V, lambda e: e.tensor_scalar(out=stat[:, 10:11], in0=stat[:, 4:5],
                                            scalar1=-1.0, scalar2=None, op0=ALU.mult),
               writes=[r_stat])
            op(V, lambda e, b=buf: e.tensor_scalar(out=xn_tok[:], in0=b[:],
                                                   scalar1=stat[:, 10:11],
                                                   scalar2=stat[:, 9:10],
                                                   op0=ALU.add, op1=ALU.mult),
               reads=[rb, r_stat], writes=[r_xntok])
            for c in range(8):
                op(TE, lambda e, c=c: e.transpose(out=pst[c % 2][:],
                                                  in_=xn_tok[:, 128 * c:128 * c + 128],
                                                  identity=ident[:]),
                   reads=[r_xntok, r_ident], writes=[r_pst[c % 2]])
                op(V, lambda e, c=c, i=i: e.tensor_copy(
                    out=xn_T[c][:, 128 * i:128 * i + 128], in_=pst[c % 2][:]),
                   reads=[r_pst[c % 2]], writes=[r_xnT[c]])

        if dump == "xnT":
            for c in range(8):
                dma(G, dumps["d_xnT"][c], xn_T[c][:], reads=[r_xnT[c]])
            break

        ngroups = 1 if dump == "att1" else 8
        for g in range(ngroups):
            for part, rowb in ((0, 128 * g), (1, C + 128 * g), (2, 2 * C + 128 * g)):
                buf, rb = wld[part % 2], r_wld[part % 2]
                dma(SY, buf[:], wqkv[rowb:rowb + 128, :], writes=[rb])
                for hf in range(2):
                    op(V, lambda e, b=buf, hf=hf: e.tensor_copy(
                        out=wbf[:], in_=b[:, 512 * hf:512 * hf + 512]),
                       reads=[rb], writes=[r_wbf])
                    for cq in range(4):
                        c = 4 * hf + cq
                        op(TE, lambda e, cq=cq: e.transpose(
                            out=pst[cq % 2][:], in_=wbf[:, 128 * cq:128 * cq + 128],
                            identity=ident[:]),
                           reads=[r_wbf, r_ident], writes=[r_pst[cq % 2]])
                        op(V, lambda e, c=c, cq=cq, p=part: e.tensor_copy(
                            out=wqkvT_g[c][:, 128 * p:128 * p + 128],
                            in_=pst[cq % 2][:]),
                           reads=[r_pst[cq % 2]], writes=[r_wqkvT[c]])
            for a in range(2):
                op(V, lambda e, a=a: e.memset(v_ext[a][:], 1.0), writes=[r_vext[a]])

            for i in range(NT):
                pq, rpq = (ps_a, r_psa) if i % 2 == 0 else (ps_b, r_psb)
                for c in range(8):
                    op(TE, lambda e, c=c, i=i, pq=pq: e.matmul(
                        out=pq[:, 0:384], lhsT=xn_T[c][:, 128 * i:128 * i + 128],
                        rhs=wqkvT_g[c][:, 0:384], start=(c == 0), stop=(c == 7)),
                       reads=[r_xnT[c], r_wqkvT[c]], writes=[rpq])
                for qk in range(2):
                    for a in range(2):
                        base = 128 * qk + 64 * a
                        lo = slice(base, base + 32)
                        hi = slice(base + 32, base + 64)
                        cs = cos_sb[:, 32 * i:32 * i + 32]
                        sn = sin_sb[:, 32 * i:32 * i + 32]
                        op(V, lambda e, pq=pq, lo=lo, cs=cs: e.tensor_mul(
                            out=ropet[:, 0:32], in0=pq[:, lo], in1=cs),
                           reads=[rpq, r_cos], writes=[r_ropet])
                        op(V, lambda e, pq=pq, hi=hi, sn=sn: e.tensor_mul(
                            out=ropet[:, 32:64], in0=pq[:, hi], in1=sn),
                           reads=[rpq, r_sin], writes=[r_ropet])
                        op(V, lambda e, lo=lo: e.tensor_add(
                            out=roped[:, lo], in0=ropet[:, 0:32], in1=ropet[:, 32:64]),
                           reads=[r_ropet], writes=[r_roped])
                        op(V, lambda e, pq=pq, lo=lo, sn=sn: e.tensor_mul(
                            out=ropet[:, 0:32], in0=pq[:, lo], in1=sn),
                           reads=[rpq, r_sin], writes=[r_ropet])
                        op(V, lambda e, pq=pq, hi=hi, cs=cs: e.tensor_mul(
                            out=ropet[:, 32:64], in0=pq[:, hi], in1=cs),
                           reads=[rpq, r_cos], writes=[r_ropet])
                        op(V, lambda e, hi=hi: e.tensor_sub(
                            out=roped[:, hi], in0=ropet[:, 32:64], in1=ropet[:, 0:32]),
                           reads=[r_ropet], writes=[r_roped])
                for a in range(2):
                    op(V, lambda e, pq=pq, a=a, i=i: e.tensor_copy(
                        out=v_ext[a][:, 65 * i:65 * i + 64],
                        in_=pq[:, 256 + 64 * a:256 + 64 * a + 64]),
                       reads=[rpq], writes=[r_vext[a]])
                for qk in range(2):
                    for a in range(2):
                        idx = a
                        op(TE, lambda e, qk=qk, a=a, idx=idx: e.transpose(
                            out=pst[idx][0:64, :],
                            in_=roped[:, 128 * qk + 64 * a:128 * qk + 64 * a + 64],
                            identity=ident[:]),
                           reads=[r_roped, r_ident], writes=[r_pst[idx]])
                        if qk == 0:
                            h = 2 * g + a
                            op(V, lambda e, a=a, i=i, idx=idx, h=h: e.tensor_scalar(
                                out=q_ext[a][0:64, 128 * i:128 * i + 128],
                                in0=pst[idx][0:64, :],
                                scalar1=scl[0:64, O_NEGRK + h:O_NEGRK + h + 1],
                                scalar2=None, op0=ALU.mult),
                               reads=[r_pst[idx], r_scl], writes=[r_qext[a]])
                        else:
                            op(V, lambda e, a=a, i=i, idx=idx: e.tensor_copy(
                                out=k_ext[a][0:64, 128 * i:128 * i + 128],
                                in_=pst[idx][0:64, :]),
                               reads=[r_pst[idx]], writes=[r_kext[a]])
            for a in range(2):
                h = 2 * g + a
                for qk in range(2):
                    src, rsrc = (q_ext[a], r_qext[a]) if qk == 0 else (k_ext[a], r_kext[a])
                    bias_ap = hkd[0:1, O_RK + h:O_RK + h + 1] if qk == 0 \
                        else hk_sb[0:1, h:h + 1]
                    bias_res = r_hkd if qk == 0 else r_hk
                    dstt, rdst = (q_ext[a], r_qext[a]) if qk == 0 \
                        else (k_ext[a], r_kext[a])
                    for blk in range(4):
                        sl = slice(512 * blk, 512 * blk + 512)
                        op(V, lambda e, src=src, sl=sl: e.tensor_mul(
                            out=sq_sl[0:64, :], in0=src[0:64, sl], in1=src[0:64, sl]),
                           reads=[rsrc], writes=[ar_res["sq"]])
                        op(V, lambda e: e.tensor_copy(out=sqb[0:64, :],
                                                      in_=sq_sl[0:64, :]),
                           reads=[ar_res["sq"]], writes=[r_sqb])
                        op(TE, lambda e: e.matmul(out=ps_c[0:1, 0:512], lhsT=ones64[:],
                                                  rhs=sqb[0:64, :], start=True,
                                                  stop=True),
                           reads=[r_ones64, r_sqb], writes=[r_psc])
                        op(S, lambda e, b=bias_ap: e.activation(
                            out=q0row[:], in_=ps_c[0:1, 0:512], func=AF.Ln, bias=b),
                           reads=[bias_res, r_psc], writes=[r_q0row])
                        op(S, lambda e: e.activation(out=q0row[:], in_=q0row[:],
                                                     func=AF.Exp, scale=0.5),
                           writes=[r_q0row])
                        op(V, lambda e: e.tensor_copy(out=q0rowb[:], in_=q0row[:]),
                           reads=[r_q0row], writes=[r_q0rowb])
                        dma(G, dstt[64:65, sl], q0rowb[:], reads=[r_q0rowb],
                            writes=[rdst])

            if dump == "qkv":
                for a in range(2):
                    dma(G, dumps["d_q"][a][0:65, :], q_ext[a][:], reads=[r_qext[a]])
                    dma(G, dumps["d_k"][a][0:65, :], k_ext[a][:], reads=[r_kext[a]])
                    dma(G, dumps["d_v"][a], v_ext[a][:], reads=[r_vext[a]])
                break

            for a in range(2):
                h = 2 * g + a
                for lt in range(NT):
                    W = 128 * (lt + 1)
                    nb = (W + 511) // 512
                    for sbk in range(nb):
                        wd = min(512, W - 512 * sbk)
                        pq, rpq = (ps_a, r_psa) if sbk % 2 == 0 else (ps_b, r_psb)
                        tb_, rtb = tb[sbk % 2], ar_res["tb" + str(sbk % 2)]
                        ub_, rub = ub[sbk % 2], ar_res["ub" + str(sbk % 2)]
                        op(TE, lambda e, a=a, lt=lt, sbk=sbk, wd=wd, pq=pq: e.matmul(
                            out=pq[:, 0:wd], lhsT=q_ext[a][:, 128 * lt:128 * lt + 128],
                            rhs=k_ext[a][:, 512 * sbk:512 * sbk + wd],
                            start=True, stop=True),
                           reads=[r_qext[a], r_kext[a]], writes=[rpq])
                        op(V, lambda e, wd=wd, pq=pq, tb_=tb_: e.tensor_scalar(
                            out=tb_[:, 0:wd], in0=pq[:, 0:wd],
                            scalar1=1.0 + EPS_ATT, scalar2=None, op0=ALU.max),
                           reads=[rpq], writes=[rtb])
                        op(S, lambda e, wd=wd, tb_=tb_, ub_=ub_: e.activation(
                            out=ub_[:, 0:wd], in_=tb_[:, 0:wd], func=AF.Square),
                           reads=[rtb], writes=[rub])
                        op(S, lambda e, wd=wd, ub_=ub_: e.activation(
                            out=ub_[:, 0:wd], in_=ub_[:, 0:wd], func=AF.Ln,
                            bias=cstb[:, 2:3]), reads=[r_cstb], writes=[rub])
                        op(S, lambda e, wd=wd, ub_=ub_: e.activation(
                            out=ub_[:, 0:wd], in_=ub_[:, 0:wd], func=AF.Exp,
                            scale=0.5), writes=[rub])
                        op(V, lambda e, wd=wd, tb_=tb_, ub_=ub_: e.tensor_add(
                            out=tb_[:, 0:wd], in0=tb_[:, 0:wd], in1=ub_[:, 0:wd]),
                           reads=[rub], writes=[rtb])
                        op(S, lambda e, wd=wd, tb_=tb_: e.activation(
                            out=tb_[:, 0:wd], in_=tb_[:, 0:wd], func=AF.Ln),
                           writes=[rtb])
                        op(V, lambda e, wd=wd, tb_=tb_, h=h: e.tensor_scalar(
                            out=tb_[:, 0:wd], in0=tb_[:, 0:wd],
                            scalar1=scl[:, O_B + h:O_B + h + 1],
                            scalar2=None, op0=ALU.add),
                           reads=[r_scl], writes=[rtb])
                        op(V, lambda e, wd=wd, sbk=sbk, tb_=tb_: e.reciprocal(
                            out=r_s[:, 512 * sbk:512 * sbk + wd], in_=tb_[:, 0:wd]),
                           reads=[rtb], writes=[ar_res["rs"]])
                    op(V, lambda e, W=W: e.tensor_add(out=r_s[:, W - 128:W],
                                                      in0=r_s[:, W - 128:W],
                                                      in1=maskt[:]),
                       reads=[r_maskt], writes=[ar_res["rs"]])
                    op(V, lambda e, W=W: e.tensor_reduce(
                        out=mcol[:, 0:1], in_=r_s[:, 0:W], axis=AX.X, op=ALU.max,
                        negate=True),
                       reads=[ar_res["rs"]], writes=[r_mcol])
                    op(V, lambda e, h=h: e.tensor_scalar(
                        out=mcol[:, 1:2], in0=mcol[:, 0:1],
                        scalar1=scl[:, O_A + h:O_A + h + 1], scalar2=None,
                        op0=ALU.mult), reads=[r_scl], writes=[r_mcol])
                    op(S, lambda e, W=W, h=h: e.activation(
                        out=p_bf[:, 0:W], in_=r_s[:, 0:W], func=AF.Exp,
                        bias=mcol[:, 1:2], scale=scl[:, O_A + h:O_A + h + 1]),
                       reads=[ar_res["rs"], r_mcol, r_scl], writes=[r_pbf])
                    for cch in range(lt + 1):
                        op(TE, lambda e, cch=cch: e.transpose(
                            out=pst[cch % 2][:], in_=p_bf[:, 128 * cch:128 * cch + 128],
                            identity=ident[:]),
                           reads=[r_pbf, r_ident], writes=[r_pst[cch % 2]])
                        op(V, lambda e, cch=cch: e.tensor_copy(
                            out=pT_bf[:, 128 * cch:128 * cch + 128],
                            in_=pst[cch % 2][:]),
                           reads=[r_pst[cch % 2]], writes=[r_pT])
                    for cch in range(lt + 1):
                        op(TE, lambda e, a=a, cch=cch, lt=lt: e.matmul(
                            out=ps_pv[0:65, 0:128],
                            lhsT=v_ext[a][:, 65 * cch:65 * cch + 65],
                            rhs=pT_bf[:, 128 * cch:128 * cch + 128],
                            start=(cch == 0), stop=(cch == lt)),
                           reads=[r_vext[a], r_pT], writes=[r_pspv])
                    op(V, lambda e: e.tensor_copy(out=pvst[:],
                                                  in_=ps_pv[0:65, 0:128]),
                       reads=[r_pspv], writes=[r_pvst])
                    dma(G, rsrow[:], pvst[64:65, :], reads=[r_pvst], writes=[r_rsrow])
                    op(V, lambda e: e.reciprocal(out=rsrow[:], in_=rsrow[:]),
                       writes=[r_rsrow])
                    op(V, lambda e: e.tensor_copy(out=rsrow_b[:], in_=rsrow[:]),
                       reads=[r_rsrow], writes=[r_rsrowb])
                    op(TE, lambda e: e.matmul(out=ps_bc[:, 0:128],
                                              lhsT=ones_rb[0:1, 0:64], rhs=rsrow_b[:],
                                              start=True, stop=True),
                       reads=[r_onesrb, r_rsrowb], writes=[r_psbc])
                    op(V, lambda e: e.tensor_copy(out=bc_sb[:], in_=ps_bc[:, 0:128]),
                       reads=[r_psbc], writes=[r_bc])
                    if dump == "att1" and g == 0 and a == 0 and lt < 2:
                        dma(G, dumps["d_p"][:, 128 * lt:128 * lt + 128],
                            p_bf[:, 0:128], reads=[r_pbf])
                        dma(G, dumps["d_ts"][:, 128 * lt:128 * lt + 128],
                            r_s[:, 0:128], reads=[ar_res["rs"]])
                        if lt == 0:
                            dma(G, dumps["d_mcol"], mcol[:], reads=[r_mcol])
                            dma(G, dumps["d_pvst"][0:65, :], pvst[:], reads=[r_pvst])
                            dma(G, dumps["d_bc"], bc_sb[:], reads=[r_bc])
                            dma(G, dumps["d_rs"], rsrow[:], reads=[r_rsrow])
                    if a == 0:
                        op(V, lambda e, g=g, lt=lt: e.tensor_mul(
                            out=attn_out[g][0:64, 128 * lt:128 * lt + 128],
                            in0=pvst[0:64, :], in1=bc_sb[:]),
                           reads=[r_pvst, r_bc], writes=[r_aout[g]])
                    else:
                        op(V, lambda e: e.tensor_mul(out=ash[:], in0=pvst[0:64, :],
                                                     in1=bc_sb[:]),
                           reads=[r_pvst, r_bc], writes=[r_ash])
                        dma(G, attn_out[g][64:128, 128 * lt:128 * lt + 128], ash[:],
                            reads=[r_ash], writes=[r_aout[g]])

        if dump == "qkv":
            break
        if dump in ("att", "att1"):
            for cc in range(ngroups):
                dma(G, dumps["d_ao"][cc], attn_out[cc][:], reads=[r_aout[cc]])
            break

        for qd in range(4):
            for j in range(2):
                buf, rb = wld[j % 2], r_wld[j % 2]
                dma(SY, buf[:], wout[256 * qd + 128 * j:256 * qd + 128 * j + 128, :],
                    writes=[rb])
                for hf in range(2):
                    op(V, lambda e, b=buf, hf=hf: e.tensor_copy(
                        out=wbf[:], in_=b[:, 512 * hf:512 * hf + 512]),
                       reads=[rb], writes=[r_wbf])
                    for cq in range(4):
                        c = 4 * hf + cq
                        op(TE, lambda e, cq=cq: e.transpose(
                            out=pst[cq % 2][:], in_=wbf[:, 128 * cq:128 * cq + 128],
                            identity=ident[:]),
                           reads=[r_wbf, r_ident], writes=[r_pst[cq % 2]])
                        op(V, lambda e, c=c, cq=cq, j=j: e.tensor_copy(
                            out=woutT[c][:, 128 * j:128 * j + 128],
                            in_=pst[cq % 2][:]),
                           reads=[r_pst[cq % 2]], writes=[r_woutT[c]])
            for i in range(NT):
                pq, rpq = (ps_a, r_psa) if i % 2 == 0 else (ps_b, r_psb)
                for c in range(8):
                    op(TE, lambda e, c=c, i=i, pq=pq: e.matmul(
                        out=pq[:, 0:256], lhsT=attn_out[c][:, 128 * i:128 * i + 128],
                        rhs=woutT[c][:, 0:256], start=(c == 0), stop=(c == 7)),
                       reads=[r_aout[c], r_woutT[c]], writes=[rpq])
                st, rst = afst[i % 2], r_afst[i % 2]
                op(V, lambda e, pq=pq, st=st: e.tensor_copy(out=st[:, 0:256],
                                                            in_=pq[:, 0:256]),
                   reads=[rpq], writes=[rst])
                dma(G, ares_d[128 * i:128 * i + 128, 256 * qd:256 * qd + 256],
                    st[:, 0:256], reads=[rst])

        if dump == "ares":
            for i in range(NT):
                for half in range(2):
                    st, rst = arld[i % 2], r_arld[i % 2]
                    dma(SY, st[:],
                        ares_d[128 * i:128 * i + 128, 512 * half:512 * half + 512],
                        writes=[rst])
                    dma(G, dumps["d_ares"][128 * i:128 * i + 128,
                                           512 * half:512 * half + 512],
                        st[:], reads=[rst])
            break

        arena_switch()
        for qt in range(4):
            tok0 = 512 * qt
            for hc in range(32):
                wbuf, rwb = w1t_hc[hc % 2], r_w1thc[hc % 2]
                dma(SY, wbuf[:], w1t_d[:, C * hc:C * hc + C], writes=[rwb])
                pq, rpq = (ps_c, r_psc) if hc % 2 == 0 else (ps_d, r_psd)
                for c in range(8):
                    op(TE, lambda e, c=c, pq=pq, wbuf=wbuf, tok0=tok0: e.matmul(
                        out=pq[:, 0:512], lhsT=wbuf[:, 128 * c:128 * c + 128],
                        rhs=xn_T[c][:, tok0:tok0 + 512], start=(c == 0), stop=(c == 7)),
                       reads=[rwb, r_xnT[c]], writes=[rpq])
                op(S, lambda e, pq=pq, hc=hc: e.activation(
                    out=h_sl, in_=pq[:, 0:512], func=AF.Relu,
                    bias=b1_sb[:, hc:hc + 1]),
                   reads=[rpq, r_b1], writes=[ar_res["h"]])
                quant_chain(h_sl, hq[hc][:], 3, -6, ar_res["h"], r_hq[hc],
                            q1_sl, q2_sl, q4_sl, q5_sl, qi[:, 0:512], None, sg_sl,
                            ar_res["q1"], ar_res["q2"], ar_res["q4"], ar_res["q5"],
                            r_qi, ar_res["sg"], None, False)
            if dump == "hq" and qt == 0:
                for hc in range(32):
                    dma(G, dumps["d_hq"][hc], hq[hc][:], reads=[r_hq[hc]])
                break
            for dh in range(2):
                for hc in range(32):
                    wb2, rw2 = w2s[hc % 2], r_w2s[hc % 2]
                    dma(SY, wb2[:],
                        w2qt_d[:, C * hc + 512 * dh:C * hc + 512 * dh + 512],
                        writes=[rw2])
                    for tl in range(4):
                        op(TE, lambda e, hc=hc, tl=tl, wb2=wb2: e.matmul(
                            out=ps_p[tl][:, 0:512],
                            lhsT=hq[hc][:, 128 * tl:128 * tl + 128], rhs=wb2[:],
                            start=(hc == 0), stop=(hc == 31)),
                           reads=[r_hq[hc], rw2], writes=[r_psp[tl]])
                for tl in range(4):
                    i = 4 * qt + tl
                    op(V, lambda e, tl=tl, dh=dh: e.tensor_add(
                        out=of_sl, in0=ps_p[tl][:, 0:512],
                        in1=b2q_bc[:, 512 * dh:512 * dh + 512]),
                       reads=[r_psp[tl], r_b2bc], writes=[ar_res["of"]])
                    quant_chain(of_sl, fin_sl, 10, -14, ar_res["of"], ar_res["fin"],
                                q1_sl, q2_sl, q4_sl, q5_sl, qi[:, 0:512], None,
                                sg_sl, ar_res["q1"], ar_res["q2"], ar_res["q4"],
                                ar_res["q5"], r_qi, ar_res["sg"], None, True)
                    xb, rxb = xld[tl % 2], r_xld[tl % 2]
                    dma(SY, xb[:, 0:512],
                        x_in[sq, 128 * i:128 * i + 128, 512 * dh:512 * dh + 512],
                        writes=[rxb])
                    ab, rab = arld[tl % 2], r_arld[tl % 2]
                    dma(SY, ab[:],
                        ares_d[128 * i:128 * i + 128, 512 * dh:512 * dh + 512],
                        writes=[rab])
                    op(V, lambda e, xb=xb: e.tensor_add(
                        out=fin_sl, in0=fin_sl, in1=xb[:, 0:512]),
                       reads=[rxb], writes=[ar_res["fin"]])
                    op(V, lambda e, ab=ab: e.tensor_add(
                        out=fin_sl, in0=fin_sl, in1=ab[:]),
                       reads=[rab], writes=[ar_res["fin"]])
                    dma(G, y_out[sq, 128 * i:128 * i + 128, 512 * dh:512 * dh + 512],
                        fin_sl, reads=[ar_res["fin"]])

    global _last_P
    _last_P = P
    P.replay(nc, None)
    ctx.close()
    return nc


_consts_cache = {}


def _host_consts():
    if _consts_cache:
        return _consts_cache
    import ml_dtypes
    ident = np.eye(128, dtype=np.float32).astype(ml_dtypes.bfloat16)
    mask = np.where(np.arange(128)[None, :] <= np.arange(128)[:, None],
                    np.float32(0.0), np.float32(NEG_BIG)).astype(ml_dtypes.bfloat16)
    import jax
    import jax.numpy as jnp
    with jax.default_device(jax.devices("cpu")[0]):
        inv_freq = 1.0 / 10000.0 ** (jnp.arange(0, 64, 2, dtype=jnp.float32) / 64)
        freqs = jnp.arange(T, dtype=jnp.float32)[:, None] * inv_freq[None, :]
        cos_t = np.asarray(jnp.cos(freqs), dtype=np.float32)
        sin_t = np.asarray(jnp.sin(freqs), dtype=np.float32)
    _consts_cache.update(ident=ident, mask=mask, cos_t=cos_t, sin_t=sin_t)
    return _consts_cache


_prog_cache = {}


def kernel(L0_attn_x, L0_attn_W_qkv, L0_attn_W_out, L0_attn_hyp_k,
           L0_ffn_w1, L0_ffn_b1, L0_ffn_w2, L0_ffn_b2):
    x = np.ascontiguousarray(np.asarray(L0_attn_x, dtype=np.float32))
    wqkv = np.ascontiguousarray(np.asarray(L0_attn_W_qkv, dtype=np.float32))
    wo = np.ascontiguousarray(np.asarray(L0_attn_W_out, dtype=np.float32))
    hk = np.ascontiguousarray(np.asarray(L0_attn_hyp_k, dtype=np.float32)
                              .reshape(1, NH))
    w1 = np.ascontiguousarray(np.asarray(L0_ffn_w1, dtype=np.float32))
    b1 = np.ascontiguousarray(np.asarray(L0_ffn_b1, dtype=np.float32)
                              .reshape(32, 128).T)
    w2 = np.ascontiguousarray(np.asarray(L0_ffn_w2, dtype=np.float32))
    b2 = np.ascontiguousarray(np.asarray(L0_ffn_b2, dtype=np.float32)
                              .reshape(1, C))
    cst = _host_consts()

    if "nc" not in _prog_cache:
        _prog_cache["nc"] = build_core_program()
    nc = _prog_cache["nc"]

    in_maps = []
    for core in range(8):
        in_maps.append({
            "x_in": x[B_LOC * core:B_LOC * core + B_LOC],
            "wqkv": wqkv, "wout": wo, "hypk": hk,
            "w1": w1, "b1": b1, "w2": w2, "b2": b2,
            "ident": cst["ident"], "mask": cst["mask"],
            "cos_t": cst["cos_t"], "sin_t": cst["sin_t"],
        })
    res = run_bass_kernel_spmd(nc, in_maps, core_ids=list(range(8)))
    y = np.concatenate([np.asarray(r["y"], dtype=np.float32)
                        for r in res.results], axis=0)
    return y


# version 06
# speedup vs baseline: 1.0639x; 1.0639x over previous
import numpy as np
import concourse.bass as bass
import concourse.mybir as mybir
from concourse.bass_utils import run_bass_kernel_spmd

F32 = mybir.dt.float32
BF16 = mybir.dt.bfloat16
FP16 = mybir.dt.float16
I32 = mybir.dt.int32
AF = mybir.ActivationFunctionType
ALU = mybir.AluOpType
AX = mybir.AxisListType

T = 2048
C = 1024
NH = 16
NT = T // 128
B_LOC = 2
EPS_ATT = 1e-6
EPS_LN = 1e-5
NEG_BIG = -1.0e30
INV_LN2 = float(np.float32(1.0 / np.log(2.0)))

V, S, TE, G, SY = "vector", "scalar", "tensor", "gpsimd", "sync"


class Res:
    __slots__ = ("w", "rs")

    def __init__(self):
        self.w = None
        self.rs = []


class Prog:
    ENGINES = (SY, V, S, TE, G)
    SEM = {SY: "ld", V: "sv", S: "ss", TE: "st", G: "sg"}

    def __init__(self):
        self.steps = {e: [] for e in self.ENGINES}
        self.counts = {"ld": 0, "sv": 0, "ss": 0, "st": 0, "sg": 0, "gst": 0}
        self.hwm = {}

    def _emit(self, eng, fn, deps, sem, amt, is_dma=False):
        mysems = ("st",) if eng == TE else ()
        waits = []
        for d in deps:
            if d is None:
                continue
            dsem, dval = d
            if dsem in mysems:
                continue
            if dsem in ("ld", "gst"):
                dval = self.counts[dsem]
                if dval == 0:
                    continue
            key = (eng, dsem)
            if self.hwm.get(key, -1) < dval:
                self.hwm[key] = dval
                waits.append((dsem, dval))
        self.counts[sem] += amt
        tok = (sem, self.counts[sem])
        self.steps[eng].append([fn, waits, sem, amt, tok])
        return tok

    def op(self, eng, fn, reads=(), writes=(), deps=()):
        dd = [r.w for r in reads if r is not None] + list(deps)
        for wres in writes:
            dd.append(wres.w)
            dd.extend(wres.rs)
        sem = self.SEM[eng] if eng != G else "sg"
        tok = self._emit(eng, fn, dd, sem, 1)
        for r in reads:
            if r is not None:
                r.rs.append(tok)
        for wres in writes:
            wres.w = tok
            wres.rs = []
        return tok

    def dma(self, eng, dst_ap, src_ap, reads=(), writes=(), deps=()):
        dd = [r.w for r in reads if r is not None] + list(deps)
        for wres in writes:
            dd.append(wres.w)
            dd.extend(wres.rs)
        sem = "ld" if eng == SY else "gst"
        tok = self._emit(eng, lambda e, d=dst_ap, r=src_ap: e.dma_start(out=d, in_=r),
                         dd, sem, 16, is_dma=True)
        for r in reads:
            if r is not None:
                r.rs.append(tok)
        for wres in writes:
            wres.w = tok
            wres.rs = []
        return tok

    def replay(self, nc, final_sem_totals):
        waited = {}
        for e in self.ENGINES:
            for _, waits, _, _, _ in self.steps[e]:
                for sem, val in waits:
                    waited.setdefault(sem, set()).add(val)
        remap = {}
        newcount = {k: 0 for k in self.counts}
        order = {}
        for e in self.ENGINES:
            for st in self.steps[e]:
                order.setdefault(st[2], []).append(st)
        for sem, sts in order.items():
            sts.sort(key=lambda st: st[4][1])
            for st in sts:
                keep = (st[4][1] in waited.get(sem, ())) or sem in ("ld", "gst")
                if keep:
                    newcount[sem] += st[3]
                    remap[(sem, st[4][1])] = newcount[sem]
                else:
                    st[3] = 0
        final_gst = newcount["gst"]

        waited_new = {}
        for e in self.ENGINES:
            for _, waits, _, _, _ in self.steps[e]:
                for sem, val in waits:
                    waited_new.setdefault(sem, set()).add(remap[(sem, val)])

        from contextlib import ExitStack
        with ExitStack() as sctx:
            sems = {k: sctx.enter_context(nc.semaphore(k)) for k in self.counts}
            with nc.Block() as block:
                def make(name):
                    def body(eng):
                        qpos = {"ld": 0, "gst": 0}
                        for fn, waits, sem, amt, tok in self.steps[name]:
                            for wsem, wval in waits:
                                eng.wait_ge(sems[wsem], remap[(wsem, wval)])
                            if amt and sem in qpos:
                                pre = remap[(sem, tok[1])] - amt
                                if pre in waited_new.get(sem, ()):
                                    eng.wait_ge(sems[sem], pre)
                            ins = fn(eng)
                            if amt:
                                ins.then_inc(sems[sem], amt)
                        if name == G and final_gst:
                            eng.wait_ge(sems["gst"], final_gst)
                    return body
                block.sync(make(SY))
                block.vector(make(V))
                block.scalar(make(S))
                block.tensor(make(TE))
                block.gpsimd(make(G))


def build_core_program(dump=None):
    nc = bass.Bass("TRN2", target_bir_lowering=False)

    x_in = nc.dram_tensor("x_in", [B_LOC, T, C], F32, kind="ExternalInput").ap()
    wqkv = nc.dram_tensor("wqkv", [3 * C, C], F32, kind="ExternalInput").ap()
    wout = nc.dram_tensor("wout", [C, C], F32, kind="ExternalInput").ap()
    hypk = nc.dram_tensor("hypk", [1, NH], F32, kind="ExternalInput").ap()
    w1 = nc.dram_tensor("w1", [4 * C, C], F32, kind="ExternalInput").ap()
    b1 = nc.dram_tensor("b1", [128, 32], F32, kind="ExternalInput").ap()
    w2 = nc.dram_tensor("w2", [C, 4 * C], F32, kind="ExternalInput").ap()
    b2 = nc.dram_tensor("b2", [1, C], F32, kind="ExternalInput").ap()
    ident_in = nc.dram_tensor("ident", [128, 128], BF16, kind="ExternalInput").ap()
    mask_in = nc.dram_tensor("mask", [128, 128], BF16, kind="ExternalInput").ap()
    cos_in = nc.dram_tensor("cos_t", [T, 32], F32, kind="ExternalInput").ap()
    sin_in = nc.dram_tensor("sin_t", [T, 32], F32, kind="ExternalInput").ap()
    y_out = nc.dram_tensor("y", [B_LOC, T, C], F32, kind="ExternalOutput").ap()

    w1t_d = nc.dram_tensor("w1t_d", [128, 32 * C], BF16, kind="Internal").ap()
    w2qt_d = nc.dram_tensor("w2qt_d", [128, 32 * C], BF16, kind="Internal").ap()
    ares_d = nc.dram_tensor("ares_d", [T, C], FP16, kind="Internal").ap()

    dumps = {}
    def dump_t(name, shape, dt):
        dumps[name] = nc.dram_tensor(name, shape, dt, kind="ExternalOutput").ap()
    if dump == "xnT":
        dump_t("d_xnT", [8, 128, T], BF16)
    elif dump == "qkv":
        dump_t("d_q", [2, 128, T], BF16)
        dump_t("d_k", [2, 128, T], BF16)
        dump_t("d_v", [2, 128, NT * 65], BF16)
    elif dump in ("att", "att1"):
        dump_t("d_ao", [8, 128, T], BF16)

    elif dump == "ares":
        dump_t("d_ares", [T, C], FP16)
    elif dump == "hq":
        dump_t("d_hq", [32, 128, 512], BF16)
    elif dump == "wprep":
        dump_t("d_w1t", [128, 32 * C], BF16)
        dump_t("d_w2qt", [128, 32 * C], BF16)

    P = Prog()
    from contextlib import ExitStack
    ctx = ExitStack()

    def sb(name, shape, dt):
        h = ctx.enter_context(nc.sbuf_tensor(name, shape, dt))
        return h, Res()

    def psum(name, shape, dt):
        h = ctx.enter_context(nc.psum_tensor(name, shape, dt))
        return h, Res()

    ident, r_ident = sb("identb", [128, 128], BF16)
    maskt, r_maskt = sb("maskt", [128, 128], BF16)
    cos_sb, r_cos = sb("cos_sb", [128, NT * 32], F32)
    sin_sb, r_sin = sb("sin_sb", [128, NT * 32], F32)
    b1_sb, r_b1 = sb("b1_sb", [128, 32], F32)
    b2_sb, r_b2 = sb("b2_sb", [1, C], F32)
    b2q_bc, r_b2bc = sb("b2q_bc", [128, C], FP16)
    hk_sb, r_hk = sb("hk_sb", [1, NH], F32)
    hkd, r_hkd = sb("hkd", [1, 8 * NH], F32)
    scl, r_scl = sb("scl", [128, 8 * NH], F32)
    ones_r, r_onesr = sb("ones_r", [1, 128], F32)
    ones_rb, r_onesrb = sb("ones_rb", [1, 128], BF16)
    ones64, r_ones64 = sb("ones64", [64, 1], BF16)
    O_NEGRK, O_RK, O_K2, O_A, O_B, O_KC = 0, NH, 2 * NH, 3 * NH, 4 * NH, 5 * NH

    xld, r_xld = zip(*[sb(f"xld{i}", [128, C], F32) for i in range(2)])
    xn_tok, r_xntok = sb("xn_tok", [128, C], BF16)
    xn_T, r_xnT = zip(*[sb(f"xnT{c}", [128, T], BF16) for c in range(8)])
    stat, r_stat = sb("stat", [128, 16], F32)

    wld, r_wld = zip(*[sb(f"wld{i}", [128, C], F32) for i in range(2)])
    wbf, r_wbf = sb("wbf", [128, 512], BF16)
    wbf2, r_wbf2 = sb("wbf2", [128, 512], BF16)
    wqkvT_g, r_wqkvT = zip(*[sb(f"wqkvTg{c}", [128, 3 * 128], BF16) for c in range(8)])
    woutT, r_woutT = zip(*[sb(f"woutT{c}", [128, 256], BF16) for c in range(8)])

    q_ext, r_qext = zip(*[sb(f"q_ext{a}", [65, T], BF16) for a in range(2)])
    k_ext, r_kext = zip(*[sb(f"k_ext{a}", [65, T], BF16) for a in range(2)])
    v_ext, r_vext = zip(*[sb(f"v_ext{a}", [128, NT * 65], BF16) for a in range(2)])
    roped, r_roped = sb("roped", [128, 256], BF16)
    ropet, r_ropet = sb("ropet", [128, 64], F32)
    q0row, r_q0row = sb("q0row", [1, 512], F32)
    q0rowb, r_q0rowb = sb("q0rowb", [1, 512], BF16)
    attn_out, r_aout = zip(*[sb(f"attn_out{cc}", [128, T], BF16) for cc in range(8)])
    ash, r_ash = sb("ash", [64, 128], BF16)
    afst, r_afst = zip(*[sb(f"afst{i}", [128, 512], FP16) for i in range(2)])
    arld, r_arld = zip(*[sb(f"arld{i}", [128, 512], FP16) for i in range(2)])

    arena, _r_arena = sb("arena", [128, 5120], F32)
    tb = [arena[:, 0:512], arena[:, 512:1024]]
    ub = [arena[:, 1024:1536], arena[:, 1536:2048]]
    r_s = arena[:, 2048:4096]
    h_sl = arena[:, 0:512]
    q1_sl = arena[:, 512:1024]
    q2_sl = arena[:, 1024:1536]
    q4_sl = arena[:, 1536:2048]
    q5_sl = arena[:, 2048:2560]
    of_sl = arena[:, 2560:3072]
    sg_sl = arena[:, 3072:3584]
    fin_sl = arena[:, 3584:4096]
    sq_sl = arena[:, 4608:5120]
    ar_res = {k: Res() for k in
              ("tb0", "tb1", "ub0", "ub1", "rs",
               "h", "q1", "q2", "q4", "q5", "of", "sg", "fin", "sq")}
    AR_CORE = ("tb0", "tb1", "ub0", "ub1", "rs",
               "h", "q1", "q2", "q4", "q5", "of", "sg", "fin")

    def arena_switch(keys=AR_CORE):
        toks = []
        for k in keys:
            r = ar_res[k]
            toks.append(r.w)
            toks.extend(r.rs)
        toks = [t for t in toks if t is not None]
        for k in keys:
            ar_res[k] = Res()
            ar_res[k].rs = list(toks)
        return toks

    sqb, r_sqb = sb("sqb", [128, 512], BF16)
    p_bf, r_pbf = sb("p_bf", [128, T], BF16)
    pT_bf, r_pT = sb("pT_bf", [128, T], BF16)
    mcol, r_mcol = sb("mcol", [128, 4], F32)
    ash2, r_ash2 = sb("ash2", [64, 128], BF16)
    qi, r_qi = sb("qi", [128, 512], I32)
    cstb, r_cstb = sb("cstb", [128, 6], F32)

    w1t_hc, r_w1thc = zip(*[sb(f"w1thc{i}", [128, C], BF16) for i in range(2)])
    w2s, r_w2s = zip(*[sb(f"w2s{i}", [128, 512], BF16) for i in range(2)])
    hq, r_hq = zip(*[sb(f"hq{c}", [128, 512], BF16) for c in range(32)])

    ps_p, r_psp = zip(*[psum(f"ps{i}", [128, 512], F32) for i in range(4)])
    ps_pv, r_pspv = psum("ps_pv", [128, 128], F32)
    ps_bc, r_psbc = psum("ps_bc", [128, 128], F32)
    pst, r_pst = zip(*[psum(f"pst{i}", [128, 128], BF16) for i in range(2)])
    ps_a, ps_b, ps_c, ps_d = ps_p
    r_psa, r_psb, r_psc, r_psd = r_psp

    op, dma = P.op, P.dma

    dma(SY, ident[:], ident_in, writes=[r_ident])
    dma(SY, maskt[:], mask_in, writes=[r_maskt])
    dma(SY, cos_sb[:].rearrange("p (n f) -> p n f", f=32),
        cos_in.rearrange("(n p) f -> p n f", p=128), writes=[r_cos])
    dma(SY, sin_sb[:].rearrange("p (n f) -> p n f", f=32),
        sin_in.rearrange("(n p) f -> p n f", p=128), writes=[r_sin])
    dma(SY, b1_sb[:], b1, writes=[r_b1])
    dma(SY, b2_sb[:], b2, writes=[r_b2])
    dma(SY, hk_sb[:], hypk, writes=[r_hk])

    op(V, lambda e: e.memset(ones_r[:], 1.0), writes=[r_onesr])
    op(V, lambda e: e.memset(ones_rb[:], 1.0), writes=[r_onesrb])
    op(V, lambda e: e.memset(ones64[:], 1.0), writes=[r_ones64])
    op(V, lambda e: e.memset(cstb[:, 0:1], 2.0 ** -30), writes=[r_cstb])
    op(V, lambda e: e.memset(cstb[:, 1:2], EPS_LN), writes=[r_cstb])
    op(V, lambda e: e.memset(cstb[:, 2:3], -1.0 + 2e-7), writes=[r_cstb])
    op(V, lambda e: e.memset(cstb[:, 3:4], 2.0), writes=[r_cstb])
    op(S, lambda e: e.activation(out=cstb[:, 3:4], in_=cstb[:, 3:4], func=AF.Ln),
       writes=[r_cstb])
    op(V, lambda e: e.tensor_scalar(out=cstb[:, 4:5], in0=cstb[:, 3:4], scalar1=-1.0,
                                    scalar2=None, op0=ALU.mult), writes=[r_cstb])

    op(V, lambda e: e.memset(hkd[:], 0.0), writes=[r_hkd])
    op(V, lambda e: e.memset(mcol[:], 0.0), writes=[r_mcol])
    op(V, lambda e: e.reciprocal(out=hkd[:, O_RK:O_RK + NH], in_=hk_sb[:]),
       reads=[r_hk], writes=[r_hkd])
    op(V, lambda e: e.tensor_scalar(out=hkd[:, O_NEGRK:O_NEGRK + NH],
                                    in0=hkd[:, O_RK:O_RK + NH], scalar1=-1.0,
                                    scalar2=None, op0=ALU.mult), writes=[r_hkd])
    op(S, lambda e: e.activation(out=hkd[:, O_A:O_A + NH], in_=hk_sb[:], func=AF.Ln),
       reads=[r_hk], writes=[r_hkd])
    op(S, lambda e: e.activation(out=hkd[:, O_A:O_A + NH], in_=hkd[:, O_A:O_A + NH],
                                 func=AF.Exp, scale=-0.5), writes=[r_hkd])
    op(V, lambda e: e.tensor_scalar(out=hkd[:, O_B:O_B + NH], in0=hkd[:, O_A:O_A + NH],
                                    scalar1=EPS_ATT, scalar2=None, op0=ALU.mult),
       writes=[r_hkd])
    op(TE, lambda e: e.matmul(out=ps_a[:, 0:8 * NH], lhsT=ones_r[0:1, :], rhs=hkd[0:1, :],
                              start=True, stop=True),
       reads=[r_onesr, r_hkd], writes=[r_psa])
    op(V, lambda e: e.tensor_copy(out=scl[:], in_=ps_a[:, 0:8 * NH]),
       reads=[r_psa], writes=[r_scl])

    def quant_chain(src_ap, dst_ap, man, minexp, res_src, res_dst,
                    c1, c2, c4, c5, ci, tw, sg, r_c1, r_c2, r_c4, r_c5, r_ci, r_sgn,
                    r_tw, signed, nrow=128):
        nlev = man + 2
        if signed:
            op(S, lambda e: e.activation(out=sg, in_=src_ap, func=AF.Sign),
               reads=[res_src], writes=[r_sgn])
            op(S, lambda e: e.activation(out=c5, in_=src_ap, func=AF.Abs),
               reads=[res_src], writes=[r_c5])
            ax, r_ax = c5, r_c5
        else:
            ax, r_ax = src_ap, res_src
        zb = cstb[0:nrow, 0:1]
        op(S, lambda e: e.activation(out=c1, in_=ax, func=AF.Ln, bias=zb),
           reads=[r_ax, r_cstb], writes=[r_c1])
        op(V, lambda e: e.tensor_scalar(out=c1, in0=c1, scalar1=INV_LN2, scalar2=63.5,
                                        op0=ALU.mult, op1=ALU.add), writes=[r_c1])
        op(V, lambda e: e.tensor_copy(out=ci, in_=c1), reads=[r_c1], writes=[r_ci])
        op(V, lambda e: e.tensor_copy(out=c1, in_=ci), reads=[r_ci], writes=[r_c1])
        op(V, lambda e: e.tensor_scalar(out=c2, in0=c1, scalar1=-1.0,
                                        scalar2=float(64 + man), op0=ALU.mult,
                                        op1=ALU.add), reads=[r_c1], writes=[r_c2])
        op(V, lambda e: e.tensor_scalar(out=c2, in0=c2, scalar1=float(man - minexp),
                                        scalar2=None, op0=ALU.min), writes=[r_c2])
        op(S, lambda e: e.activation(out=c4, in_=c2, func=AF.Exp,
                                     scale=cstb[0:nrow, 4:5]),
           reads=[r_c2, r_cstb], writes=[r_c4])
        op(S, lambda e: e.activation(out=c2, in_=c2, func=AF.Exp,
                                     scale=cstb[0:nrow, 3:4]),
           reads=[r_cstb], writes=[r_c2])
        op(V, lambda e: e.tensor_mul(out=c2, in0=ax, in1=c2),
           reads=[r_ax], writes=[r_c2])
        op(V, lambda e: e.tensor_copy(out=c1, in_=c2), reads=[r_c2], writes=[r_c1])
        for lev in range(nlev):
            stepv = float(2 ** (nlev - 1 - lev))
            thr = stepv - 0.5
            op(V, lambda e, th=thr, sv=stepv: e.tensor_scalar(
                out=ci, in0=c1, scalar1=th, scalar2=sv, op0=ALU.is_ge, op1=ALU.mult),
               reads=[r_c1], writes=[r_ci])
            op(V, lambda e: e.tensor_sub(out=c1, in0=c1, in1=ci),
               reads=[r_ci], writes=[r_c1])
        op(V, lambda e: e.tensor_sub(out=c2, in0=c2, in1=c1),
           reads=[r_c1], writes=[r_c2])
        tok = op(V, lambda e: e.tensor_mul(out=dst_ap, in0=c2, in1=c4),
                 reads=[r_c2, r_c4], writes=[res_dst])
        if signed:
            tok = op(V, lambda e: e.tensor_mul(out=dst_ap, in0=dst_ap, in1=sg),
                     reads=[r_sgn], writes=[res_dst])
        return tok

    for hf in range(2):
        quant_chain(b2_sb[0:1, 512 * hf:512 * hf + 512],
                    b2_sb[0:1, 512 * hf:512 * hf + 512], 10, -14, r_b2, r_b2,
                    q1_sl[0:1, 0:512], q2_sl[0:1, 0:512], q4_sl[0:1, 0:512],
                    q5_sl[0:1, 0:512], qi[0:1, 0:512], None,
                    sg_sl[0:1, 0:512],
                    ar_res["q1"], ar_res["q2"], ar_res["q4"], ar_res["q5"], r_qi,
                    ar_res["sg"], None, True, nrow=1)
    for hf in range(2):
        pq, rq = (ps_a, r_psa) if hf == 0 else (ps_b, r_psb)
        op(TE, lambda e, hf=hf, pq=pq: e.matmul(
            out=pq[:, 0:512], lhsT=ones_r[0:1, :], rhs=b2_sb[0:1, 512 * hf:512 * hf + 512],
            start=True, stop=True), reads=[r_onesr, r_b2], writes=[rq])
        op(V, lambda e, hf=hf, pq=pq: e.tensor_copy(
            out=b2q_bc[:, 512 * hf:512 * hf + 512], in_=pq[:, 0:512]),
           reads=[rq], writes=[r_b2bc])

    for hc in range(32):
        buf, rb = wld[hc % 2], r_wld[hc % 2]
        dma(SY, buf[:], w1[128 * hc:128 * hc + 128, :], writes=[rb])
        for hf in range(2):
            op(V, lambda e, b=buf, hf=hf: e.tensor_copy(
                out=wbf[:], in_=b[:, 512 * hf:512 * hf + 512]),
               reads=[rb], writes=[r_wbf])
            for c in range(4):
                op(TE, lambda e, c=c: e.transpose(out=pst[c % 2][:],
                                                  in_=wbf[:, 128 * c:128 * c + 128],
                                                  identity=ident[:]),
                   reads=[r_wbf, r_ident], writes=[r_pst[c % 2]])
                op(V, lambda e, c=c: e.tensor_copy(out=wbf2[:, 128 * c:128 * c + 128],
                                                   in_=pst[c % 2][:]),
                   reads=[r_pst[c % 2]], writes=[r_wbf2])
            dma(G, w1t_d[:, C * hc + 512 * hf:C * hc + 512 * hf + 512], wbf2[:],
                reads=[r_wbf2])

    for cc in range(8):
        for piece in range(8):
            buf, rb = wld[piece % 2], r_wld[piece % 2]
            dma(SY, buf[:, 0:512],
                w2[128 * cc:128 * cc + 128, 512 * piece:512 * piece + 512], writes=[rb])
            quant_chain(buf[:, 0:512], buf[:, 0:512], 3, -6, rb, rb,
                        q1_sl, q2_sl, q4_sl, q5_sl, qi[:, 0:512], None, sg_sl,
                        ar_res["q1"], ar_res["q2"], ar_res["q4"], ar_res["q5"], r_qi,
                        ar_res["sg"], None, True)
            op(V, lambda e, b=buf: e.tensor_copy(out=wbf[:, 0:512], in_=b[:, 0:512]),
               reads=[rb], writes=[r_wbf])
            for j in range(4):
                hcg = 4 * piece + j
                op(TE, lambda e, j=j: e.transpose(out=pst[j % 2][:],
                                                  in_=wbf[:, 128 * j:128 * j + 128],
                                                  identity=ident[:]),
                   reads=[r_wbf, r_ident], writes=[r_pst[j % 2]])
                op(V, lambda e, j=j: e.tensor_copy(out=wbf2[:, 128 * j:128 * j + 128],
                                                   in_=pst[j % 2][:]),
                   reads=[r_pst[j % 2]], writes=[r_wbf2])
                dma(G, w2qt_d[:, C * hcg + 128 * cc:C * hcg + 128 * cc + 128],
                    wbf2[:, 128 * j:128 * j + 128], reads=[r_wbf2])

    if dump == "wprep":
        for hc in range(64):
            dma(SY, wbf2[:], w1t_d[:, 512 * hc:512 * hc + 512], writes=[r_wbf2])
            dma(G, dumps["d_w1t"][:, 512 * hc:512 * hc + 512], wbf2[:],
                reads=[r_wbf2])
            dma(SY, wbf2[:], w2qt_d[:, 512 * hc:512 * hc + 512], writes=[r_wbf2])
            dma(G, dumps["d_w2qt"][:, 512 * hc:512 * hc + 512], wbf2[:],
                reads=[r_wbf2])

    nseq = B_LOC if dump is None else (0 if dump == "wprep" else 1)
    for sq in range(nseq):
        arena_switch()
        for i in range(NT):
            buf, rb = xld[i % 2], r_xld[i % 2]
            dma(SY, buf[:], x_in[sq, 128 * i:128 * i + 128, :], writes=[rb])
            op(V, lambda e, b=buf: e.reduce_sum(out=stat[:, 0:1], in_=b[:], axis=AX.X),
               reads=[rb], writes=[r_stat])
            for hf in range(2):
                op(V, lambda e, b=buf, hf=hf: e.tensor_mul(
                    out=sq_sl, in0=b[:, 512 * hf:512 * hf + 512],
                    in1=b[:, 512 * hf:512 * hf + 512]),
                   reads=[rb], writes=[ar_res["sq"]])
                op(V, lambda e, hf=hf: e.reduce_sum(out=stat[:, 1 + hf:2 + hf],
                                                    in_=sq_sl, axis=AX.X),
                   reads=[ar_res["sq"]], writes=[r_stat])
            op(V, lambda e: e.tensor_add(out=stat[:, 3:4], in0=stat[:, 1:2],
                                         in1=stat[:, 2:3]), writes=[r_stat])
            op(V, lambda e: e.tensor_scalar(out=stat[:, 4:5], in0=stat[:, 0:1],
                                            scalar1=1.0 / C, scalar2=None,
                                            op0=ALU.mult), writes=[r_stat])
            op(V, lambda e: e.tensor_scalar(out=stat[:, 5:6], in0=stat[:, 3:4],
                                            scalar1=1.0 / C, scalar2=None,
                                            op0=ALU.mult), writes=[r_stat])
            op(V, lambda e: e.tensor_mul(out=stat[:, 6:7], in0=stat[:, 4:5],
                                         in1=stat[:, 4:5]), writes=[r_stat])
            op(V, lambda e: e.tensor_sub(out=stat[:, 7:8], in0=stat[:, 5:6],
                                         in1=stat[:, 6:7]), writes=[r_stat])
            op(S, lambda e: e.activation(out=stat[:, 8:9], in_=stat[:, 7:8],
                                         func=AF.Ln, bias=cstb[:, 1:2]),
               reads=[r_stat, r_cstb], writes=[r_stat])
            op(S, lambda e: e.activation(out=stat[:, 9:10], in_=stat[:, 8:9],
                                         func=AF.Exp, scale=-0.5), writes=[r_stat])
            op(V, lambda e: e.tensor_scalar(out=stat[:, 10:11], in0=stat[:, 4:5],
                                            scalar1=-1.0, scalar2=None, op0=ALU.mult),
               writes=[r_stat])
            op(V, lambda e, b=buf: e.tensor_scalar(out=xn_tok[:], in0=b[:],
                                                   scalar1=stat[:, 10:11],
                                                   scalar2=stat[:, 9:10],
                                                   op0=ALU.add, op1=ALU.mult),
               reads=[rb, r_stat], writes=[r_xntok])
            for c in range(8):
                op(TE, lambda e, c=c: e.transpose(out=pst[c % 2][:],
                                                  in_=xn_tok[:, 128 * c:128 * c + 128],
                                                  identity=ident[:]),
                   reads=[r_xntok, r_ident], writes=[r_pst[c % 2]])
                op(V, lambda e, c=c, i=i: e.tensor_copy(
                    out=xn_T[c][:, 128 * i:128 * i + 128], in_=pst[c % 2][:]),
                   reads=[r_pst[c % 2]], writes=[r_xnT[c]])

        if dump == "xnT":
            for c in range(8):
                dma(G, dumps["d_xnT"][c], xn_T[c][:], reads=[r_xnT[c]])
            break

        ngroups = 1 if dump == "att1" else 8
        for g in range(ngroups):
            for part, rowb in ((0, 128 * g), (1, C + 128 * g), (2, 2 * C + 128 * g)):
                buf, rb = wld[part % 2], r_wld[part % 2]
                dma(SY, buf[:], wqkv[rowb:rowb + 128, :], writes=[rb])
                for hf in range(2):
                    op(V, lambda e, b=buf, hf=hf: e.tensor_copy(
                        out=wbf[:], in_=b[:, 512 * hf:512 * hf + 512]),
                       reads=[rb], writes=[r_wbf])
                    for cq in range(4):
                        c = 4 * hf + cq
                        op(TE, lambda e, cq=cq: e.transpose(
                            out=pst[cq % 2][:], in_=wbf[:, 128 * cq:128 * cq + 128],
                            identity=ident[:]),
                           reads=[r_wbf, r_ident], writes=[r_pst[cq % 2]])
                        op(V, lambda e, c=c, cq=cq, p=part: e.tensor_copy(
                            out=wqkvT_g[c][:, 128 * p:128 * p + 128],
                            in_=pst[cq % 2][:]),
                           reads=[r_pst[cq % 2]], writes=[r_wqkvT[c]])
            for a in range(2):
                op(V, lambda e, a=a: e.memset(v_ext[a][:], 1.0), writes=[r_vext[a]])

            for i in range(NT):
                pq, rpq = (ps_a, r_psa) if i % 2 == 0 else (ps_b, r_psb)
                for c in range(8):
                    op(TE, lambda e, c=c, i=i, pq=pq: e.matmul(
                        out=pq[:, 0:384], lhsT=xn_T[c][:, 128 * i:128 * i + 128],
                        rhs=wqkvT_g[c][:, 0:384], start=(c == 0), stop=(c == 7)),
                       reads=[r_xnT[c], r_wqkvT[c]], writes=[rpq])
                for qk in range(2):
                    for a in range(2):
                        base = 128 * qk + 64 * a
                        lo = slice(base, base + 32)
                        hi = slice(base + 32, base + 64)
                        cs = cos_sb[:, 32 * i:32 * i + 32]
                        sn = sin_sb[:, 32 * i:32 * i + 32]
                        op(V, lambda e, pq=pq, lo=lo, cs=cs: e.tensor_mul(
                            out=ropet[:, 0:32], in0=pq[:, lo], in1=cs),
                           reads=[rpq, r_cos], writes=[r_ropet])
                        op(V, lambda e, pq=pq, hi=hi, sn=sn: e.tensor_mul(
                            out=ropet[:, 32:64], in0=pq[:, hi], in1=sn),
                           reads=[rpq, r_sin], writes=[r_ropet])
                        op(V, lambda e, lo=lo: e.tensor_add(
                            out=roped[:, lo], in0=ropet[:, 0:32], in1=ropet[:, 32:64]),
                           reads=[r_ropet], writes=[r_roped])
                        op(V, lambda e, pq=pq, lo=lo, sn=sn: e.tensor_mul(
                            out=ropet[:, 0:32], in0=pq[:, lo], in1=sn),
                           reads=[rpq, r_sin], writes=[r_ropet])
                        op(V, lambda e, pq=pq, hi=hi, cs=cs: e.tensor_mul(
                            out=ropet[:, 32:64], in0=pq[:, hi], in1=cs),
                           reads=[rpq, r_cos], writes=[r_ropet])
                        op(V, lambda e, hi=hi: e.tensor_sub(
                            out=roped[:, hi], in0=ropet[:, 32:64], in1=ropet[:, 0:32]),
                           reads=[r_ropet], writes=[r_roped])
                for a in range(2):
                    op(V, lambda e, pq=pq, a=a, i=i: e.tensor_copy(
                        out=v_ext[a][:, 65 * i:65 * i + 64],
                        in_=pq[:, 256 + 64 * a:256 + 64 * a + 64]),
                       reads=[rpq], writes=[r_vext[a]])
                for qk in range(2):
                    for a in range(2):
                        idx = a
                        op(TE, lambda e, qk=qk, a=a, idx=idx: e.transpose(
                            out=pst[idx][0:64, :],
                            in_=roped[:, 128 * qk + 64 * a:128 * qk + 64 * a + 64],
                            identity=ident[:]),
                           reads=[r_roped, r_ident], writes=[r_pst[idx]])
                        if qk == 0:
                            h = 2 * g + a
                            op(V, lambda e, a=a, i=i, idx=idx, h=h: e.tensor_scalar(
                                out=q_ext[a][0:64, 128 * i:128 * i + 128],
                                in0=pst[idx][0:64, :],
                                scalar1=scl[0:64, O_NEGRK + h:O_NEGRK + h + 1],
                                scalar2=None, op0=ALU.mult),
                               reads=[r_pst[idx], r_scl], writes=[r_qext[a]])
                        else:
                            op(V, lambda e, a=a, i=i, idx=idx: e.tensor_copy(
                                out=k_ext[a][0:64, 128 * i:128 * i + 128],
                                in_=pst[idx][0:64, :]),
                               reads=[r_pst[idx]], writes=[r_kext[a]])
            for a in range(2):
                h = 2 * g + a
                for qk in range(2):
                    src, rsrc = (q_ext[a], r_qext[a]) if qk == 0 else (k_ext[a], r_kext[a])
                    bias_ap = hkd[0:1, O_RK + h:O_RK + h + 1] if qk == 0 \
                        else hk_sb[0:1, h:h + 1]
                    bias_res = r_hkd if qk == 0 else r_hk
                    dstt, rdst = (q_ext[a], r_qext[a]) if qk == 0 \
                        else (k_ext[a], r_kext[a])
                    for blk in range(4):
                        sl = slice(512 * blk, 512 * blk + 512)
                        op(V, lambda e, src=src, sl=sl: e.tensor_mul(
                            out=sq_sl[0:64, :], in0=src[0:64, sl], in1=src[0:64, sl]),
                           reads=[rsrc], writes=[ar_res["sq"]])
                        op(V, lambda e: e.tensor_copy(out=sqb[0:64, :],
                                                      in_=sq_sl[0:64, :]),
                           reads=[ar_res["sq"]], writes=[r_sqb])
                        op(TE, lambda e: e.matmul(out=ps_c[0:1, 0:512], lhsT=ones64[:],
                                                  rhs=sqb[0:64, :], start=True,
                                                  stop=True),
                           reads=[r_ones64, r_sqb], writes=[r_psc])
                        op(S, lambda e, b=bias_ap: e.activation(
                            out=q0row[:], in_=ps_c[0:1, 0:512], func=AF.Ln, bias=b),
                           reads=[bias_res, r_psc], writes=[r_q0row])
                        op(S, lambda e: e.activation(out=q0row[:], in_=q0row[:],
                                                     func=AF.Exp, scale=0.5),
                           writes=[r_q0row])
                        op(V, lambda e: e.tensor_copy(out=q0rowb[:], in_=q0row[:]),
                           reads=[r_q0row], writes=[r_q0rowb])
                        dma(G, dstt[64:65, sl], q0rowb[:], reads=[r_q0rowb],
                            writes=[rdst])

            if dump == "qkv":
                for a in range(2):
                    dma(G, dumps["d_q"][a][0:65, :], q_ext[a][:], reads=[r_qext[a]])
                    dma(G, dumps["d_k"][a][0:65, :], k_ext[a][:], reads=[r_kext[a]])
                    dma(G, dumps["d_v"][a], v_ext[a][:], reads=[r_vext[a]])
                break

            for a in range(2):
                h = 2 * g + a
                for lt in range(NT):
                    W = 128 * (lt + 1)
                    nb = (W + 511) // 512
                    for sbk in range(nb):
                        wd = min(512, W - 512 * sbk)
                        pq, rpq = (ps_a, r_psa) if sbk % 2 == 0 else (ps_b, r_psb)
                        tb_, rtb = tb[sbk % 2], ar_res["tb" + str(sbk % 2)]
                        ub_, rub = ub[sbk % 2], ar_res["ub" + str(sbk % 2)]
                        op(TE, lambda e, a=a, lt=lt, sbk=sbk, wd=wd, pq=pq: e.matmul(
                            out=pq[:, 0:wd], lhsT=q_ext[a][:, 128 * lt:128 * lt + 128],
                            rhs=k_ext[a][:, 512 * sbk:512 * sbk + wd],
                            start=True, stop=True),
                           reads=[r_qext[a], r_kext[a]], writes=[rpq])
                        op(V, lambda e, wd=wd, pq=pq, tb_=tb_: e.tensor_scalar(
                            out=tb_[:, 0:wd], in0=pq[:, 0:wd],
                            scalar1=1.0 + EPS_ATT, scalar2=None, op0=ALU.max),
                           reads=[rpq], writes=[rtb])
                        op(S, lambda e, wd=wd, tb_=tb_, ub_=ub_: e.activation(
                            out=ub_[:, 0:wd], in_=tb_[:, 0:wd], func=AF.Square),
                           reads=[rtb], writes=[rub])
                        op(S, lambda e, wd=wd, ub_=ub_: e.activation(
                            out=ub_[:, 0:wd], in_=ub_[:, 0:wd], func=AF.Ln,
                            bias=cstb[:, 2:3]), reads=[r_cstb], writes=[rub])
                        op(S, lambda e, wd=wd, ub_=ub_: e.activation(
                            out=ub_[:, 0:wd], in_=ub_[:, 0:wd], func=AF.Exp,
                            scale=0.5), writes=[rub])
                        op(V, lambda e, wd=wd, tb_=tb_, ub_=ub_: e.tensor_add(
                            out=tb_[:, 0:wd], in0=tb_[:, 0:wd], in1=ub_[:, 0:wd]),
                           reads=[rub], writes=[rtb])
                        op(S, lambda e, wd=wd, tb_=tb_: e.activation(
                            out=tb_[:, 0:wd], in_=tb_[:, 0:wd], func=AF.Ln),
                           writes=[rtb])
                        op(V, lambda e, wd=wd, tb_=tb_, h=h: e.tensor_scalar(
                            out=tb_[:, 0:wd], in0=tb_[:, 0:wd],
                            scalar1=scl[:, O_B + h:O_B + h + 1],
                            scalar2=None, op0=ALU.add),
                           reads=[r_scl], writes=[rtb])
                        op(V, lambda e, wd=wd, sbk=sbk, tb_=tb_: e.reciprocal(
                            out=r_s[:, 512 * sbk:512 * sbk + wd], in_=tb_[:, 0:wd]),
                           reads=[rtb], writes=[ar_res["rs"]])
                    op(V, lambda e, W=W: e.tensor_add(out=r_s[:, W - 128:W],
                                                      in0=r_s[:, W - 128:W],
                                                      in1=maskt[:]),
                       reads=[r_maskt], writes=[ar_res["rs"]])
                    op(V, lambda e, W=W: e.tensor_reduce(
                        out=mcol[:, 0:1], in_=r_s[:, 0:W], axis=AX.X, op=ALU.max,
                        negate=True),
                       reads=[ar_res["rs"]], writes=[r_mcol])
                    op(V, lambda e, h=h: e.tensor_scalar(
                        out=mcol[:, 1:2], in0=mcol[:, 0:1],
                        scalar1=scl[:, O_A + h:O_A + h + 1], scalar2=None,
                        op0=ALU.mult), reads=[r_scl], writes=[r_mcol])
                    op(S, lambda e, W=W, h=h: e.activation(
                        out=p_bf[:, 0:W], in_=r_s[:, 0:W], func=AF.Exp,
                        bias=mcol[:, 1:2], scale=scl[:, O_A + h:O_A + h + 1],
                        accum_out=mcol[:, 2:3]),
                       reads=[ar_res["rs"], r_mcol, r_scl],
                       writes=[r_pbf, r_mcol])
                    op(V, lambda e: e.reciprocal(out=mcol[:, 3:4], in_=mcol[:, 2:3]),
                       reads=[r_mcol], writes=[r_mcol])
                    op(V, lambda e, W=W: e.tensor_scalar(
                        out=p_bf[:, 0:W], in0=p_bf[:, 0:W], scalar1=mcol[:, 3:4],
                        scalar2=None, op0=ALU.mult),
                       reads=[r_mcol], writes=[r_pbf])
                    for cch in range(lt + 1):
                        op(TE, lambda e, cch=cch: e.transpose(
                            out=pst[cch % 2][:], in_=p_bf[:, 128 * cch:128 * cch + 128],
                            identity=ident[:]),
                           reads=[r_pbf, r_ident], writes=[r_pst[cch % 2]])
                        op(V, lambda e, cch=cch: e.tensor_copy(
                            out=pT_bf[:, 128 * cch:128 * cch + 128],
                            in_=pst[cch % 2][:]),
                           reads=[r_pst[cch % 2]], writes=[r_pT])
                    ppv, rppv = (ps_pv, r_pspv) if lt % 2 == 0 else (ps_bc, r_psbc)
                    for cch in range(lt + 1):
                        op(TE, lambda e, a=a, cch=cch, lt=lt, ppv=ppv: e.matmul(
                            out=ppv[0:65, 0:128],
                            lhsT=v_ext[a][:, 65 * cch:65 * cch + 65],
                            rhs=pT_bf[:, 128 * cch:128 * cch + 128],
                            start=(cch == 0), stop=(cch == lt)),
                           reads=[r_vext[a], r_pT], writes=[rppv])
                    if a == 0:
                        op(V, lambda e, g=g, lt=lt, ppv=ppv: e.tensor_copy(
                            out=attn_out[g][0:64, 128 * lt:128 * lt + 128],
                            in_=ppv[0:64, 0:128]),
                           reads=[rppv], writes=[r_aout[g]])
                    else:
                        ash_, rash = (ash, r_ash) if lt % 2 == 0 else (ash2, r_ash2)
                        op(V, lambda e, ppv=ppv, ash_=ash_: e.tensor_copy(
                            out=ash_[:], in_=ppv[0:64, 0:128]),
                           reads=[rppv], writes=[rash])
                        dma(G, attn_out[g][64:128, 128 * lt:128 * lt + 128], ash_[:],
                            reads=[rash], writes=[r_aout[g]])

        if dump == "qkv":
            break
        if dump in ("att", "att1"):
            for cc in range(ngroups):
                dma(G, dumps["d_ao"][cc], attn_out[cc][:], reads=[r_aout[cc]])
            break

        for qd in range(4):
            for j in range(2):
                buf, rb = wld[j % 2], r_wld[j % 2]
                dma(SY, buf[:], wout[256 * qd + 128 * j:256 * qd + 128 * j + 128, :],
                    writes=[rb])
                for hf in range(2):
                    op(V, lambda e, b=buf, hf=hf: e.tensor_copy(
                        out=wbf[:], in_=b[:, 512 * hf:512 * hf + 512]),
                       reads=[rb], writes=[r_wbf])
                    for cq in range(4):
                        c = 4 * hf + cq
                        op(TE, lambda e, cq=cq: e.transpose(
                            out=pst[cq % 2][:], in_=wbf[:, 128 * cq:128 * cq + 128],
                            identity=ident[:]),
                           reads=[r_wbf, r_ident], writes=[r_pst[cq % 2]])
                        op(V, lambda e, c=c, cq=cq, j=j: e.tensor_copy(
                            out=woutT[c][:, 128 * j:128 * j + 128],
                            in_=pst[cq % 2][:]),
                           reads=[r_pst[cq % 2]], writes=[r_woutT[c]])
            for i in range(NT):
                pq, rpq = (ps_a, r_psa) if i % 2 == 0 else (ps_b, r_psb)
                for c in range(8):
                    op(TE, lambda e, c=c, i=i, pq=pq: e.matmul(
                        out=pq[:, 0:256], lhsT=attn_out[c][:, 128 * i:128 * i + 128],
                        rhs=woutT[c][:, 0:256], start=(c == 0), stop=(c == 7)),
                       reads=[r_aout[c], r_woutT[c]], writes=[rpq])
                st, rst = afst[i % 2], r_afst[i % 2]
                op(V, lambda e, pq=pq, st=st: e.tensor_copy(out=st[:, 0:256],
                                                            in_=pq[:, 0:256]),
                   reads=[rpq], writes=[rst])
                dma(G, ares_d[128 * i:128 * i + 128, 256 * qd:256 * qd + 256],
                    st[:, 0:256], reads=[rst])

        if dump == "ares":
            for i in range(NT):
                for half in range(2):
                    st, rst = arld[i % 2], r_arld[i % 2]
                    dma(SY, st[:],
                        ares_d[128 * i:128 * i + 128, 512 * half:512 * half + 512],
                        writes=[rst])
                    dma(G, dumps["d_ares"][128 * i:128 * i + 128,
                                           512 * half:512 * half + 512],
                        st[:], reads=[rst])
            break

        arena_switch()
        for qt in range(4):
            tok0 = 512 * qt
            for hc in range(32):
                wbuf, rwb = w1t_hc[hc % 2], r_w1thc[hc % 2]
                dma(SY, wbuf[:], w1t_d[:, C * hc:C * hc + C], writes=[rwb])
                pq, rpq = (ps_c, r_psc) if hc % 2 == 0 else (ps_d, r_psd)
                for c in range(8):
                    op(TE, lambda e, c=c, pq=pq, wbuf=wbuf, tok0=tok0: e.matmul(
                        out=pq[:, 0:512], lhsT=wbuf[:, 128 * c:128 * c + 128],
                        rhs=xn_T[c][:, tok0:tok0 + 512], start=(c == 0), stop=(c == 7)),
                       reads=[rwb, r_xnT[c]], writes=[rpq])
                op(S, lambda e, pq=pq, hc=hc: e.activation(
                    out=h_sl, in_=pq[:, 0:512], func=AF.Relu,
                    bias=b1_sb[:, hc:hc + 1]),
                   reads=[rpq, r_b1], writes=[ar_res["h"]])
                quant_chain(h_sl, hq[hc][:], 3, -6, ar_res["h"], r_hq[hc],
                            q1_sl, q2_sl, q4_sl, q5_sl, qi[:, 0:512], None, sg_sl,
                            ar_res["q1"], ar_res["q2"], ar_res["q4"], ar_res["q5"],
                            r_qi, ar_res["sg"], None, False)
            if dump == "hq" and qt == 0:
                for hc in range(32):
                    dma(G, dumps["d_hq"][hc], hq[hc][:], reads=[r_hq[hc]])
                break
            for dh in range(2):
                for hc in range(32):
                    wb2, rw2 = w2s[hc % 2], r_w2s[hc % 2]
                    dma(SY, wb2[:],
                        w2qt_d[:, C * hc + 512 * dh:C * hc + 512 * dh + 512],
                        writes=[rw2])
                    for tl in range(4):
                        op(TE, lambda e, hc=hc, tl=tl, wb2=wb2: e.matmul(
                            out=ps_p[tl][:, 0:512],
                            lhsT=hq[hc][:, 128 * tl:128 * tl + 128], rhs=wb2[:],
                            start=(hc == 0), stop=(hc == 31)),
                           reads=[r_hq[hc], rw2], writes=[r_psp[tl]])
                for tl in range(4):
                    i = 4 * qt + tl
                    op(V, lambda e, tl=tl, dh=dh: e.tensor_add(
                        out=of_sl, in0=ps_p[tl][:, 0:512],
                        in1=b2q_bc[:, 512 * dh:512 * dh + 512]),
                       reads=[r_psp[tl], r_b2bc], writes=[ar_res["of"]])
                    quant_chain(of_sl, fin_sl, 10, -14, ar_res["of"], ar_res["fin"],
                                q1_sl, q2_sl, q4_sl, q5_sl, qi[:, 0:512], None,
                                sg_sl, ar_res["q1"], ar_res["q2"], ar_res["q4"],
                                ar_res["q5"], r_qi, ar_res["sg"], None, True)
                    xb, rxb = xld[tl % 2], r_xld[tl % 2]
                    dma(SY, xb[:, 0:512],
                        x_in[sq, 128 * i:128 * i + 128, 512 * dh:512 * dh + 512],
                        writes=[rxb])
                    ab, rab = arld[tl % 2], r_arld[tl % 2]
                    dma(SY, ab[:],
                        ares_d[128 * i:128 * i + 128, 512 * dh:512 * dh + 512],
                        writes=[rab])
                    op(V, lambda e, xb=xb: e.tensor_add(
                        out=fin_sl, in0=fin_sl, in1=xb[:, 0:512]),
                       reads=[rxb], writes=[ar_res["fin"]])
                    op(V, lambda e, ab=ab: e.tensor_add(
                        out=fin_sl, in0=fin_sl, in1=ab[:]),
                       reads=[rab], writes=[ar_res["fin"]])
                    dma(G, y_out[sq, 128 * i:128 * i + 128, 512 * dh:512 * dh + 512],
                        fin_sl, reads=[ar_res["fin"]])

    global _last_P
    _last_P = P
    P.replay(nc, None)
    ctx.close()
    return nc


_consts_cache = {}


def _host_consts():
    if _consts_cache:
        return _consts_cache
    import ml_dtypes
    ident = np.eye(128, dtype=np.float32).astype(ml_dtypes.bfloat16)
    mask = np.where(np.arange(128)[None, :] <= np.arange(128)[:, None],
                    np.float32(0.0), np.float32(NEG_BIG)).astype(ml_dtypes.bfloat16)
    import jax
    import jax.numpy as jnp
    with jax.default_device(jax.devices("cpu")[0]):
        inv_freq = 1.0 / 10000.0 ** (jnp.arange(0, 64, 2, dtype=jnp.float32) / 64)
        freqs = jnp.arange(T, dtype=jnp.float32)[:, None] * inv_freq[None, :]
        cos_t = np.asarray(jnp.cos(freqs), dtype=np.float32)
        sin_t = np.asarray(jnp.sin(freqs), dtype=np.float32)
    _consts_cache.update(ident=ident, mask=mask, cos_t=cos_t, sin_t=sin_t)
    return _consts_cache


_prog_cache = {}


def kernel(L0_attn_x, L0_attn_W_qkv, L0_attn_W_out, L0_attn_hyp_k,
           L0_ffn_w1, L0_ffn_b1, L0_ffn_w2, L0_ffn_b2):
    x = np.ascontiguousarray(np.asarray(L0_attn_x, dtype=np.float32))
    wqkv = np.ascontiguousarray(np.asarray(L0_attn_W_qkv, dtype=np.float32))
    wo = np.ascontiguousarray(np.asarray(L0_attn_W_out, dtype=np.float32))
    hk = np.ascontiguousarray(np.asarray(L0_attn_hyp_k, dtype=np.float32)
                              .reshape(1, NH))
    w1 = np.ascontiguousarray(np.asarray(L0_ffn_w1, dtype=np.float32))
    b1 = np.ascontiguousarray(np.asarray(L0_ffn_b1, dtype=np.float32)
                              .reshape(32, 128).T)
    w2 = np.ascontiguousarray(np.asarray(L0_ffn_w2, dtype=np.float32))
    b2 = np.ascontiguousarray(np.asarray(L0_ffn_b2, dtype=np.float32)
                              .reshape(1, C))
    cst = _host_consts()

    if "nc" not in _prog_cache:
        _prog_cache["nc"] = build_core_program()
    nc = _prog_cache["nc"]

    in_maps = []
    for core in range(8):
        in_maps.append({
            "x_in": x[B_LOC * core:B_LOC * core + B_LOC],
            "wqkv": wqkv, "wout": wo, "hypk": hk,
            "w1": w1, "b1": b1, "w2": w2, "b2": b2,
            "ident": cst["ident"], "mask": cst["mask"],
            "cos_t": cst["cos_t"], "sin_t": cst["sin_t"],
        })
    res = run_bass_kernel_spmd(nc, in_maps, core_ids=list(range(8)))
    y = np.concatenate([np.asarray(r["y"], dtype=np.float32)
                        for r in res.results], axis=0)
    return y


# version 07
# speedup vs baseline: 1.1330x; 1.0650x over previous
import numpy as np
import concourse.bass as bass
import concourse.mybir as mybir
from concourse.bass_utils import run_bass_kernel_spmd

F32 = mybir.dt.float32
BF16 = mybir.dt.bfloat16
FP16 = mybir.dt.float16
I32 = mybir.dt.int32
AF = mybir.ActivationFunctionType
ALU = mybir.AluOpType
AX = mybir.AxisListType

T = 2048
C = 1024
NH = 16
NT = T // 128
B_LOC = 2
EPS_ATT = 1e-6
EPS_LN = 1e-5
NEG_BIG = -1.0e30
INV_LN2 = float(np.float32(1.0 / np.log(2.0)))

V, S, TE, G, SY = "vector", "scalar", "tensor", "gpsimd", "sync"


class Res:
    __slots__ = ("w", "rs")

    def __init__(self):
        self.w = None
        self.rs = []


class Prog:
    ENGINES = (SY, V, S, TE, G)
    SEM = {SY: "ld", V: "sv", S: "ss", TE: "st", G: "sg"}
    DEF_COST = {SY: 1600, V: 700, S: 800, TE: 260, G: 900}

    def __init__(self):
        self.ops = []
        self.bysem = {}
        self.counts = {"ld": 0, "sv": 0, "ss": 0, "st": 0, "sg": 0, "gst": 0}

    def _emit(self, eng, fn, deps, sem, amt, cost, fuse):
        idx = len(self.ops)
        dd = sorted({self.bysem[d] for d in deps if d is not None})
        self.counts[sem] += amt
        tok = (sem, self.counts[sem])
        self.bysem[tok] = idx
        self.ops.append(dict(eng=eng, fn=fn, sem=sem, amt=amt, deps=dd,
                             cost=cost or self.DEF_COST[eng], fuse=fuse, idx=idx))
        return tok

    def op(self, eng, fn, reads=(), writes=(), deps=(), cost=None, fuse=False):
        dd = [r.w for r in reads if r is not None] + list(deps)
        for wres in writes:
            dd.append(wres.w)
            dd.extend(wres.rs)
        sem = self.SEM[eng] if eng != G else "sg"
        tok = self._emit(eng, fn, dd, sem, 1, cost, fuse)
        for r in reads:
            if r is not None:
                r.rs.append(tok)
        for wres in writes:
            wres.w = tok
            wres.rs = []
        return tok

    def dma(self, eng, dst_ap, src_ap, reads=(), writes=(), deps=(), cost=None):
        dd = [r.w for r in reads if r is not None] + list(deps)
        for wres in writes:
            dd.append(wres.w)
            dd.extend(wres.rs)
        sem = "ld" if eng == SY else "gst"
        tok = self._emit(eng, lambda e, d=dst_ap, r=src_ap: e.dma_start(out=d, in_=r),
                         dd, sem, 16, cost, False)
        for r in reads:
            if r is not None:
                r.rs.append(tok)
        for wres in writes:
            wres.w = tok
            wres.rs = []
        return tok

    def _schedule(self):
        import heapq
        n = len(self.ops)
        unit_of = list(range(n))
        units = {}
        last_unit_on_eng = {}
        for o in self.ops:
            if o["fuse"] and o["eng"] in last_unit_on_eng:
                u = last_unit_on_eng[o["eng"]]
            else:
                u = o["idx"]
            unit_of[o["idx"]] = u
            units.setdefault(u, []).append(o["idx"])
            last_unit_on_eng[o["eng"]] = u
    
        udeps = {u: set() for u in units}
        for o in self.ops:
            u = unit_of[o["idx"]]
            for d in o["deps"]:
                du = unit_of[d]
                if du != u:
                    udeps[u].add(du)
        nsucc = {u: 0 for u in units}
        succs = {u: [] for u in units}
        indeg = {}
        for u, ds in udeps.items():
            indeg[u] = len(ds)
            for d in ds:
                succs[d].append(u)
        ucost = {u: sum(self.ops[i]["cost"] for i in ids) for u, ids in units.items()}
        eng_free = {e: 0.0 for e in self.ENGINES}
        fin = {}
        ready = []
        for u in units:
            if indeg[u] == 0:
                heapq.heappush(ready, u)
        order = {e: [] for e in self.ENGINES}
        utime = {}
        import heapq as hq
        remaining = len(units)
        pending = []
        while ready:
            u = hq.heappop(ready)
            e = self.ops[u]["eng"]
            start = eng_free[e]
            for d in udeps[u]:
                start = max(start, fin[d])
            t = start + ucost[u]
            eng_free[e] = t
            fin[u] = t
            order[e].extend(units[u])
            for v in succs[u]:
                indeg[v] -= 1
                if indeg[v] == 0:
                    hq.heappush(ready, v)
            remaining -= 1
        assert remaining == 0, f"cycle: {remaining} units unscheduled"
        return order

    def replay(self, nc, final_sem_totals):
        order = self._schedule()
        newval = {}
        cnt = {k: 0 for k in self.counts}
        for e in self.ENGINES:
            for i in order[e]:
                o = self.ops[i]
                cnt[o["sem"]] += o["amt"]
                newval[i] = cnt[o["sem"]]
        waited_ops = set()
        for o in self.ops:
            for d in o["deps"]:
                if self.ops[d]["eng"] == TE and o["eng"] == TE:
                    continue
                waited_ops.add(d)
        final = {k: 0 for k in self.counts}
        remap = {}
        for e in self.ENGINES:
            for i in order[e]:
                o = self.ops[i]
                keep = (i in waited_ops) or o["sem"] in ("ld", "gst")
                if keep:
                    final[o["sem"]] += o["amt"]
                    remap[i] = final[o["sem"]]
                else:
                    o["amt"] = 0
        final_gst = final["gst"]
        waited_vals = {}
        for o in self.ops:
            for d in o["deps"]:
                if self.ops[d]["eng"] == TE and o["eng"] == TE:
                    continue
                waited_vals.setdefault(self.ops[d]["sem"], set()).add(remap[d])

        from contextlib import ExitStack
        with ExitStack() as sctx:
            sems = {k: sctx.enter_context(nc.semaphore(k)) for k in self.counts}
            with nc.Block() as block:
                def make(name):
                    def body(eng):
                        hwm = {}
                        pos_in_stream = {}
                        for rank, i in enumerate(order[name]):
                            pos_in_stream[i] = rank
                        for i in order[name]:
                            o = self.ops[i]
                            for d in o["deps"]:
                                dop = self.ops[d]
                                if dop["eng"] == TE and name == TE:
                                    continue
                                wv = remap[d]
                                dsem = dop["sem"]
                                key = dsem
                                if hwm.get(key, -1) < wv:
                                    hwm[key] = wv
                                    eng.wait_ge(sems[dsem], wv)
                            if o["amt"] and o["sem"] in ("ld", "gst"):
                                pre = remap.get(i, 0) - o["amt"]
                                if pre in waited_vals.get(o["sem"], ()):
                                    if hwm.get(o["sem"], -1) < pre:
                                        hwm[o["sem"]] = pre
                                        eng.wait_ge(sems[o["sem"]], pre)
                            ins = o["fn"](eng)
                            if o["amt"]:
                                ins.then_inc(sems[o["sem"]], o["amt"])
                        if name == G and final_gst:
                            eng.wait_ge(sems["gst"], final_gst)
                    return body
                block.sync(make(SY))
                block.vector(make(V))
                block.scalar(make(S))
                block.tensor(make(TE))
                block.gpsimd(make(G))


def build_core_program(dump=None):
    nc = bass.Bass("TRN2", target_bir_lowering=False)

    x_in = nc.dram_tensor("x_in", [B_LOC, T, C], F32, kind="ExternalInput").ap()
    wqkv = nc.dram_tensor("wqkv", [3 * C, C], F32, kind="ExternalInput").ap()
    wout = nc.dram_tensor("wout", [C, C], F32, kind="ExternalInput").ap()
    hypk = nc.dram_tensor("hypk", [1, NH], F32, kind="ExternalInput").ap()
    w1 = nc.dram_tensor("w1", [4 * C, C], F32, kind="ExternalInput").ap()
    b1 = nc.dram_tensor("b1", [128, 32], F32, kind="ExternalInput").ap()
    w2 = nc.dram_tensor("w2", [C, 4 * C], F32, kind="ExternalInput").ap()
    b2 = nc.dram_tensor("b2", [1, C], F32, kind="ExternalInput").ap()
    ident_in = nc.dram_tensor("ident", [128, 128], BF16, kind="ExternalInput").ap()
    mask_in = nc.dram_tensor("mask", [128, 128], BF16, kind="ExternalInput").ap()
    cos_in = nc.dram_tensor("cos_t", [T, 32], F32, kind="ExternalInput").ap()
    sin_in = nc.dram_tensor("sin_t", [T, 32], F32, kind="ExternalInput").ap()
    y_out = nc.dram_tensor("y", [B_LOC, T, C], F32, kind="ExternalOutput").ap()

    w1t_d = nc.dram_tensor("w1t_d", [128, 32 * C], BF16, kind="Internal").ap()
    w2qt_d = nc.dram_tensor("w2qt_d", [128, 32 * C], BF16, kind="Internal").ap()
    ares_d = nc.dram_tensor("ares_d", [T, C], FP16, kind="Internal").ap()

    dumps = {}
    def dump_t(name, shape, dt):
        dumps[name] = nc.dram_tensor(name, shape, dt, kind="ExternalOutput").ap()
    if dump == "xnT":
        dump_t("d_xnT", [8, 128, T], BF16)
    elif dump == "qkv":
        dump_t("d_q", [2, 128, T], BF16)
        dump_t("d_k", [2, 128, T], BF16)
        dump_t("d_v", [2, 128, NT * 65], BF16)
    elif dump in ("att", "att1"):
        dump_t("d_ao", [8, 128, T], BF16)

    elif dump == "ares":
        dump_t("d_ares", [T, C], FP16)
    elif dump == "hq":
        dump_t("d_hq", [32, 128, 512], BF16)
    elif dump == "wprep":
        dump_t("d_w1t", [128, 32 * C], BF16)
        dump_t("d_w2qt", [128, 32 * C], BF16)

    P = Prog()
    from contextlib import ExitStack
    ctx = ExitStack()

    def sb(name, shape, dt):
        h = ctx.enter_context(nc.sbuf_tensor(name, shape, dt))
        return h, Res()

    def psum(name, shape, dt):
        h = ctx.enter_context(nc.psum_tensor(name, shape, dt))
        return h, Res()

    ident, r_ident = sb("identb", [128, 128], BF16)
    maskt, r_maskt = sb("maskt", [128, 128], BF16)
    cos_sb, r_cos = sb("cos_sb", [128, NT * 32], F32)
    sin_sb, r_sin = sb("sin_sb", [128, NT * 32], F32)
    b1_sb, r_b1 = sb("b1_sb", [128, 32], F32)
    b2_sb, r_b2 = sb("b2_sb", [1, C], F32)
    b2q_bc, r_b2bc = sb("b2q_bc", [128, C], FP16)
    hk_sb, r_hk = sb("hk_sb", [1, NH], F32)
    hkd, r_hkd = sb("hkd", [1, 8 * NH], F32)
    scl, r_scl = sb("scl", [128, 8 * NH], F32)
    ones_r, r_onesr = sb("ones_r", [1, 128], F32)
    ones_rb, r_onesrb = sb("ones_rb", [1, 128], BF16)
    ones64, r_ones64 = sb("ones64", [64, 1], BF16)
    O_NEGRK, O_RK, O_K2, O_A, O_B, O_KC = 0, NH, 2 * NH, 3 * NH, 4 * NH, 5 * NH

    xld, r_xld = zip(*[sb(f"xld{i}", [128, C], F32) for i in range(2)])
    xn_tok, r_xntok = sb("xn_tok", [128, C], BF16)
    xn_T, r_xnT = zip(*[sb(f"xnT{c}", [128, T], BF16) for c in range(8)])
    stat, r_stat = sb("stat", [128, 16], F32)

    wld, r_wld = zip(*[sb(f"wld{i}", [128, C], F32) for i in range(2)])
    wbf, r_wbf = sb("wbf", [128, 512], BF16)
    wbf2, r_wbf2 = sb("wbf2", [128, 512], BF16)
    wqkvT_g, r_wqkvT = zip(*[sb(f"wqkvTg{c}", [128, 3 * 128], BF16) for c in range(8)])
    woutT, r_woutT = zip(*[sb(f"woutT{c}", [128, 256], BF16) for c in range(8)])

    q_ext, r_qext = zip(*[sb(f"q_ext{a}", [65, T], BF16) for a in range(2)])
    k_ext, r_kext = zip(*[sb(f"k_ext{a}", [65, T], BF16) for a in range(2)])
    v_ext, r_vext = zip(*[sb(f"v_ext{a}", [128, NT * 65], BF16) for a in range(2)])
    roped, r_roped = sb("roped", [128, 256], BF16)
    ropet, r_ropet = sb("ropet", [128, 64], F32)
    q0row, r_q0row = sb("q0row", [1, 512], F32)
    q0rowb, r_q0rowb = sb("q0rowb", [1, 512], BF16)
    attn_out, r_aout = zip(*[sb(f"attn_out{cc}", [128, T], BF16) for cc in range(8)])
    ash, r_ash = sb("ash", [64, 128], BF16)
    afst, r_afst = zip(*[sb(f"afst{i}", [128, 512], FP16) for i in range(2)])
    arld, r_arld = zip(*[sb(f"arld{i}", [128, 512], FP16) for i in range(2)])

    arena, _r_arena = sb("arena", [128, 5120], F32)
    tb = [arena[:, 0:512], arena[:, 512:1024]]
    ub = [arena[:, 1024:1536], arena[:, 1536:2048]]
    r_s = arena[:, 2048:4096]
    h_sl = arena[:, 0:512]
    q1_sl = arena[:, 512:1024]
    q2_sl = arena[:, 1024:1536]
    q4_sl = arena[:, 1536:2048]
    q5_sl = arena[:, 2048:2560]
    of_sl = arena[:, 2560:3072]
    sg_sl = arena[:, 3072:3584]
    fin_sl = arena[:, 3584:4096]
    sq_sl = arena[:, 4608:5120]
    ar_res = {k: Res() for k in
              ("tb0", "tb1", "ub0", "ub1", "rs",
               "h", "q1", "q2", "q4", "q5", "of", "sg", "fin", "sq")}
    AR_CORE = ("tb0", "tb1", "ub0", "ub1", "rs",
               "h", "q1", "q2", "q4", "q5", "of", "sg", "fin")

    def arena_switch(keys=AR_CORE):
        toks = []
        for k in keys:
            r = ar_res[k]
            toks.append(r.w)
            toks.extend(r.rs)
        toks = [t for t in toks if t is not None]
        for k in keys:
            ar_res[k] = Res()
            ar_res[k].rs = list(toks)
        return toks

    sqb, r_sqb = sb("sqb", [128, 512], BF16)
    p_bf, r_pbf = sb("p_bf", [128, T], BF16)
    pT_bf, r_pT = sb("pT_bf", [128, T], BF16)
    mcol, r_mcol = sb("mcol", [128, 4], F32)
    ash2, r_ash2 = sb("ash2", [64, 128], BF16)
    qi, r_qi = sb("qi", [128, 512], I32)
    cstb, r_cstb = sb("cstb", [128, 6], F32)

    w1t_hc, r_w1thc = zip(*[sb(f"w1thc{i}", [128, C], BF16) for i in range(2)])
    w2s, r_w2s = zip(*[sb(f"w2s{i}", [128, 512], BF16) for i in range(2)])
    hq, r_hq = zip(*[sb(f"hq{c}", [128, 512], BF16) for c in range(32)])

    ps_p, r_psp = zip(*[psum(f"ps{i}", [128, 512], F32) for i in range(4)])
    ps_pv, r_pspv = psum("ps_pv", [128, 128], F32)
    ps_bc, r_psbc = psum("ps_bc", [128, 128], F32)
    pst, r_pst = zip(*[psum(f"pst{i}", [128, 128], BF16) for i in range(2)])
    ps_a, ps_b, ps_c, ps_d = ps_p
    r_psa, r_psb, r_psc, r_psd = r_psp

    op, dma = P.op, P.dma

    dma(SY, ident[:], ident_in, writes=[r_ident])
    dma(SY, maskt[:], mask_in, writes=[r_maskt])
    dma(SY, cos_sb[:].rearrange("p (n f) -> p n f", f=32),
        cos_in.rearrange("(n p) f -> p n f", p=128), writes=[r_cos])
    dma(SY, sin_sb[:].rearrange("p (n f) -> p n f", f=32),
        sin_in.rearrange("(n p) f -> p n f", p=128), writes=[r_sin])
    dma(SY, b1_sb[:], b1, writes=[r_b1])
    dma(SY, b2_sb[:], b2, writes=[r_b2])
    dma(SY, hk_sb[:], hypk, writes=[r_hk])

    op(V, lambda e: e.memset(ones_r[:], 1.0), writes=[r_onesr])
    op(V, lambda e: e.memset(ones_rb[:], 1.0), writes=[r_onesrb])
    op(V, lambda e: e.memset(ones64[:], 1.0), writes=[r_ones64])
    op(V, lambda e: e.memset(cstb[:, 0:1], 2.0 ** -30), writes=[r_cstb])
    op(V, lambda e: e.memset(cstb[:, 1:2], EPS_LN), writes=[r_cstb])
    op(V, lambda e: e.memset(cstb[:, 2:3], -1.0 + 2e-7), writes=[r_cstb])
    op(V, lambda e: e.memset(cstb[:, 3:4], 2.0), writes=[r_cstb])
    op(S, lambda e: e.activation(out=cstb[:, 3:4], in_=cstb[:, 3:4], func=AF.Ln),
       writes=[r_cstb])
    op(V, lambda e: e.tensor_scalar(out=cstb[:, 4:5], in0=cstb[:, 3:4], scalar1=-1.0,
                                    scalar2=None, op0=ALU.mult), writes=[r_cstb])

    op(V, lambda e: e.memset(hkd[:], 0.0), writes=[r_hkd])
    op(V, lambda e: e.memset(mcol[:], 0.0), writes=[r_mcol])
    op(V, lambda e: e.reciprocal(out=hkd[:, O_RK:O_RK + NH], in_=hk_sb[:]),
       reads=[r_hk], writes=[r_hkd])
    op(V, lambda e: e.tensor_scalar(out=hkd[:, O_NEGRK:O_NEGRK + NH],
                                    in0=hkd[:, O_RK:O_RK + NH], scalar1=-1.0,
                                    scalar2=None, op0=ALU.mult), writes=[r_hkd])
    op(S, lambda e: e.activation(out=hkd[:, O_A:O_A + NH], in_=hk_sb[:], func=AF.Ln),
       reads=[r_hk], writes=[r_hkd])
    op(S, lambda e: e.activation(out=hkd[:, O_A:O_A + NH], in_=hkd[:, O_A:O_A + NH],
                                 func=AF.Exp, scale=-0.5), writes=[r_hkd])
    op(V, lambda e: e.tensor_scalar(out=hkd[:, O_B:O_B + NH], in0=hkd[:, O_A:O_A + NH],
                                    scalar1=EPS_ATT, scalar2=None, op0=ALU.mult),
       writes=[r_hkd])
    op(TE, lambda e: e.matmul(out=ps_a[:, 0:8 * NH], lhsT=ones_r[0:1, :], rhs=hkd[0:1, :],
                              start=True, stop=True),
       reads=[r_onesr, r_hkd], writes=[r_psa])
    op(V, lambda e: e.tensor_copy(out=scl[:], in_=ps_a[:, 0:8 * NH]),
       reads=[r_psa], writes=[r_scl])

    def quant_chain(src_ap, dst_ap, man, minexp, res_src, res_dst,
                    c1, c2, c4, c5, ci, tw, sg, r_c1, r_c2, r_c4, r_c5, r_ci, r_sgn,
                    r_tw, signed, nrow=128):
        nlev = man + 2
        if signed:
            op(S, lambda e: e.activation(out=sg, in_=src_ap, func=AF.Sign),
               reads=[res_src], writes=[r_sgn])
            op(S, lambda e: e.activation(out=c5, in_=src_ap, func=AF.Abs),
               reads=[res_src], writes=[r_c5])
            ax, r_ax = c5, r_c5
        else:
            ax, r_ax = src_ap, res_src
        zb = cstb[0:nrow, 0:1]
        op(S, lambda e: e.activation(out=c1, in_=ax, func=AF.Ln, bias=zb),
           reads=[r_ax, r_cstb], writes=[r_c1])
        op(V, lambda e: e.tensor_scalar(out=c1, in0=c1, scalar1=INV_LN2, scalar2=63.5,
                                        op0=ALU.mult, op1=ALU.add), writes=[r_c1])
        op(V, lambda e: e.tensor_copy(out=ci, in_=c1), reads=[r_c1], writes=[r_ci])
        op(V, lambda e: e.tensor_copy(out=c1, in_=ci), reads=[r_ci], writes=[r_c1])
        op(V, lambda e: e.tensor_scalar(out=c2, in0=c1, scalar1=-1.0,
                                        scalar2=float(64 + man), op0=ALU.mult,
                                        op1=ALU.add), reads=[r_c1], writes=[r_c2])
        op(V, lambda e: e.tensor_scalar(out=c2, in0=c2, scalar1=float(man - minexp),
                                        scalar2=None, op0=ALU.min), writes=[r_c2])
        op(S, lambda e: e.activation(out=c4, in_=c2, func=AF.Exp,
                                     scale=cstb[0:nrow, 4:5]),
           reads=[r_c2, r_cstb], writes=[r_c4])
        op(S, lambda e: e.activation(out=c2, in_=c2, func=AF.Exp,
                                     scale=cstb[0:nrow, 3:4]),
           reads=[r_cstb], writes=[r_c2])
        op(V, lambda e: e.tensor_mul(out=c2, in0=ax, in1=c2),
           reads=[r_ax], writes=[r_c2])
        op(V, lambda e: e.tensor_copy(out=c1, in_=c2), reads=[r_c2], writes=[r_c1])
        for lev in range(nlev):
            stepv = float(2 ** (nlev - 1 - lev))
            thr = stepv - 0.5
            op(V, lambda e, th=thr, sv=stepv: e.tensor_scalar(
                out=ci, in0=c1, scalar1=th, scalar2=sv, op0=ALU.is_ge, op1=ALU.mult),
               reads=[r_c1], writes=[r_ci])
            op(V, lambda e: e.tensor_sub(out=c1, in0=c1, in1=ci),
               reads=[r_ci], writes=[r_c1])
        op(V, lambda e: e.tensor_sub(out=c2, in0=c2, in1=c1),
           reads=[r_c1], writes=[r_c2])
        tok = op(V, lambda e: e.tensor_mul(out=dst_ap, in0=c2, in1=c4),
                 reads=[r_c2, r_c4], writes=[res_dst])
        if signed:
            tok = op(V, lambda e: e.tensor_mul(out=dst_ap, in0=dst_ap, in1=sg),
                     reads=[r_sgn], writes=[res_dst])
        return tok

    for hf in range(2):
        quant_chain(b2_sb[0:1, 512 * hf:512 * hf + 512],
                    b2_sb[0:1, 512 * hf:512 * hf + 512], 10, -14, r_b2, r_b2,
                    q1_sl[0:1, 0:512], q2_sl[0:1, 0:512], q4_sl[0:1, 0:512],
                    q5_sl[0:1, 0:512], qi[0:1, 0:512], None,
                    sg_sl[0:1, 0:512],
                    ar_res["q1"], ar_res["q2"], ar_res["q4"], ar_res["q5"], r_qi,
                    ar_res["sg"], None, True, nrow=1)
    for hf in range(2):
        pq, rq = (ps_a, r_psa) if hf == 0 else (ps_b, r_psb)
        op(TE, lambda e, hf=hf, pq=pq: e.matmul(
            out=pq[:, 0:512], lhsT=ones_r[0:1, :], rhs=b2_sb[0:1, 512 * hf:512 * hf + 512],
            start=True, stop=True), reads=[r_onesr, r_b2], writes=[rq])
        op(V, lambda e, hf=hf, pq=pq: e.tensor_copy(
            out=b2q_bc[:, 512 * hf:512 * hf + 512], in_=pq[:, 0:512]),
           reads=[rq], writes=[r_b2bc])

    for hc in range(32):
        buf, rb = wld[hc % 2], r_wld[hc % 2]
        dma(SY, buf[:], w1[128 * hc:128 * hc + 128, :], writes=[rb])
        for hf in range(2):
            op(V, lambda e, b=buf, hf=hf: e.tensor_copy(
                out=wbf[:], in_=b[:, 512 * hf:512 * hf + 512]),
               reads=[rb], writes=[r_wbf])
            for c in range(4):
                op(TE, lambda e, c=c: e.transpose(out=pst[c % 2][:],
                                                  in_=wbf[:, 128 * c:128 * c + 128],
                                                  identity=ident[:]),
                   reads=[r_wbf, r_ident], writes=[r_pst[c % 2]])
                op(V, lambda e, c=c: e.tensor_copy(out=wbf2[:, 128 * c:128 * c + 128],
                                                   in_=pst[c % 2][:]),
                   reads=[r_pst[c % 2]], writes=[r_wbf2])
            dma(G, w1t_d[:, C * hc + 512 * hf:C * hc + 512 * hf + 512], wbf2[:],
                reads=[r_wbf2])

    for cc in range(8):
        for piece in range(8):
            buf, rb = wld[piece % 2], r_wld[piece % 2]
            dma(SY, buf[:, 0:512],
                w2[128 * cc:128 * cc + 128, 512 * piece:512 * piece + 512], writes=[rb])
            quant_chain(buf[:, 0:512], buf[:, 0:512], 3, -6, rb, rb,
                        q1_sl, q2_sl, q4_sl, q5_sl, qi[:, 0:512], None, sg_sl,
                        ar_res["q1"], ar_res["q2"], ar_res["q4"], ar_res["q5"], r_qi,
                        ar_res["sg"], None, True)
            op(V, lambda e, b=buf: e.tensor_copy(out=wbf[:, 0:512], in_=b[:, 0:512]),
               reads=[rb], writes=[r_wbf])
            for j in range(4):
                hcg = 4 * piece + j
                op(TE, lambda e, j=j: e.transpose(out=pst[j % 2][:],
                                                  in_=wbf[:, 128 * j:128 * j + 128],
                                                  identity=ident[:]),
                   reads=[r_wbf, r_ident], writes=[r_pst[j % 2]])
                op(V, lambda e, j=j: e.tensor_copy(out=wbf2[:, 128 * j:128 * j + 128],
                                                   in_=pst[j % 2][:]),
                   reads=[r_pst[j % 2]], writes=[r_wbf2])
                dma(G, w2qt_d[:, C * hcg + 128 * cc:C * hcg + 128 * cc + 128],
                    wbf2[:, 128 * j:128 * j + 128], reads=[r_wbf2])

    if dump == "wprep":
        for hc in range(64):
            dma(SY, wbf2[:], w1t_d[:, 512 * hc:512 * hc + 512], writes=[r_wbf2])
            dma(G, dumps["d_w1t"][:, 512 * hc:512 * hc + 512], wbf2[:],
                reads=[r_wbf2])
            dma(SY, wbf2[:], w2qt_d[:, 512 * hc:512 * hc + 512], writes=[r_wbf2])
            dma(G, dumps["d_w2qt"][:, 512 * hc:512 * hc + 512], wbf2[:],
                reads=[r_wbf2])

    nseq = B_LOC if dump is None else (0 if dump == "wprep" else 1)
    for sq in range(nseq):
        arena_switch()
        for i in range(NT):
            buf, rb = xld[i % 2], r_xld[i % 2]
            dma(SY, buf[:], x_in[sq, 128 * i:128 * i + 128, :], writes=[rb])
            op(V, lambda e, b=buf: e.reduce_sum(out=stat[:, 0:1], in_=b[:], axis=AX.X),
               reads=[rb], writes=[r_stat])
            for hf in range(2):
                op(V, lambda e, b=buf, hf=hf: e.tensor_mul(
                    out=sq_sl, in0=b[:, 512 * hf:512 * hf + 512],
                    in1=b[:, 512 * hf:512 * hf + 512]),
                   reads=[rb], writes=[ar_res["sq"]])
                op(V, lambda e, hf=hf: e.reduce_sum(out=stat[:, 1 + hf:2 + hf],
                                                    in_=sq_sl, axis=AX.X),
                   reads=[ar_res["sq"]], writes=[r_stat])
            op(V, lambda e: e.tensor_add(out=stat[:, 3:4], in0=stat[:, 1:2],
                                         in1=stat[:, 2:3]), writes=[r_stat])
            op(V, lambda e: e.tensor_scalar(out=stat[:, 4:5], in0=stat[:, 0:1],
                                            scalar1=1.0 / C, scalar2=None,
                                            op0=ALU.mult), writes=[r_stat])
            op(V, lambda e: e.tensor_scalar(out=stat[:, 5:6], in0=stat[:, 3:4],
                                            scalar1=1.0 / C, scalar2=None,
                                            op0=ALU.mult), writes=[r_stat])
            op(V, lambda e: e.tensor_mul(out=stat[:, 6:7], in0=stat[:, 4:5],
                                         in1=stat[:, 4:5]), writes=[r_stat])
            op(V, lambda e: e.tensor_sub(out=stat[:, 7:8], in0=stat[:, 5:6],
                                         in1=stat[:, 6:7]), writes=[r_stat])
            op(S, lambda e: e.activation(out=stat[:, 8:9], in_=stat[:, 7:8],
                                         func=AF.Ln, bias=cstb[:, 1:2]),
               reads=[r_stat, r_cstb], writes=[r_stat])
            op(S, lambda e: e.activation(out=stat[:, 9:10], in_=stat[:, 8:9],
                                         func=AF.Exp, scale=-0.5), writes=[r_stat])
            op(V, lambda e: e.tensor_scalar(out=stat[:, 10:11], in0=stat[:, 4:5],
                                            scalar1=-1.0, scalar2=None, op0=ALU.mult),
               writes=[r_stat])
            op(V, lambda e, b=buf: e.tensor_scalar(out=xn_tok[:], in0=b[:],
                                                   scalar1=stat[:, 10:11],
                                                   scalar2=stat[:, 9:10],
                                                   op0=ALU.add, op1=ALU.mult),
               reads=[rb, r_stat], writes=[r_xntok])
            for c in range(8):
                op(TE, lambda e, c=c: e.transpose(out=pst[c % 2][:],
                                                  in_=xn_tok[:, 128 * c:128 * c + 128],
                                                  identity=ident[:]),
                   reads=[r_xntok, r_ident], writes=[r_pst[c % 2]])
                op(V, lambda e, c=c, i=i: e.tensor_copy(
                    out=xn_T[c][:, 128 * i:128 * i + 128], in_=pst[c % 2][:]),
                   reads=[r_pst[c % 2]], writes=[r_xnT[c]])

        if dump == "xnT":
            for c in range(8):
                dma(G, dumps["d_xnT"][c], xn_T[c][:], reads=[r_xnT[c]])
            break

        ngroups = 1 if dump == "att1" else 8
        for g in range(ngroups):
            for part, rowb in ((0, 128 * g), (1, C + 128 * g), (2, 2 * C + 128 * g)):
                buf, rb = wld[part % 2], r_wld[part % 2]
                dma(SY, buf[:], wqkv[rowb:rowb + 128, :], writes=[rb])
                for hf in range(2):
                    op(V, lambda e, b=buf, hf=hf: e.tensor_copy(
                        out=wbf[:], in_=b[:, 512 * hf:512 * hf + 512]),
                       reads=[rb], writes=[r_wbf])
                    for cq in range(4):
                        c = 4 * hf + cq
                        op(TE, lambda e, cq=cq: e.transpose(
                            out=pst[cq % 2][:], in_=wbf[:, 128 * cq:128 * cq + 128],
                            identity=ident[:]),
                           reads=[r_wbf, r_ident], writes=[r_pst[cq % 2]])
                        op(V, lambda e, c=c, cq=cq, p=part: e.tensor_copy(
                            out=wqkvT_g[c][:, 128 * p:128 * p + 128],
                            in_=pst[cq % 2][:]),
                           reads=[r_pst[cq % 2]], writes=[r_wqkvT[c]])
            for a in range(2):
                op(V, lambda e, a=a: e.memset(v_ext[a][:], 1.0), writes=[r_vext[a]])

            for i in range(NT):
                pq, rpq = (ps_a, r_psa) if i % 2 == 0 else (ps_b, r_psb)
                for c in range(8):
                    op(TE, lambda e, c=c, i=i, pq=pq: e.matmul(
                        out=pq[:, 0:384], lhsT=xn_T[c][:, 128 * i:128 * i + 128],
                        rhs=wqkvT_g[c][:, 0:384], start=(c == 0), stop=(c == 7)),
                       reads=[r_xnT[c], r_wqkvT[c]], writes=[rpq])
                for qk in range(2):
                    for a in range(2):
                        base = 128 * qk + 64 * a
                        lo = slice(base, base + 32)
                        hi = slice(base + 32, base + 64)
                        cs = cos_sb[:, 32 * i:32 * i + 32]
                        sn = sin_sb[:, 32 * i:32 * i + 32]
                        op(V, lambda e, pq=pq, lo=lo, cs=cs: e.tensor_mul(
                            out=ropet[:, 0:32], in0=pq[:, lo], in1=cs),
                           reads=[rpq, r_cos], writes=[r_ropet])
                        op(V, lambda e, pq=pq, hi=hi, sn=sn: e.tensor_mul(
                            out=ropet[:, 32:64], in0=pq[:, hi], in1=sn),
                           reads=[rpq, r_sin], writes=[r_ropet])
                        op(V, lambda e, lo=lo: e.tensor_add(
                            out=roped[:, lo], in0=ropet[:, 0:32], in1=ropet[:, 32:64]),
                           reads=[r_ropet], writes=[r_roped])
                        op(V, lambda e, pq=pq, lo=lo, sn=sn: e.tensor_mul(
                            out=ropet[:, 0:32], in0=pq[:, lo], in1=sn),
                           reads=[rpq, r_sin], writes=[r_ropet])
                        op(V, lambda e, pq=pq, hi=hi, cs=cs: e.tensor_mul(
                            out=ropet[:, 32:64], in0=pq[:, hi], in1=cs),
                           reads=[rpq, r_cos], writes=[r_ropet])
                        op(V, lambda e, hi=hi: e.tensor_sub(
                            out=roped[:, hi], in0=ropet[:, 32:64], in1=ropet[:, 0:32]),
                           reads=[r_ropet], writes=[r_roped])
                for a in range(2):
                    op(V, lambda e, pq=pq, a=a, i=i: e.tensor_copy(
                        out=v_ext[a][:, 65 * i:65 * i + 64],
                        in_=pq[:, 256 + 64 * a:256 + 64 * a + 64]),
                       reads=[rpq], writes=[r_vext[a]])
                for qk in range(2):
                    for a in range(2):
                        idx = a
                        op(TE, lambda e, qk=qk, a=a, idx=idx: e.transpose(
                            out=pst[idx][0:64, :],
                            in_=roped[:, 128 * qk + 64 * a:128 * qk + 64 * a + 64],
                            identity=ident[:]),
                           reads=[r_roped, r_ident], writes=[r_pst[idx]])
                        if qk == 0:
                            h = 2 * g + a
                            op(V, lambda e, a=a, i=i, idx=idx, h=h: e.tensor_scalar(
                                out=q_ext[a][0:64, 128 * i:128 * i + 128],
                                in0=pst[idx][0:64, :],
                                scalar1=scl[0:64, O_NEGRK + h:O_NEGRK + h + 1],
                                scalar2=None, op0=ALU.mult),
                               reads=[r_pst[idx], r_scl], writes=[r_qext[a]])
                        else:
                            op(V, lambda e, a=a, i=i, idx=idx: e.tensor_copy(
                                out=k_ext[a][0:64, 128 * i:128 * i + 128],
                                in_=pst[idx][0:64, :]),
                               reads=[r_pst[idx]], writes=[r_kext[a]])
            for a in range(2):
                h = 2 * g + a
                for qk in range(2):
                    src, rsrc = (q_ext[a], r_qext[a]) if qk == 0 else (k_ext[a], r_kext[a])
                    bias_ap = hkd[0:1, O_RK + h:O_RK + h + 1] if qk == 0 \
                        else hk_sb[0:1, h:h + 1]
                    bias_res = r_hkd if qk == 0 else r_hk
                    dstt, rdst = (q_ext[a], r_qext[a]) if qk == 0 \
                        else (k_ext[a], r_kext[a])
                    for blk in range(4):
                        sl = slice(512 * blk, 512 * blk + 512)
                        op(V, lambda e, src=src, sl=sl: e.tensor_mul(
                            out=sq_sl[0:64, :], in0=src[0:64, sl], in1=src[0:64, sl]),
                           reads=[rsrc], writes=[ar_res["sq"]])
                        op(V, lambda e: e.tensor_copy(out=sqb[0:64, :],
                                                      in_=sq_sl[0:64, :]),
                           reads=[ar_res["sq"]], writes=[r_sqb])
                        op(TE, lambda e: e.matmul(out=ps_c[0:1, 0:512], lhsT=ones64[:],
                                                  rhs=sqb[0:64, :], start=True,
                                                  stop=True),
                           reads=[r_ones64, r_sqb], writes=[r_psc])
                        op(S, lambda e, b=bias_ap: e.activation(
                            out=q0row[:], in_=ps_c[0:1, 0:512], func=AF.Ln, bias=b),
                           reads=[bias_res, r_psc], writes=[r_q0row])
                        op(S, lambda e: e.activation(out=q0row[:], in_=q0row[:],
                                                     func=AF.Exp, scale=0.5),
                           writes=[r_q0row])
                        op(V, lambda e: e.tensor_copy(out=q0rowb[:], in_=q0row[:]),
                           reads=[r_q0row], writes=[r_q0rowb])
                        dma(G, dstt[64:65, sl], q0rowb[:], reads=[r_q0rowb],
                            writes=[rdst])

            if dump == "qkv":
                for a in range(2):
                    dma(G, dumps["d_q"][a][0:65, :], q_ext[a][:], reads=[r_qext[a]])
                    dma(G, dumps["d_k"][a][0:65, :], k_ext[a][:], reads=[r_kext[a]])
                    dma(G, dumps["d_v"][a], v_ext[a][:], reads=[r_vext[a]])
                break

            for a in range(2):
                h = 2 * g + a
                for lt in range(NT):
                    W = 128 * (lt + 1)
                    nb = (W + 511) // 512
                    for sbk in range(nb):
                        wd = min(512, W - 512 * sbk)
                        pq, rpq = (ps_a, r_psa) if sbk % 2 == 0 else (ps_b, r_psb)
                        tb_, rtb = tb[sbk % 2], ar_res["tb" + str(sbk % 2)]
                        ub_, rub = ub[sbk % 2], ar_res["ub" + str(sbk % 2)]
                        op(TE, lambda e, a=a, lt=lt, sbk=sbk, wd=wd, pq=pq: e.matmul(
                            out=pq[:, 0:wd], lhsT=q_ext[a][:, 128 * lt:128 * lt + 128],
                            rhs=k_ext[a][:, 512 * sbk:512 * sbk + wd],
                            start=True, stop=True),
                           reads=[r_qext[a], r_kext[a]], writes=[rpq])
                        op(V, lambda e, wd=wd, pq=pq, tb_=tb_: e.tensor_scalar(
                            out=tb_[:, 0:wd], in0=pq[:, 0:wd],
                            scalar1=1.0 + EPS_ATT, scalar2=None, op0=ALU.max),
                           reads=[rpq], writes=[rtb])
                        op(S, lambda e, wd=wd, tb_=tb_, ub_=ub_: e.activation(
                            out=ub_[:, 0:wd], in_=tb_[:, 0:wd], func=AF.Square),
                           reads=[rtb], writes=[rub])
                        op(S, lambda e, wd=wd, ub_=ub_: e.activation(
                            out=ub_[:, 0:wd], in_=ub_[:, 0:wd], func=AF.Ln,
                            bias=cstb[:, 2:3]), reads=[r_cstb], writes=[rub])
                        op(S, lambda e, wd=wd, ub_=ub_: e.activation(
                            out=ub_[:, 0:wd], in_=ub_[:, 0:wd], func=AF.Exp,
                            scale=0.5), writes=[rub])
                        op(V, lambda e, wd=wd, tb_=tb_, ub_=ub_: e.tensor_add(
                            out=tb_[:, 0:wd], in0=tb_[:, 0:wd], in1=ub_[:, 0:wd]),
                           reads=[rub], writes=[rtb])
                        op(S, lambda e, wd=wd, tb_=tb_: e.activation(
                            out=tb_[:, 0:wd], in_=tb_[:, 0:wd], func=AF.Ln),
                           writes=[rtb])
                        op(V, lambda e, wd=wd, tb_=tb_, h=h: e.tensor_scalar(
                            out=tb_[:, 0:wd], in0=tb_[:, 0:wd],
                            scalar1=scl[:, O_B + h:O_B + h + 1],
                            scalar2=None, op0=ALU.add),
                           reads=[r_scl], writes=[rtb])
                        op(V, lambda e, wd=wd, sbk=sbk, tb_=tb_: e.reciprocal(
                            out=r_s[:, 512 * sbk:512 * sbk + wd], in_=tb_[:, 0:wd]),
                           reads=[rtb], writes=[ar_res["rs"]])
                    op(V, lambda e, W=W: e.tensor_add(out=r_s[:, W - 128:W],
                                                      in0=r_s[:, W - 128:W],
                                                      in1=maskt[:]),
                       reads=[r_maskt], writes=[ar_res["rs"]])
                    op(V, lambda e, W=W: e.tensor_reduce(
                        out=mcol[:, 0:1], in_=r_s[:, 0:W], axis=AX.X, op=ALU.max,
                        negate=True),
                       reads=[ar_res["rs"]], writes=[r_mcol])
                    op(V, lambda e, h=h: e.tensor_scalar(
                        out=mcol[:, 1:2], in0=mcol[:, 0:1],
                        scalar1=scl[:, O_A + h:O_A + h + 1], scalar2=None,
                        op0=ALU.mult), reads=[r_scl], writes=[r_mcol])
                    op(S, lambda e, W=W, h=h: e.activation(
                        out=p_bf[:, 0:W], in_=r_s[:, 0:W], func=AF.Exp,
                        bias=mcol[:, 1:2], scale=scl[:, O_A + h:O_A + h + 1],
                        accum_out=mcol[:, 2:3]),
                       reads=[ar_res["rs"], r_mcol, r_scl],
                       writes=[r_pbf, r_mcol])
                    op(V, lambda e: e.reciprocal(out=mcol[:, 3:4], in_=mcol[:, 2:3]),
                       reads=[r_mcol], writes=[r_mcol])
                    op(V, lambda e, W=W: e.tensor_scalar(
                        out=p_bf[:, 0:W], in0=p_bf[:, 0:W], scalar1=mcol[:, 3:4],
                        scalar2=None, op0=ALU.mult),
                       reads=[r_mcol], writes=[r_pbf])
                    for cch in range(lt + 1):
                        op(TE, lambda e, cch=cch: e.transpose(
                            out=pst[cch % 2][:], in_=p_bf[:, 128 * cch:128 * cch + 128],
                            identity=ident[:]),
                           reads=[r_pbf, r_ident], writes=[r_pst[cch % 2]])
                        op(V, lambda e, cch=cch: e.tensor_copy(
                            out=pT_bf[:, 128 * cch:128 * cch + 128],
                            in_=pst[cch % 2][:]),
                           reads=[r_pst[cch % 2]], writes=[r_pT])
                    ppv, rppv = (ps_pv, r_pspv) if lt % 2 == 0 else (ps_bc, r_psbc)
                    for cch in range(lt + 1):
                        op(TE, lambda e, a=a, cch=cch, lt=lt, ppv=ppv: e.matmul(
                            out=ppv[0:65, 0:128],
                            lhsT=v_ext[a][:, 65 * cch:65 * cch + 65],
                            rhs=pT_bf[:, 128 * cch:128 * cch + 128],
                            start=(cch == 0), stop=(cch == lt)),
                           reads=[r_vext[a], r_pT], writes=[rppv])
                    if a == 0:
                        op(V, lambda e, g=g, lt=lt, ppv=ppv: e.tensor_copy(
                            out=attn_out[g][0:64, 128 * lt:128 * lt + 128],
                            in_=ppv[0:64, 0:128]),
                           reads=[rppv], writes=[r_aout[g]])
                    else:
                        ash_, rash = (ash, r_ash) if lt % 2 == 0 else (ash2, r_ash2)
                        op(V, lambda e, ppv=ppv, ash_=ash_: e.tensor_copy(
                            out=ash_[:], in_=ppv[0:64, 0:128]),
                           reads=[rppv], writes=[rash])
                        dma(G, attn_out[g][64:128, 128 * lt:128 * lt + 128], ash_[:],
                            reads=[rash], writes=[r_aout[g]])

        if dump == "qkv":
            break
        if dump in ("att", "att1"):
            for cc in range(ngroups):
                dma(G, dumps["d_ao"][cc], attn_out[cc][:], reads=[r_aout[cc]])
            break

        for qd in range(4):
            for j in range(2):
                buf, rb = wld[j % 2], r_wld[j % 2]
                dma(SY, buf[:], wout[256 * qd + 128 * j:256 * qd + 128 * j + 128, :],
                    writes=[rb])
                for hf in range(2):
                    op(V, lambda e, b=buf, hf=hf: e.tensor_copy(
                        out=wbf[:], in_=b[:, 512 * hf:512 * hf + 512]),
                       reads=[rb], writes=[r_wbf])
                    for cq in range(4):
                        c = 4 * hf + cq
                        op(TE, lambda e, cq=cq: e.transpose(
                            out=pst[cq % 2][:], in_=wbf[:, 128 * cq:128 * cq + 128],
                            identity=ident[:]),
                           reads=[r_wbf, r_ident], writes=[r_pst[cq % 2]])
                        op(V, lambda e, c=c, cq=cq, j=j: e.tensor_copy(
                            out=woutT[c][:, 128 * j:128 * j + 128],
                            in_=pst[cq % 2][:]),
                           reads=[r_pst[cq % 2]], writes=[r_woutT[c]])
            for i in range(NT):
                pq, rpq = (ps_a, r_psa) if i % 2 == 0 else (ps_b, r_psb)
                for c in range(8):
                    op(TE, lambda e, c=c, i=i, pq=pq: e.matmul(
                        out=pq[:, 0:256], lhsT=attn_out[c][:, 128 * i:128 * i + 128],
                        rhs=woutT[c][:, 0:256], start=(c == 0), stop=(c == 7)),
                       reads=[r_aout[c], r_woutT[c]], writes=[rpq])
                st, rst = afst[i % 2], r_afst[i % 2]
                op(V, lambda e, pq=pq, st=st: e.tensor_copy(out=st[:, 0:256],
                                                            in_=pq[:, 0:256]),
                   reads=[rpq], writes=[rst])
                dma(G, ares_d[128 * i:128 * i + 128, 256 * qd:256 * qd + 256],
                    st[:, 0:256], reads=[rst])

        if dump == "ares":
            for i in range(NT):
                for half in range(2):
                    st, rst = arld[i % 2], r_arld[i % 2]
                    dma(SY, st[:],
                        ares_d[128 * i:128 * i + 128, 512 * half:512 * half + 512],
                        writes=[rst])
                    dma(G, dumps["d_ares"][128 * i:128 * i + 128,
                                           512 * half:512 * half + 512],
                        st[:], reads=[rst])
            break

        arena_switch()
        for qt in range(4):
            tok0 = 512 * qt
            for hc in range(32):
                wbuf, rwb = w1t_hc[hc % 2], r_w1thc[hc % 2]
                dma(SY, wbuf[:], w1t_d[:, C * hc:C * hc + C], writes=[rwb])
                pq, rpq = (ps_c, r_psc) if hc % 2 == 0 else (ps_d, r_psd)
                for c in range(8):
                    op(TE, lambda e, c=c, pq=pq, wbuf=wbuf, tok0=tok0: e.matmul(
                        out=pq[:, 0:512], lhsT=wbuf[:, 128 * c:128 * c + 128],
                        rhs=xn_T[c][:, tok0:tok0 + 512], start=(c == 0), stop=(c == 7)),
                       reads=[rwb, r_xnT[c]], writes=[rpq])
                op(S, lambda e, pq=pq, hc=hc: e.activation(
                    out=h_sl, in_=pq[:, 0:512], func=AF.Relu,
                    bias=b1_sb[:, hc:hc + 1]),
                   reads=[rpq, r_b1], writes=[ar_res["h"]])
                quant_chain(h_sl, hq[hc][:], 3, -6, ar_res["h"], r_hq[hc],
                            q1_sl, q2_sl, q4_sl, q5_sl, qi[:, 0:512], None, sg_sl,
                            ar_res["q1"], ar_res["q2"], ar_res["q4"], ar_res["q5"],
                            r_qi, ar_res["sg"], None, False)
            if dump == "hq" and qt == 0:
                for hc in range(32):
                    dma(G, dumps["d_hq"][hc], hq[hc][:], reads=[r_hq[hc]])
                break
            for dh in range(2):
                for hc in range(32):
                    wb2, rw2 = w2s[hc % 2], r_w2s[hc % 2]
                    dma(SY, wb2[:],
                        w2qt_d[:, C * hc + 512 * dh:C * hc + 512 * dh + 512],
                        writes=[rw2])
                    for tl in range(4):
                        op(TE, lambda e, hc=hc, tl=tl, wb2=wb2: e.matmul(
                            out=ps_p[tl][:, 0:512],
                            lhsT=hq[hc][:, 128 * tl:128 * tl + 128], rhs=wb2[:],
                            start=(hc == 0), stop=(hc == 31)),
                           reads=[r_hq[hc], rw2], writes=[r_psp[tl]])
                for tl in range(4):
                    i = 4 * qt + tl
                    op(V, lambda e, tl=tl, dh=dh: e.tensor_add(
                        out=of_sl, in0=ps_p[tl][:, 0:512],
                        in1=b2q_bc[:, 512 * dh:512 * dh + 512]),
                       reads=[r_psp[tl], r_b2bc], writes=[ar_res["of"]])
                    quant_chain(of_sl, fin_sl, 10, -14, ar_res["of"], ar_res["fin"],
                                q1_sl, q2_sl, q4_sl, q5_sl, qi[:, 0:512], None,
                                sg_sl, ar_res["q1"], ar_res["q2"], ar_res["q4"],
                                ar_res["q5"], r_qi, ar_res["sg"], None, True)
                    xb, rxb = xld[tl % 2], r_xld[tl % 2]
                    dma(SY, xb[:, 0:512],
                        x_in[sq, 128 * i:128 * i + 128, 512 * dh:512 * dh + 512],
                        writes=[rxb])
                    ab, rab = arld[tl % 2], r_arld[tl % 2]
                    dma(SY, ab[:],
                        ares_d[128 * i:128 * i + 128, 512 * dh:512 * dh + 512],
                        writes=[rab])
                    op(V, lambda e, xb=xb: e.tensor_add(
                        out=fin_sl, in0=fin_sl, in1=xb[:, 0:512]),
                       reads=[rxb], writes=[ar_res["fin"]])
                    op(V, lambda e, ab=ab: e.tensor_add(
                        out=fin_sl, in0=fin_sl, in1=ab[:]),
                       reads=[rab], writes=[ar_res["fin"]])
                    dma(G, y_out[sq, 128 * i:128 * i + 128, 512 * dh:512 * dh + 512],
                        fin_sl, reads=[ar_res["fin"]])

    global _last_P
    _last_P = P
    P.replay(nc, None)
    ctx.close()
    return nc


_consts_cache = {}


def _host_consts():
    if _consts_cache:
        return _consts_cache
    import ml_dtypes
    ident = np.eye(128, dtype=np.float32).astype(ml_dtypes.bfloat16)
    mask = np.where(np.arange(128)[None, :] <= np.arange(128)[:, None],
                    np.float32(0.0), np.float32(NEG_BIG)).astype(ml_dtypes.bfloat16)
    import jax
    import jax.numpy as jnp
    with jax.default_device(jax.devices("cpu")[0]):
        inv_freq = 1.0 / 10000.0 ** (jnp.arange(0, 64, 2, dtype=jnp.float32) / 64)
        freqs = jnp.arange(T, dtype=jnp.float32)[:, None] * inv_freq[None, :]
        cos_t = np.asarray(jnp.cos(freqs), dtype=np.float32)
        sin_t = np.asarray(jnp.sin(freqs), dtype=np.float32)
    _consts_cache.update(ident=ident, mask=mask, cos_t=cos_t, sin_t=sin_t)
    return _consts_cache


_prog_cache = {}


def kernel(L0_attn_x, L0_attn_W_qkv, L0_attn_W_out, L0_attn_hyp_k,
           L0_ffn_w1, L0_ffn_b1, L0_ffn_w2, L0_ffn_b2):
    x = np.ascontiguousarray(np.asarray(L0_attn_x, dtype=np.float32))
    wqkv = np.ascontiguousarray(np.asarray(L0_attn_W_qkv, dtype=np.float32))
    wo = np.ascontiguousarray(np.asarray(L0_attn_W_out, dtype=np.float32))
    hk = np.ascontiguousarray(np.asarray(L0_attn_hyp_k, dtype=np.float32)
                              .reshape(1, NH))
    w1 = np.ascontiguousarray(np.asarray(L0_ffn_w1, dtype=np.float32))
    b1 = np.ascontiguousarray(np.asarray(L0_ffn_b1, dtype=np.float32)
                              .reshape(32, 128).T)
    w2 = np.ascontiguousarray(np.asarray(L0_ffn_w2, dtype=np.float32))
    b2 = np.ascontiguousarray(np.asarray(L0_ffn_b2, dtype=np.float32)
                              .reshape(1, C))
    cst = _host_consts()

    if "nc" not in _prog_cache:
        _prog_cache["nc"] = build_core_program()
    nc = _prog_cache["nc"]

    in_maps = []
    for core in range(8):
        in_maps.append({
            "x_in": x[B_LOC * core:B_LOC * core + B_LOC],
            "wqkv": wqkv, "wout": wo, "hypk": hk,
            "w1": w1, "b1": b1, "w2": w2, "b2": b2,
            "ident": cst["ident"], "mask": cst["mask"],
            "cos_t": cst["cos_t"], "sin_t": cst["sin_t"],
        })
    res = run_bass_kernel_spmd(nc, in_maps, core_ids=list(range(8)))
    y = np.concatenate([np.asarray(r["y"], dtype=np.float32)
                        for r in res.results], axis=0)
    return y


# version 11
# speedup vs baseline: 1.1345x; 1.0013x over previous
import numpy as np
import concourse.bass as bass
import concourse.mybir as mybir
from concourse.bass_utils import run_bass_kernel_spmd

F32 = mybir.dt.float32
BF16 = mybir.dt.bfloat16
FP16 = mybir.dt.float16
I32 = mybir.dt.int32
AF = mybir.ActivationFunctionType
ALU = mybir.AluOpType
AX = mybir.AxisListType

T = 2048
C = 1024
NH = 16
NT = T // 128
B_LOC = 2
EPS_ATT = 1e-6
EPS_LN = 1e-5
NEG_BIG = -1.0e30
INV_LN2 = float(np.float32(1.0 / np.log(2.0)))

V, S, TE, G, SY = "vector", "scalar", "tensor", "gpsimd", "sync"


class Res:
    __slots__ = ("w", "rs")

    def __init__(self):
        self.w = None
        self.rs = []


class Prog:
    ENGINES = (SY, V, S, TE, G)
    SEM = {SY: "ld", V: "sv", S: "ss", TE: "st", G: "sg"}
    DEF_COST = {SY: 1600, V: 700, S: 800, TE: 260, G: 900}

    def __init__(self):
        self.ops = []
        self.bysem = {}
        self.counts = {"ld": 0, "sv": 0, "ss": 0, "st": 0, "sg": 0, "gst": 0}

    def _emit(self, eng, fn, deps, sem, amt, cost, fuse):
        idx = len(self.ops)
        dd = sorted({self.bysem[d] for d in deps if d is not None})
        self.counts[sem] += amt
        tok = (sem, self.counts[sem])
        self.bysem[tok] = idx
        self.ops.append(dict(eng=eng, fn=fn, sem=sem, amt=amt, deps=dd,
                             cost=cost or self.DEF_COST[eng], fuse=fuse, idx=idx))
        return tok

    def op(self, eng, fn, reads=(), writes=(), deps=(), cost=None, fuse=False):
        dd = [r.w for r in reads if r is not None] + list(deps)
        for wres in writes:
            dd.append(wres.w)
            dd.extend(wres.rs)
        sem = self.SEM[eng] if eng != G else "sg"
        tok = self._emit(eng, fn, dd, sem, 1, cost, fuse)
        for r in reads:
            if r is not None:
                r.rs.append(tok)
        for wres in writes:
            wres.w = tok
            wres.rs = []
        return tok

    def dma(self, eng, dst_ap, src_ap, reads=(), writes=(), deps=(), cost=None):
        dd = [r.w for r in reads if r is not None] + list(deps)
        for wres in writes:
            dd.append(wres.w)
            dd.extend(wres.rs)
        sem = "ld" if eng == SY else "gst"
        tok = self._emit(eng, lambda e, d=dst_ap, r=src_ap: e.dma_start(out=d, in_=r),
                         dd, sem, 16, cost, False)
        for r in reads:
            if r is not None:
                r.rs.append(tok)
        for wres in writes:
            wres.w = tok
            wres.rs = []
        return tok

    def _schedule(self):
        import heapq
        n = len(self.ops)
        unit_of = list(range(n))
        units = {}
        last_unit_on_eng = {}
        for o in self.ops:
            if o["fuse"] and o["eng"] in last_unit_on_eng:
                u = last_unit_on_eng[o["eng"]]
            else:
                u = o["idx"]
            unit_of[o["idx"]] = u
            units.setdefault(u, []).append(o["idx"])
            last_unit_on_eng[o["eng"]] = u
    
        udeps = {u: set() for u in units}
        for o in self.ops:
            u = unit_of[o["idx"]]
            for d in o["deps"]:
                du = unit_of[d]
                if du != u:
                    udeps[u].add(du)
        nsucc = {u: 0 for u in units}
        succs = {u: [] for u in units}
        indeg = {}
        for u, ds in udeps.items():
            indeg[u] = len(ds)
            for d in ds:
                succs[d].append(u)
        ucost = {u: sum(self.ops[i]["cost"] for i in ids) for u, ids in units.items()}
        eng_free = {e: 0.0 for e in self.ENGINES}
        fin = {}
        ready = []
        for u in units:
            if indeg[u] == 0:
                heapq.heappush(ready, u)
        order = {e: [] for e in self.ENGINES}
        utime = {}
        import heapq as hq
        remaining = len(units)
        pending = []
        while ready:
            u = hq.heappop(ready)
            e = self.ops[u]["eng"]
            start = eng_free[e]
            for d in udeps[u]:
                start = max(start, fin[d])
            t = start + ucost[u]
            eng_free[e] = t
            fin[u] = t
            order[e].extend(units[u])
            for v in succs[u]:
                indeg[v] -= 1
                if indeg[v] == 0:
                    hq.heappush(ready, v)
            remaining -= 1
        assert remaining == 0, f"cycle: {remaining} units unscheduled"
        return order

    def replay(self, nc, final_sem_totals):
        order = self._schedule()
        newval = {}
        cnt = {k: 0 for k in self.counts}
        for e in self.ENGINES:
            for i in order[e]:
                o = self.ops[i]
                cnt[o["sem"]] += o["amt"]
                newval[i] = cnt[o["sem"]]
        waited_ops = set()
        for o in self.ops:
            for d in o["deps"]:
                if self.ops[d]["eng"] == TE and o["eng"] == TE:
                    continue
                waited_ops.add(d)
        final = {k: 0 for k in self.counts}
        remap = {}
        for e in self.ENGINES:
            for i in order[e]:
                o = self.ops[i]
                keep = (i in waited_ops) or o["sem"] in ("ld", "gst")
                if keep:
                    final[o["sem"]] += o["amt"]
                    remap[i] = final[o["sem"]]
                else:
                    o["amt"] = 0
        final_gst = final["gst"]
        waited_vals = {}
        for o in self.ops:
            for d in o["deps"]:
                if self.ops[d]["eng"] == TE and o["eng"] == TE:
                    continue
                waited_vals.setdefault(self.ops[d]["sem"], set()).add(remap[d])

        from contextlib import ExitStack
        with ExitStack() as sctx:
            sems = {k: sctx.enter_context(nc.semaphore(k)) for k in self.counts}
            sctx.enter_context(nc.allow_low_precision(
                reason="deliberate bf16 carriers; fp32 kept where precision matters"))
            with nc.Block() as block:
                def make(name):
                    def body(eng):
                        hwm = {}
                        pos_in_stream = {}
                        for rank, i in enumerate(order[name]):
                            pos_in_stream[i] = rank
                        for i in order[name]:
                            o = self.ops[i]
                            for d in o["deps"]:
                                dop = self.ops[d]
                                if dop["eng"] == TE and name == TE:
                                    continue
                                wv = remap[d]
                                dsem = dop["sem"]
                                key = dsem
                                if hwm.get(key, -1) < wv:
                                    hwm[key] = wv
                                    eng.wait_ge(sems[dsem], wv)
                            if o["amt"] and o["sem"] in ("ld", "gst"):
                                pre = remap.get(i, 0) - o["amt"]
                                if pre in waited_vals.get(o["sem"], ()):
                                    if hwm.get(o["sem"], -1) < pre:
                                        hwm[o["sem"]] = pre
                                        eng.wait_ge(sems[o["sem"]], pre)
                            ins = o["fn"](eng)
                            if o["amt"]:
                                ins.then_inc(sems[o["sem"]], o["amt"])
                        if name == G and final_gst:
                            eng.wait_ge(sems["gst"], final_gst)
                    return body
                block.sync(make(SY))
                block.vector(make(V))
                block.scalar(make(S))
                block.tensor(make(TE))
                block.gpsimd(make(G))


def build_core_program(dump=None):
    nc = bass.Bass("TRN2", target_bir_lowering=False)

    x_in = nc.dram_tensor("x_in", [B_LOC, T, C], F32, kind="ExternalInput").ap()
    wqkv = nc.dram_tensor("wqkv", [3 * C, C], F32, kind="ExternalInput").ap()
    wout = nc.dram_tensor("wout", [C, C], F32, kind="ExternalInput").ap()
    hypk = nc.dram_tensor("hypk", [1, NH], F32, kind="ExternalInput").ap()
    w1 = nc.dram_tensor("w1", [4 * C, C], F32, kind="ExternalInput").ap()
    b1 = nc.dram_tensor("b1", [128, 32], F32, kind="ExternalInput").ap()
    w2 = nc.dram_tensor("w2", [C, 4 * C], F32, kind="ExternalInput").ap()
    b2 = nc.dram_tensor("b2", [1, C], F32, kind="ExternalInput").ap()
    ident_in = nc.dram_tensor("ident", [128, 128], BF16, kind="ExternalInput").ap()
    mask_in = nc.dram_tensor("mask", [128, 128], BF16, kind="ExternalInput").ap()
    cos_in = nc.dram_tensor("cos_t", [T, 32], F32, kind="ExternalInput").ap()
    sin_in = nc.dram_tensor("sin_t", [T, 32], F32, kind="ExternalInput").ap()
    y_out = nc.dram_tensor("y", [B_LOC, T, C], F32, kind="ExternalOutput").ap()

    w1t_d = nc.dram_tensor("w1t_d", [128, 32 * C], BF16, kind="Internal").ap()
    w2qt_d = nc.dram_tensor("w2qt_d", [128, 32 * C], BF16, kind="Internal").ap()
    ares_d = nc.dram_tensor("ares_d", [T, C], FP16, kind="Internal").ap()

    dumps = {}
    def dump_t(name, shape, dt):
        dumps[name] = nc.dram_tensor(name, shape, dt, kind="ExternalOutput").ap()
    if dump == "xnT":
        dump_t("d_xnT", [8, 128, T], BF16)
    elif dump == "qkv":
        dump_t("d_q", [2, 128, T], BF16)
        dump_t("d_k", [2, 128, T], BF16)
        dump_t("d_v", [2, 128, NT * 65], BF16)
    elif dump in ("att", "att1"):
        dump_t("d_ao", [8, 128, T], BF16)

    elif dump == "ares":
        dump_t("d_ares", [T, C], FP16)
    elif dump == "hq":
        dump_t("d_hq", [32, 128, 512], BF16)
    elif dump == "wprep":
        dump_t("d_w1t", [128, 32 * C], BF16)
        dump_t("d_w2qt", [128, 32 * C], BF16)

    P = Prog()
    from contextlib import ExitStack
    ctx = ExitStack()

    def sb(name, shape, dt):
        h = ctx.enter_context(nc.sbuf_tensor(name, shape, dt))
        return h, Res()

    def psum(name, shape, dt):
        h = ctx.enter_context(nc.psum_tensor(name, shape, dt))
        return h, Res()

    ident, r_ident = sb("identb", [128, 128], BF16)
    maskt, r_maskt = sb("maskt", [128, 128], BF16)
    cos_sb, r_cos = sb("cos_sb", [128, NT * 32], F32)
    sin_sb, r_sin = sb("sin_sb", [128, NT * 32], F32)
    b1_sb, r_b1 = sb("b1_sb", [128, 32], F32)
    b2_sb, r_b2 = sb("b2_sb", [1, C], F32)
    b2q_bc, r_b2bc = sb("b2q_bc", [128, C], FP16)
    hk_sb, r_hk = sb("hk_sb", [1, NH], F32)
    hkd, r_hkd = sb("hkd", [1, 8 * NH], F32)
    scl, r_scl = sb("scl", [128, 8 * NH], F32)
    ones_r, r_onesr = sb("ones_r", [1, 128], F32)
    ones_rb, r_onesrb = sb("ones_rb", [1, 128], BF16)
    ones64, r_ones64 = sb("ones64", [64, 1], BF16)
    O_NEGRK, O_RK, O_K2, O_A, O_B, O_KC = 0, NH, 2 * NH, 3 * NH, 4 * NH, 5 * NH

    xld, r_xld = zip(*[sb(f"xld{i}", [128, C], F32) for i in range(2)])
    xn_tok, r_xntok = sb("xn_tok", [128, C], BF16)
    xn_T, r_xnT = zip(*[sb(f"xnT{c}", [128, T], BF16) for c in range(8)])
    stat, r_stat = sb("stat", [128, 16], F32)

    wld, r_wld = zip(*[sb(f"wld{i}", [128, C], F32) for i in range(2)])
    wbf, r_wbf = sb("wbf", [128, 512], BF16)
    wbf2, r_wbf2 = sb("wbf2", [128, 512], BF16)
    wqkvT_g, r_wqkvT = zip(*[sb(f"wqkvTg{c}", [128, 3 * 128], BF16) for c in range(8)])
    woutT, r_woutT = zip(*[sb(f"woutT{c}", [128, 256], BF16) for c in range(8)])

    q_ext, r_qext = zip(*[sb(f"q_ext{a}", [65, T], BF16) for a in range(2)])
    k_ext, r_kext = zip(*[sb(f"k_ext{a}", [65, T], BF16) for a in range(2)])
    v_ext, r_vext = zip(*[sb(f"v_ext{a}", [128, NT * 65], BF16) for a in range(2)])
    roped, r_roped = sb("roped", [128, 256], BF16)
    ropet, r_ropet = sb("ropet", [128, 64], F32)

    attn_out, r_aout = zip(*[sb(f"attn_out{cc}", [128, T], BF16) for cc in range(8)])
    ash, r_ash = sb("ash", [64, 128], BF16)
    afst, r_afst = zip(*[sb(f"afst{i}", [128, 512], FP16) for i in range(2)])
    arld, r_arld = zip(*[sb(f"arld{i}", [128, 512], FP16) for i in range(2)])

    arena, _r_arena = sb("arena", [128, 4608], F32)
    tb = [arena[:, 0:512], arena[:, 512:1024]]
    ub = [arena[:, 1024:1536], arena[:, 1536:2048]]
    rsb_t, _r0 = sb("rsb", [128, 2 * T], BF16)
    rsb = [rsb_t[:, 0:T], rsb_t[:, T:2 * T]]
    h_sl = arena[:, 0:512]
    q1_sl = arena[:, 512:1024]
    q2_sl = arena[:, 1024:1536]
    q4_sl = arena[:, 1536:2048]
    q5_sl = arena[:, 2048:2560]
    of_sl = arena[:, 2560:3072]
    sg_sl = arena[:, 3072:3584]
    fin_sl = arena[:, 3584:4096]
    sq_sl = arena[:, 4096:4608]
    ar_res = {k: Res() for k in
              ("tb0", "tb1", "ub0", "ub1", "rs0", "rs1",
               "h", "q1", "q2", "q4", "q5", "of", "sg", "fin", "sq")}
    AR_CORE = ("tb0", "tb1", "ub0", "ub1",
               "h", "q1", "q2", "q4", "q5", "of", "sg", "fin")

    def arena_switch(keys=AR_CORE):
        toks = []
        for k in keys:
            r = ar_res[k]
            toks.append(r.w)
            toks.extend(r.rs)
        toks = [t for t in toks if t is not None]
        for k in keys:
            ar_res[k] = Res()
            ar_res[k].rs = list(toks)
        return toks

    sqb, r_sqb = sb("sqb", [128, 512], BF16)
    p_bf, r_pbf = sb("p_bf", [128, T], BF16)
    pT_bf, r_pT = sb("pT_bf", [128, T], BF16)
    mcol, _rm = sb("mcol", [128, 8], F32)
    r_mcol2 = [Res(), Res()]
    ash2, r_ash2 = sb("ash2", [64, 128], BF16)
    qi, r_qi = sb("qi", [128, 512], I32)
    cstb, r_cstb = sb("cstb", [128, 8], F32)

    w1t_hc, r_w1thc = zip(*[sb(f"w1thc{i}", [128, C], BF16) for i in range(2)])
    w2s, r_w2s = zip(*[sb(f"w2s{i}", [128, 512], BF16) for i in range(2)])
    hq, r_hq = zip(*[sb(f"hq{c}", [128, 512], BF16) for c in range(32)])

    ps_p, r_psp = zip(*[psum(f"ps{i}", [128, 512], F32) for i in range(4)])
    ps_pv, r_pspv = psum("ps_pv", [128, 128], F32)
    ps_bc, r_psbc = psum("ps_bc", [128, 128], F32)
    pst, r_pst = zip(*[psum(f"pst{i}", [128, 128], BF16) for i in range(2)])
    ps_a, ps_b, ps_c, ps_d = ps_p
    r_psa, r_psb, r_psc, r_psd = r_psp

    op, dma = P.op, P.dma

    dma(SY, ident[:], ident_in, writes=[r_ident])
    dma(SY, maskt[:], mask_in, writes=[r_maskt])
    dma(SY, cos_sb[:].rearrange("p (n f) -> p n f", f=32),
        cos_in.rearrange("(n p) f -> p n f", p=128), writes=[r_cos])
    dma(SY, sin_sb[:].rearrange("p (n f) -> p n f", f=32),
        sin_in.rearrange("(n p) f -> p n f", p=128), writes=[r_sin])
    dma(SY, b1_sb[:], b1, writes=[r_b1])
    dma(SY, b2_sb[:], b2, writes=[r_b2])
    dma(SY, hk_sb[:], hypk, writes=[r_hk])

    op(V, lambda e: e.memset(ones_r[:], 1.0), writes=[r_onesr])
    op(V, lambda e: e.memset(ones_rb[:], 1.0), writes=[r_onesrb])
    op(V, lambda e: e.memset(ones64[:], 1.0), writes=[r_ones64])
    op(V, lambda e: e.memset(cstb[:, 0:1], 2.0 ** -30), writes=[r_cstb])
    op(V, lambda e: e.memset(cstb[:, 1:2], EPS_LN), writes=[r_cstb])
    op(V, lambda e: e.memset(cstb[:, 2:3], -1.0 + 2e-7), writes=[r_cstb])
    op(V, lambda e: e.memset(cstb[:, 3:4], 2.0), writes=[r_cstb])
    op(V, lambda e: e.memset(cstb[:, 5:6], 1.0 + EPS_ATT), writes=[r_cstb])
    op(V, lambda e: e.memset(cstb[:, 6:7], -(1.0 + EPS_ATT)), writes=[r_cstb])
    op(S, lambda e: e.activation(out=cstb[:, 3:4], in_=cstb[:, 3:4], func=AF.Ln),
       writes=[r_cstb])
    op(V, lambda e: e.tensor_scalar(out=cstb[:, 4:5], in0=cstb[:, 3:4], scalar1=-1.0,
                                    scalar2=None, op0=ALU.mult), writes=[r_cstb])

    op(V, lambda e: e.memset(hkd[:], 0.0), writes=[r_hkd])
    op(V, lambda e: e.memset(mcol[:], 0.0), writes=[r_mcol2[0], r_mcol2[1]])
    op(V, lambda e: e.reciprocal(out=hkd[:, O_RK:O_RK + NH], in_=hk_sb[:]),
       reads=[r_hk], writes=[r_hkd])
    op(V, lambda e: e.tensor_scalar(out=hkd[:, O_NEGRK:O_NEGRK + NH],
                                    in0=hkd[:, O_RK:O_RK + NH], scalar1=-1.0,
                                    scalar2=None, op0=ALU.mult), writes=[r_hkd])
    op(S, lambda e: e.activation(out=hkd[:, O_A:O_A + NH], in_=hk_sb[:], func=AF.Ln),
       reads=[r_hk], writes=[r_hkd])
    op(S, lambda e: e.activation(out=hkd[:, O_A:O_A + NH], in_=hkd[:, O_A:O_A + NH],
                                 func=AF.Exp, scale=-0.5), writes=[r_hkd])
    op(V, lambda e: e.tensor_scalar(out=hkd[:, O_B:O_B + NH], in0=hkd[:, O_A:O_A + NH],
                                    scalar1=EPS_ATT, scalar2=None, op0=ALU.mult),
       writes=[r_hkd])
    op(TE, lambda e: e.matmul(out=ps_a[:, 0:8 * NH], lhsT=ones_r[0:1, :], rhs=hkd[0:1, :],
                              start=True, stop=True),
       reads=[r_onesr, r_hkd], writes=[r_psa])
    op(V, lambda e: e.tensor_copy(out=scl[:], in_=ps_a[:, 0:8 * NH]),
       reads=[r_psa], writes=[r_scl])

    def quant_chain(src_ap, dst_ap, man, minexp, res_src, res_dst,
                    c1, c2, c4, c5, ci, tw, sg, r_c1, r_c2, r_c4, r_c5, r_ci, r_sgn,
                    r_tw, signed, nrow=128):
        nlev = man + 2
        if signed:
            op(S, lambda e: e.activation(out=sg, in_=src_ap, func=AF.Sign),
               reads=[res_src], writes=[r_sgn])
            op(S, lambda e: e.activation(out=c5, in_=src_ap, func=AF.Abs),
               reads=[res_src], writes=[r_c5])
            ax, r_ax = c5, r_c5
        else:
            ax, r_ax = src_ap, res_src
        zb = cstb[0:nrow, 0:1]
        op(S, lambda e: e.activation(out=c1, in_=ax, func=AF.Ln, bias=zb),
           reads=[r_ax, r_cstb], writes=[r_c1])
        op(V, lambda e: e.tensor_scalar(out=c1, in0=c1, scalar1=INV_LN2, scalar2=63.5,
                                        op0=ALU.mult, op1=ALU.add), writes=[r_c1])
        op(V, lambda e: e.tensor_copy(out=ci, in_=c1), reads=[r_c1], writes=[r_ci])
        op(V, lambda e: e.tensor_copy(out=c1, in_=ci), reads=[r_ci], writes=[r_c1])
        op(V, lambda e: e.tensor_scalar(out=c2, in0=c1, scalar1=-1.0,
                                        scalar2=float(64 + man), op0=ALU.mult,
                                        op1=ALU.add), reads=[r_c1], writes=[r_c2])
        op(V, lambda e: e.tensor_scalar(out=c2, in0=c2, scalar1=float(man - minexp),
                                        scalar2=None, op0=ALU.min), writes=[r_c2])
        op(S, lambda e: e.activation(out=c4, in_=c2, func=AF.Exp,
                                     scale=cstb[0:nrow, 4:5]),
           reads=[r_c2, r_cstb], writes=[r_c4])
        op(S, lambda e: e.activation(out=c2, in_=c2, func=AF.Exp,
                                     scale=cstb[0:nrow, 3:4]),
           reads=[r_cstb], writes=[r_c2])
        op(V, lambda e: e.tensor_mul(out=c2, in0=ax, in1=c2),
           reads=[r_ax], writes=[r_c2])
        op(V, lambda e: e.tensor_copy(out=c1, in_=c2), reads=[r_c2], writes=[r_c1])
        for lev in range(nlev):
            stepv = float(2 ** (nlev - 1 - lev))
            thr = stepv - 0.5
            op(V, lambda e, th=thr, sv=stepv: e.tensor_scalar(
                out=ci, in0=c1, scalar1=th, scalar2=sv, op0=ALU.is_ge, op1=ALU.mult),
               reads=[r_c1], writes=[r_ci])
            op(V, lambda e: e.tensor_sub(out=c1, in0=c1, in1=ci),
               reads=[r_ci], writes=[r_c1])
        op(V, lambda e: e.tensor_sub(out=c2, in0=c2, in1=c1),
           reads=[r_c1], writes=[r_c2])
        tok = op(V, lambda e: e.tensor_mul(out=dst_ap, in0=c2, in1=c4),
                 reads=[r_c2, r_c4], writes=[res_dst])
        if signed:
            tok = op(V, lambda e: e.tensor_mul(out=dst_ap, in0=dst_ap, in1=sg),
                     reads=[r_sgn], writes=[res_dst])
        return tok

    for hf in range(2):
        quant_chain(b2_sb[0:1, 512 * hf:512 * hf + 512],
                    b2_sb[0:1, 512 * hf:512 * hf + 512], 10, -14, r_b2, r_b2,
                    q1_sl[0:1, 0:512], q2_sl[0:1, 0:512], q4_sl[0:1, 0:512],
                    q5_sl[0:1, 0:512], qi[0:1, 0:512], None,
                    sg_sl[0:1, 0:512],
                    ar_res["q1"], ar_res["q2"], ar_res["q4"], ar_res["q5"], r_qi,
                    ar_res["sg"], None, True, nrow=1)
    for hf in range(2):
        pq, rq = (ps_a, r_psa) if hf == 0 else (ps_b, r_psb)
        op(TE, lambda e, hf=hf, pq=pq: e.matmul(
            out=pq[:, 0:512], lhsT=ones_r[0:1, :], rhs=b2_sb[0:1, 512 * hf:512 * hf + 512],
            start=True, stop=True), reads=[r_onesr, r_b2], writes=[rq])
        op(V, lambda e, hf=hf, pq=pq: e.tensor_copy(
            out=b2q_bc[:, 512 * hf:512 * hf + 512], in_=pq[:, 0:512]),
           reads=[rq], writes=[r_b2bc])

    for hc in range(32):
        buf, rb = wld[hc % 2], r_wld[hc % 2]
        dma(SY, buf[:], w1[128 * hc:128 * hc + 128, :], writes=[rb])
        for hf in range(2):
            op(V, lambda e, b=buf, hf=hf: e.tensor_copy(
                out=wbf[:], in_=b[:, 512 * hf:512 * hf + 512]),
               reads=[rb], writes=[r_wbf])
            for c in range(4):
                op(TE, lambda e, c=c: e.transpose(out=pst[c % 2][:],
                                                  in_=wbf[:, 128 * c:128 * c + 128],
                                                  identity=ident[:]),
                   reads=[r_wbf, r_ident], writes=[r_pst[c % 2]])
                op(V, lambda e, c=c: e.tensor_copy(out=wbf2[:, 128 * c:128 * c + 128],
                                                   in_=pst[c % 2][:]),
                   reads=[r_pst[c % 2]], writes=[r_wbf2])
            dma(G, w1t_d[:, C * hc + 512 * hf:C * hc + 512 * hf + 512], wbf2[:],
                reads=[r_wbf2])

    for cc in range(8):
        for piece in range(8):
            buf, rb = wld[piece % 2], r_wld[piece % 2]
            dma(SY, buf[:, 0:512],
                w2[128 * cc:128 * cc + 128, 512 * piece:512 * piece + 512], writes=[rb])
            quant_chain(buf[:, 0:512], buf[:, 0:512], 3, -6, rb, rb,
                        q1_sl, q2_sl, q4_sl, q5_sl, qi[:, 0:512], None, sg_sl,
                        ar_res["q1"], ar_res["q2"], ar_res["q4"], ar_res["q5"], r_qi,
                        ar_res["sg"], None, True)
            op(V, lambda e, b=buf: e.tensor_copy(out=wbf[:, 0:512], in_=b[:, 0:512]),
               reads=[rb], writes=[r_wbf])
            for j in range(4):
                hcg = 4 * piece + j
                op(TE, lambda e, j=j: e.transpose(out=pst[j % 2][:],
                                                  in_=wbf[:, 128 * j:128 * j + 128],
                                                  identity=ident[:]),
                   reads=[r_wbf, r_ident], writes=[r_pst[j % 2]])
                op(V, lambda e, j=j: e.tensor_copy(out=wbf2[:, 128 * j:128 * j + 128],
                                                   in_=pst[j % 2][:]),
                   reads=[r_pst[j % 2]], writes=[r_wbf2])
                dma(G, w2qt_d[:, C * hcg + 128 * cc:C * hcg + 128 * cc + 128],
                    wbf2[:, 128 * j:128 * j + 128], reads=[r_wbf2])

    if dump == "wprep":
        for hc in range(64):
            dma(SY, wbf2[:], w1t_d[:, 512 * hc:512 * hc + 512], writes=[r_wbf2])
            dma(G, dumps["d_w1t"][:, 512 * hc:512 * hc + 512], wbf2[:],
                reads=[r_wbf2])
            dma(SY, wbf2[:], w2qt_d[:, 512 * hc:512 * hc + 512], writes=[r_wbf2])
            dma(G, dumps["d_w2qt"][:, 512 * hc:512 * hc + 512], wbf2[:],
                reads=[r_wbf2])

    nseq = B_LOC if dump is None else (0 if dump == "wprep" else 1)
    for sq in range(nseq):
        arena_switch()
        for i in range(NT):
            buf, rb = xld[i % 2], r_xld[i % 2]
            dma(SY, buf[:], x_in[sq, 128 * i:128 * i + 128, :], writes=[rb])
            op(V, lambda e, b=buf: e.reduce_sum(out=stat[:, 0:1], in_=b[:], axis=AX.X),
               reads=[rb], writes=[r_stat])
            for hf in range(2):
                op(V, lambda e, b=buf, hf=hf: e.tensor_mul(
                    out=sq_sl, in0=b[:, 512 * hf:512 * hf + 512],
                    in1=b[:, 512 * hf:512 * hf + 512]),
                   reads=[rb], writes=[ar_res["sq"]])
                op(V, lambda e, hf=hf: e.reduce_sum(out=stat[:, 1 + hf:2 + hf],
                                                    in_=sq_sl, axis=AX.X),
                   reads=[ar_res["sq"]], writes=[r_stat])
            op(V, lambda e: e.tensor_add(out=stat[:, 3:4], in0=stat[:, 1:2],
                                         in1=stat[:, 2:3]), writes=[r_stat])
            op(V, lambda e: e.tensor_scalar(out=stat[:, 4:5], in0=stat[:, 0:1],
                                            scalar1=1.0 / C, scalar2=None,
                                            op0=ALU.mult), writes=[r_stat])
            op(V, lambda e: e.tensor_scalar(out=stat[:, 5:6], in0=stat[:, 3:4],
                                            scalar1=1.0 / C, scalar2=None,
                                            op0=ALU.mult), writes=[r_stat])
            op(V, lambda e: e.tensor_mul(out=stat[:, 6:7], in0=stat[:, 4:5],
                                         in1=stat[:, 4:5]), writes=[r_stat])
            op(V, lambda e: e.tensor_sub(out=stat[:, 7:8], in0=stat[:, 5:6],
                                         in1=stat[:, 6:7]), writes=[r_stat])
            op(S, lambda e: e.activation(out=stat[:, 8:9], in_=stat[:, 7:8],
                                         func=AF.Ln, bias=cstb[:, 1:2]),
               reads=[r_stat, r_cstb], writes=[r_stat])
            op(S, lambda e: e.activation(out=stat[:, 9:10], in_=stat[:, 8:9],
                                         func=AF.Exp, scale=-0.5), writes=[r_stat])
            op(V, lambda e: e.tensor_scalar(out=stat[:, 10:11], in0=stat[:, 4:5],
                                            scalar1=-1.0, scalar2=None, op0=ALU.mult),
               writes=[r_stat])
            op(V, lambda e, b=buf: e.tensor_scalar(out=xn_tok[:], in0=b[:],
                                                   scalar1=stat[:, 10:11],
                                                   scalar2=stat[:, 9:10],
                                                   op0=ALU.add, op1=ALU.mult),
               reads=[rb, r_stat], writes=[r_xntok])
            for c in range(8):
                op(TE, lambda e, c=c: e.transpose(out=pst[c % 2][:],
                                                  in_=xn_tok[:, 128 * c:128 * c + 128],
                                                  identity=ident[:]),
                   reads=[r_xntok, r_ident], writes=[r_pst[c % 2]])
                op(V, lambda e, c=c, i=i: e.tensor_copy(
                    out=xn_T[c][:, 128 * i:128 * i + 128], in_=pst[c % 2][:]),
                   reads=[r_pst[c % 2]], writes=[r_xnT[c]])

        if dump == "xnT":
            for c in range(8):
                dma(G, dumps["d_xnT"][c], xn_T[c][:], reads=[r_xnT[c]])
            break

        ngroups = 1 if dump == "att1" else 8
        for g in range(ngroups):
            for part, rowb in ((0, 128 * g), (1, C + 128 * g), (2, 2 * C + 128 * g)):
                buf, rb = wld[part % 2], r_wld[part % 2]
                dma(SY, buf[:], wqkv[rowb:rowb + 128, :], writes=[rb])
                for hf in range(2):
                    op(V, lambda e, b=buf, hf=hf: e.tensor_copy(
                        out=wbf[:], in_=b[:, 512 * hf:512 * hf + 512]),
                       reads=[rb], writes=[r_wbf])
                    for cq in range(4):
                        c = 4 * hf + cq
                        op(TE, lambda e, cq=cq: e.transpose(
                            out=pst[cq % 2][:], in_=wbf[:, 128 * cq:128 * cq + 128],
                            identity=ident[:]),
                           reads=[r_wbf, r_ident], writes=[r_pst[cq % 2]])
                        op(V, lambda e, c=c, cq=cq, p=part: e.tensor_copy(
                            out=wqkvT_g[c][:, 128 * p:128 * p + 128],
                            in_=pst[cq % 2][:]),
                           reads=[r_pst[cq % 2]], writes=[r_wqkvT[c]])
            for a in range(2):
                op(V, lambda e, a=a: e.memset(v_ext[a][:], 1.0), writes=[r_vext[a]])

            for i in range(NT):
                pq, rpq = (ps_a, r_psa) if i % 2 == 0 else (ps_b, r_psb)
                for c in range(8):
                    op(TE, lambda e, c=c, i=i, pq=pq: e.matmul(
                        out=pq[:, 0:384], lhsT=xn_T[c][:, 128 * i:128 * i + 128],
                        rhs=wqkvT_g[c][:, 0:384], start=(c == 0), stop=(c == 7)),
                       reads=[r_xnT[c], r_wqkvT[c]], writes=[rpq])
                for qk in range(2):
                    for a in range(2):
                        base = 128 * qk + 64 * a
                        lo = slice(base, base + 32)
                        hi = slice(base + 32, base + 64)
                        cs = cos_sb[:, 32 * i:32 * i + 32]
                        sn = sin_sb[:, 32 * i:32 * i + 32]
                        op(V, lambda e, pq=pq, lo=lo, cs=cs: e.tensor_mul(
                            out=ropet[:, 0:32], in0=pq[:, lo], in1=cs),
                           reads=[rpq, r_cos], writes=[r_ropet])
                        op(V, lambda e, pq=pq, hi=hi, sn=sn: e.tensor_mul(
                            out=ropet[:, 32:64], in0=pq[:, hi], in1=sn),
                           reads=[rpq, r_sin], writes=[r_ropet])
                        op(V, lambda e, lo=lo: e.tensor_add(
                            out=roped[:, lo], in0=ropet[:, 0:32], in1=ropet[:, 32:64]),
                           reads=[r_ropet], writes=[r_roped])
                        op(V, lambda e, pq=pq, lo=lo, sn=sn: e.tensor_mul(
                            out=ropet[:, 0:32], in0=pq[:, lo], in1=sn),
                           reads=[rpq, r_sin], writes=[r_ropet])
                        op(V, lambda e, pq=pq, hi=hi, cs=cs: e.tensor_mul(
                            out=ropet[:, 32:64], in0=pq[:, hi], in1=cs),
                           reads=[rpq, r_cos], writes=[r_ropet])
                        op(V, lambda e, hi=hi: e.tensor_sub(
                            out=roped[:, hi], in0=ropet[:, 32:64], in1=ropet[:, 0:32]),
                           reads=[r_ropet], writes=[r_roped])
                for a in range(2):
                    op(V, lambda e, pq=pq, a=a, i=i: e.tensor_copy(
                        out=v_ext[a][:, 65 * i:65 * i + 64],
                        in_=pq[:, 256 + 64 * a:256 + 64 * a + 64]),
                       reads=[rpq], writes=[r_vext[a]])
                for qk in range(2):
                    for a in range(2):
                        idx = a
                        op(TE, lambda e, qk=qk, a=a, idx=idx: e.transpose(
                            out=pst[idx][0:64, :],
                            in_=roped[:, 128 * qk + 64 * a:128 * qk + 64 * a + 64],
                            identity=ident[:]),
                           reads=[r_roped, r_ident], writes=[r_pst[idx]])
                        if qk == 0:
                            h = 2 * g + a
                            op(V, lambda e, a=a, i=i, idx=idx, h=h: e.tensor_scalar(
                                out=q_ext[a][0:64, 128 * i:128 * i + 128],
                                in0=pst[idx][0:64, :],
                                scalar1=scl[0:64, O_NEGRK + h:O_NEGRK + h + 1],
                                scalar2=None, op0=ALU.mult),
                               reads=[r_pst[idx], r_scl], writes=[r_qext[a]])
                        else:
                            op(V, lambda e, a=a, i=i, idx=idx: e.tensor_copy(
                                out=k_ext[a][0:64, 128 * i:128 * i + 128],
                                in_=pst[idx][0:64, :]),
                               reads=[r_pst[idx]], writes=[r_kext[a]])
            q0row, r_q0row = of_sl[0:1, 0:512], ar_res["of"]
            q0rowb, r_q0rowb = p_bf[0:1, 0:512], r_pbf
            for a in range(2):
                h = 2 * g + a
                for qk in range(2):
                    src, rsrc = (q_ext[a], r_qext[a]) if qk == 0 else (k_ext[a], r_kext[a])
                    bias_ap = hkd[0:1, O_RK + h:O_RK + h + 1] if qk == 0 \
                        else hk_sb[0:1, h:h + 1]
                    bias_res = r_hkd if qk == 0 else r_hk
                    dstt, rdst = (q_ext[a], r_qext[a]) if qk == 0 \
                        else (k_ext[a], r_kext[a])
                    for blk in range(4):
                        sl = slice(512 * blk, 512 * blk + 512)
                        op(V, lambda e, src=src, sl=sl: e.tensor_mul(
                            out=sq_sl[0:64, :], in0=src[0:64, sl], in1=src[0:64, sl]),
                           reads=[rsrc], writes=[ar_res["sq"]])
                        op(V, lambda e: e.tensor_copy(out=sqb[0:64, :],
                                                      in_=sq_sl[0:64, :]),
                           reads=[ar_res["sq"]], writes=[r_sqb])
                        op(TE, lambda e: e.matmul(out=ps_c[0:1, 0:512], lhsT=ones64[:],
                                                  rhs=sqb[0:64, :], start=True,
                                                  stop=True),
                           reads=[r_ones64, r_sqb], writes=[r_psc])
                        op(S, lambda e, b=bias_ap: e.activation(
                            out=q0row[:], in_=ps_c[0:1, 0:512], func=AF.Ln, bias=b),
                           reads=[bias_res, r_psc], writes=[r_q0row])
                        op(S, lambda e: e.activation(out=q0row[:], in_=q0row[:],
                                                     func=AF.Exp, scale=0.5),
                           writes=[r_q0row])
                        op(V, lambda e: e.tensor_copy(out=q0rowb[:], in_=q0row[:]),
                           reads=[r_q0row], writes=[r_q0rowb])
                        dma(G, dstt[64:65, sl], q0rowb[:], reads=[r_q0rowb],
                            writes=[rdst])

            if dump == "qkv":
                for a in range(2):
                    dma(G, dumps["d_q"][a][0:65, :], q_ext[a][:], reads=[r_qext[a]])
                    dma(G, dumps["d_k"][a][0:65, :], k_ext[a][:], reads=[r_kext[a]])
                    dma(G, dumps["d_v"][a], v_ext[a][:], reads=[r_vext[a]])
                break

            for a in range(2):
                h = 2 * g + a
                for lt in range(NT):
                    W = 128 * (lt + 1)
                    nb = (W + 511) // 512
                    par = lt % 2
                    r_sl, r_rs = rsb[par], ar_res["rs" + str(par)]
                    mc = mcol[:, 4 * par:4 * par + 4]
                    r_mc = r_mcol2[par]
                    for sbk in range(nb):
                        wd = min(512, W - 512 * sbk)
                        pq, rpq = (ps_a, r_psa) if sbk % 2 == 0 else (ps_b, r_psb)
                        tb_, rtb = tb[sbk % 2], ar_res["tb" + str(sbk % 2)]
                        ub_, rub = ub[sbk % 2], ar_res["ub" + str(sbk % 2)]
                        op(TE, lambda e, a=a, lt=lt, sbk=sbk, wd=wd, pq=pq: e.matmul(
                            out=pq[:, 0:wd], lhsT=q_ext[a][:, 128 * lt:128 * lt + 128],
                            rhs=k_ext[a][:, 512 * sbk:512 * sbk + wd],
                            start=True, stop=True),
                           reads=[r_qext[a], r_kext[a]], writes=[rpq])
                        op(V, lambda e, wd=wd, pq=pq, tb_=tb_: e.tensor_scalar(
                            out=tb_[:, 0:wd], in0=pq[:, 0:wd],
                            scalar1=1.0 + EPS_ATT, scalar2=None, op0=ALU.max),
                           reads=[rpq], writes=[rtb])
                        op(S, lambda e, wd=wd, tb_=tb_, ub_=ub_: e.activation(
                            out=ub_[:, 0:wd], in_=tb_[:, 0:wd], func=AF.Square),
                           reads=[rtb], writes=[rub])
                        op(S, lambda e, wd=wd, ub_=ub_: e.activation(
                            out=ub_[:, 0:wd], in_=ub_[:, 0:wd], func=AF.Ln,
                            bias=cstb[:, 2:3]), reads=[r_cstb], writes=[rub])
                        op(S, lambda e, wd=wd, ub_=ub_: e.activation(
                            out=ub_[:, 0:wd], in_=ub_[:, 0:wd], func=AF.Exp,
                            scale=0.5), writes=[rub])
                        op(V, lambda e, wd=wd, tb_=tb_, ub_=ub_: e.tensor_add(
                            out=tb_[:, 0:wd], in0=tb_[:, 0:wd], in1=ub_[:, 0:wd]),
                           reads=[rub], writes=[rtb])
                        op(S, lambda e, wd=wd, tb_=tb_: e.activation(
                            out=tb_[:, 0:wd], in_=tb_[:, 0:wd], func=AF.Ln),
                           writes=[rtb])
                        op(V, lambda e, wd=wd, tb_=tb_, h=h: e.tensor_scalar(
                            out=tb_[:, 0:wd], in0=tb_[:, 0:wd],
                            scalar1=scl[:, O_B + h:O_B + h + 1],
                            scalar2=None, op0=ALU.add),
                           reads=[r_scl], writes=[rtb])
                        op(V, lambda e, wd=wd, sbk=sbk, tb_=tb_, r_sl=r_sl:
                           e.reciprocal(
                            out=r_sl[:, 512 * sbk:512 * sbk + wd], in_=tb_[:, 0:wd]),
                           reads=[rtb], writes=[r_rs])
                    op(V, lambda e, W=W, r_sl=r_sl: e.tensor_add(
                        out=r_sl[:, W - 128:W], in0=r_sl[:, W - 128:W], in1=maskt[:]),
                       reads=[r_maskt], writes=[r_rs])
                    op(V, lambda e, W=W, r_sl=r_sl, mc=mc: e.tensor_reduce(
                        out=mc[:, 0:1], in_=r_sl[:, 0:W], axis=AX.X, op=ALU.max,
                        negate=True),
                       reads=[r_rs], writes=[r_mc])
                    op(V, lambda e, h=h, mc=mc: e.tensor_scalar(
                        out=mc[:, 1:2], in0=mc[:, 0:1],
                        scalar1=scl[:, O_A + h:O_A + h + 1], scalar2=None,
                        op0=ALU.mult), reads=[r_scl], writes=[r_mc])
                    op(S, lambda e, W=W, h=h, r_sl=r_sl, mc=mc: e.activation(
                        out=p_bf[:, 0:W], in_=r_sl[:, 0:W], func=AF.Exp,
                        bias=mc[:, 1:2], scale=scl[:, O_A + h:O_A + h + 1],
                        accum_out=mc[:, 2:3]),
                       reads=[r_rs, r_mc, r_scl],
                       writes=[r_pbf, r_mc])
                    op(V, lambda e, mc=mc: e.reciprocal(out=mc[:, 3:4], in_=mc[:, 2:3]),
                       reads=[r_mc], writes=[r_mc])
                    op(V, lambda e, W=W, mc=mc: e.tensor_scalar(
                        out=p_bf[:, 0:W], in0=p_bf[:, 0:W], scalar1=mc[:, 3:4],
                        scalar2=None, op0=ALU.mult),
                       reads=[r_mc], writes=[r_pbf])
                    for cch in range(lt + 1):
                        op(TE, lambda e, cch=cch: e.transpose(
                            out=pst[cch % 2][:], in_=p_bf[:, 128 * cch:128 * cch + 128],
                            identity=ident[:]),
                           reads=[r_pbf, r_ident], writes=[r_pst[cch % 2]])
                        op(V, lambda e, cch=cch: e.tensor_copy(
                            out=pT_bf[:, 128 * cch:128 * cch + 128],
                            in_=pst[cch % 2][:]),
                           reads=[r_pst[cch % 2]], writes=[r_pT])
                    ppv, rppv = (ps_pv, r_pspv) if lt % 2 == 0 else (ps_bc, r_psbc)
                    for cch in range(lt + 1):
                        op(TE, lambda e, a=a, cch=cch, lt=lt, ppv=ppv: e.matmul(
                            out=ppv[0:65, 0:128],
                            lhsT=v_ext[a][:, 65 * cch:65 * cch + 65],
                            rhs=pT_bf[:, 128 * cch:128 * cch + 128],
                            start=(cch == 0), stop=(cch == lt)),
                           reads=[r_vext[a], r_pT], writes=[rppv])
                    if a == 0:
                        op(V, lambda e, g=g, lt=lt, ppv=ppv: e.tensor_copy(
                            out=attn_out[g][0:64, 128 * lt:128 * lt + 128],
                            in_=ppv[0:64, 0:128]),
                           reads=[rppv], writes=[r_aout[g]])
                    else:
                        ash_, rash = (ash, r_ash) if lt % 2 == 0 else (ash2, r_ash2)
                        op(V, lambda e, ppv=ppv, ash_=ash_: e.tensor_copy(
                            out=ash_[:], in_=ppv[0:64, 0:128]),
                           reads=[rppv], writes=[rash])
                        dma(G, attn_out[g][64:128, 128 * lt:128 * lt + 128], ash_[:],
                            reads=[rash], writes=[r_aout[g]])

        if dump == "qkv":
            break
        if dump in ("att", "att1"):
            for cc in range(ngroups):
                dma(G, dumps["d_ao"][cc], attn_out[cc][:], reads=[r_aout[cc]])
            break

        for qd in range(4):
            for j in range(2):
                buf, rb = wld[j % 2], r_wld[j % 2]
                dma(SY, buf[:], wout[256 * qd + 128 * j:256 * qd + 128 * j + 128, :],
                    writes=[rb])
                for hf in range(2):
                    op(V, lambda e, b=buf, hf=hf: e.tensor_copy(
                        out=wbf[:], in_=b[:, 512 * hf:512 * hf + 512]),
                       reads=[rb], writes=[r_wbf])
                    for cq in range(4):
                        c = 4 * hf + cq
                        op(TE, lambda e, cq=cq: e.transpose(
                            out=pst[cq % 2][:], in_=wbf[:, 128 * cq:128 * cq + 128],
                            identity=ident[:]),
                           reads=[r_wbf, r_ident], writes=[r_pst[cq % 2]])
                        op(V, lambda e, c=c, cq=cq, j=j: e.tensor_copy(
                            out=woutT[c][:, 128 * j:128 * j + 128],
                            in_=pst[cq % 2][:]),
                           reads=[r_pst[cq % 2]], writes=[r_woutT[c]])
            for i in range(NT):
                pq, rpq = (ps_a, r_psa) if i % 2 == 0 else (ps_b, r_psb)
                for c in range(8):
                    op(TE, lambda e, c=c, i=i, pq=pq: e.matmul(
                        out=pq[:, 0:256], lhsT=attn_out[c][:, 128 * i:128 * i + 128],
                        rhs=woutT[c][:, 0:256], start=(c == 0), stop=(c == 7)),
                       reads=[r_aout[c], r_woutT[c]], writes=[rpq])
                st, rst = afst[i % 2], r_afst[i % 2]
                op(V, lambda e, pq=pq, st=st: e.tensor_copy(out=st[:, 0:256],
                                                            in_=pq[:, 0:256]),
                   reads=[rpq], writes=[rst])
                dma(G, ares_d[128 * i:128 * i + 128, 256 * qd:256 * qd + 256],
                    st[:, 0:256], reads=[rst])

        if dump == "ares":
            for i in range(NT):
                for half in range(2):
                    st, rst = arld[i % 2], r_arld[i % 2]
                    dma(SY, st[:],
                        ares_d[128 * i:128 * i + 128, 512 * half:512 * half + 512],
                        writes=[rst])
                    dma(G, dumps["d_ares"][128 * i:128 * i + 128,
                                           512 * half:512 * half + 512],
                        st[:], reads=[rst])
            break

        arena_switch()
        for qt in range(4):
            tok0 = 512 * qt
            for hc in range(32):
                wbuf, rwb = w1t_hc[hc % 2], r_w1thc[hc % 2]
                dma(SY, wbuf[:], w1t_d[:, C * hc:C * hc + C], writes=[rwb])
                pq, rpq = (ps_c, r_psc) if hc % 2 == 0 else (ps_d, r_psd)
                for c in range(8):
                    op(TE, lambda e, c=c, pq=pq, wbuf=wbuf, tok0=tok0: e.matmul(
                        out=pq[:, 0:512], lhsT=wbuf[:, 128 * c:128 * c + 128],
                        rhs=xn_T[c][:, tok0:tok0 + 512], start=(c == 0), stop=(c == 7)),
                       reads=[rwb, r_xnT[c]], writes=[rpq])
                op(S, lambda e, pq=pq, hc=hc: e.activation(
                    out=h_sl, in_=pq[:, 0:512], func=AF.Relu,
                    bias=b1_sb[:, hc:hc + 1]),
                   reads=[rpq, r_b1], writes=[ar_res["h"]])
                quant_chain(h_sl, hq[hc][:], 3, -6, ar_res["h"], r_hq[hc],
                            q1_sl, q2_sl, q4_sl, q5_sl, qi[:, 0:512], None, sg_sl,
                            ar_res["q1"], ar_res["q2"], ar_res["q4"], ar_res["q5"],
                            r_qi, ar_res["sg"], None, False)
            if dump == "hq" and qt == 0:
                for hc in range(32):
                    dma(G, dumps["d_hq"][hc], hq[hc][:], reads=[r_hq[hc]])
                break
            for dh in range(2):
                for hc in range(32):
                    wb2, rw2 = w2s[hc % 2], r_w2s[hc % 2]
                    dma(SY, wb2[:],
                        w2qt_d[:, C * hc + 512 * dh:C * hc + 512 * dh + 512],
                        writes=[rw2])
                    for tl in range(4):
                        op(TE, lambda e, hc=hc, tl=tl, wb2=wb2: e.matmul(
                            out=ps_p[tl][:, 0:512],
                            lhsT=hq[hc][:, 128 * tl:128 * tl + 128], rhs=wb2[:],
                            start=(hc == 0), stop=(hc == 31)),
                           reads=[r_hq[hc], rw2], writes=[r_psp[tl]])
                for tl in range(4):
                    i = 4 * qt + tl
                    op(V, lambda e, tl=tl, dh=dh: e.tensor_add(
                        out=of_sl, in0=ps_p[tl][:, 0:512],
                        in1=b2q_bc[:, 512 * dh:512 * dh + 512]),
                       reads=[r_psp[tl], r_b2bc], writes=[ar_res["of"]])
                    quant_chain(of_sl, fin_sl, 10, -14, ar_res["of"], ar_res["fin"],
                                q1_sl, q2_sl, q4_sl, q5_sl, qi[:, 0:512], None,
                                sg_sl, ar_res["q1"], ar_res["q2"], ar_res["q4"],
                                ar_res["q5"], r_qi, ar_res["sg"], None, True)
                    xb, rxb = xld[tl % 2], r_xld[tl % 2]
                    dma(SY, xb[:, 0:512],
                        x_in[sq, 128 * i:128 * i + 128, 512 * dh:512 * dh + 512],
                        writes=[rxb])
                    ab, rab = arld[tl % 2], r_arld[tl % 2]
                    dma(SY, ab[:],
                        ares_d[128 * i:128 * i + 128, 512 * dh:512 * dh + 512],
                        writes=[rab])
                    op(V, lambda e, xb=xb: e.tensor_add(
                        out=fin_sl, in0=fin_sl, in1=xb[:, 0:512]),
                       reads=[rxb], writes=[ar_res["fin"]])
                    op(V, lambda e, ab=ab: e.tensor_add(
                        out=fin_sl, in0=fin_sl, in1=ab[:]),
                       reads=[rab], writes=[ar_res["fin"]])
                    dma(G, y_out[sq, 128 * i:128 * i + 128, 512 * dh:512 * dh + 512],
                        fin_sl, reads=[ar_res["fin"]])

    global _last_P
    _last_P = P
    P.replay(nc, None)
    ctx.close()
    return nc


_consts_cache = {}


def _host_consts():
    if _consts_cache:
        return _consts_cache
    import ml_dtypes
    ident = np.eye(128, dtype=np.float32).astype(ml_dtypes.bfloat16)
    mask = np.where(np.arange(128)[None, :] <= np.arange(128)[:, None],
                    np.float32(0.0), np.float32(NEG_BIG)).astype(ml_dtypes.bfloat16)
    import jax
    import jax.numpy as jnp
    with jax.default_device(jax.devices("cpu")[0]):
        inv_freq = 1.0 / 10000.0 ** (jnp.arange(0, 64, 2, dtype=jnp.float32) / 64)
        freqs = jnp.arange(T, dtype=jnp.float32)[:, None] * inv_freq[None, :]
        cos_t = np.asarray(jnp.cos(freqs), dtype=np.float32)
        sin_t = np.asarray(jnp.sin(freqs), dtype=np.float32)
    _consts_cache.update(ident=ident, mask=mask, cos_t=cos_t, sin_t=sin_t)
    return _consts_cache


_prog_cache = {}


def kernel(L0_attn_x, L0_attn_W_qkv, L0_attn_W_out, L0_attn_hyp_k,
           L0_ffn_w1, L0_ffn_b1, L0_ffn_w2, L0_ffn_b2):
    x = np.ascontiguousarray(np.asarray(L0_attn_x, dtype=np.float32))
    wqkv = np.ascontiguousarray(np.asarray(L0_attn_W_qkv, dtype=np.float32))
    wo = np.ascontiguousarray(np.asarray(L0_attn_W_out, dtype=np.float32))
    hk = np.ascontiguousarray(np.asarray(L0_attn_hyp_k, dtype=np.float32)
                              .reshape(1, NH))
    w1 = np.ascontiguousarray(np.asarray(L0_ffn_w1, dtype=np.float32))
    b1 = np.ascontiguousarray(np.asarray(L0_ffn_b1, dtype=np.float32)
                              .reshape(32, 128).T)
    w2 = np.ascontiguousarray(np.asarray(L0_ffn_w2, dtype=np.float32))
    b2 = np.ascontiguousarray(np.asarray(L0_ffn_b2, dtype=np.float32)
                              .reshape(1, C))
    cst = _host_consts()

    if "nc" not in _prog_cache:
        _prog_cache["nc"] = build_core_program()
    nc = _prog_cache["nc"]

    in_maps = []
    for core in range(8):
        in_maps.append({
            "x_in": x[B_LOC * core:B_LOC * core + B_LOC],
            "wqkv": wqkv, "wout": wo, "hypk": hk,
            "w1": w1, "b1": b1, "w2": w2, "b2": b2,
            "ident": cst["ident"], "mask": cst["mask"],
            "cos_t": cst["cos_t"], "sin_t": cst["sin_t"],
        })
    res = run_bass_kernel_spmd(nc, in_maps, core_ids=list(range(8)))
    y = np.concatenate([np.asarray(r["y"], dtype=np.float32)
                        for r in res.results], axis=0)
    return y
